# version 19
# baseline (speedup 1.0000x reference)
import numpy as np
import ml_dtypes
import concourse.bass as bass
import concourse.mybir as mybir
from concourse.bass_utils import run_bass_kernel_spmd

F32 = mybir.dt.float32
BF16 = mybir.dt.bfloat16
AF = mybir.ActivationFunctionType
ALU = mybir.AluOpType
AX = mybir.AxisListType

B, S, DM = 4, 2048, 2048
H, KVH, HD = 16, 4, 128
NT = 1024
TCH = NT // 128
ICH = DM // 128
NE, IFF, SI = 8, 1408, 5632
IFCH = IFF // 128
NUNIT = 12
LN_EPS = 1e-5
ATT_EPS = 1e-6
ROPE_BASE = 10000.0

ENGS = ("tensor", "vector", "scalar", "gpsimd", "sync")
NSEM = {"gpsimd": 10, "sync": 6, "scalar": 4}


class Op:
    __slots__ = ("eng", "emit", "deps", "kind", "seq", "observed", "qidx", "tag", "eidx")

    def __init__(self, eng, emit, deps, kind, tag=""):
        self.eng = eng
        self.emit = emit
        self.deps = deps
        self.kind = kind
        self.seq = None
        self.observed = False
        self.qidx = None
        self.tag = tag


class Buf:
    __slots__ = ("writer", "readers", "name", "psum")

    def __init__(self, name="", psum=False):
        self.writer = None
        self.readers = []
        self.name = name
        self.psum = psum


class Prog:
    def __init__(self):
        self.ops = []
        self.dma_counts = {"gpsimd": 0, "sync": 0, "scalar": 0}

    def add(self, eng, emit, reads=(), writes=(), deps=(), kind="compute", tag=""):
        dd = list(deps)
        for b in reads:
            if b.writer is not None:
                dd.append(b.writer)
            if b.psum:
                dd.extend(r for r in b.readers if r.eng != eng)
        for b in writes:
            if b.writer is not None:
                dd.append(b.writer)
            dd.extend(b.readers)
        op = Op(eng, emit, dd, kind, tag)
        if kind == "dma":
            i = self.dma_counts[eng]
            self.dma_counts[eng] += 1
            op.qidx = (i % NSEM[eng], i // NSEM[eng])
        self.ops.append(op)
        for b in reads:
            b.readers.append(op)
        for b in writes:
            b.writer = op
            b.readers = []
        return op

    def finalize_and_emit(self, block, sems):
        for op in self.ops:
            for d in op.deps:
                if d.kind != "dma" and d.eng != op.eng:
                    d.observed = True
        cnt = {e: 0 for e in ENGS}
        for op in self.ops:
            if op.kind == "compute" and op.observed:
                cnt[op.eng] += 1
                op.seq = cnt[op.eng]
        eidx_cnt = {e: 0 for e in ENGS}
        for op in self.ops:
            op.eidx = eidx_cnt[op.eng]
            eidx_cnt[op.eng] += 1
        per_eng = {e: [] for e in ENGS}
        for op in self.ops:
            waits = {}
            if op.kind == "dma" and op.qidx[1] > 0:
                waits[("dma", op.eng, op.qidx[0])] = 16 * op.qidx[1]
            for d in op.deps:
                if d.kind == "dma":
                    key, val = ("dma", d.eng, d.qidx[0]), 16 * (d.qidx[1] + 1)
                elif d.eng != op.eng:
                    key, val = ("eng", d.eng), d.seq
                else:
                    continue
                if waits.get(key, -1) < val:
                    waits[key] = val
            per_eng[op.eng].append((op, waits))

        def emit_engine(eng_obj, eng_name):
            hw = {}
            last_drain = -1
            for op, waits in per_eng[eng_name]:
                need_drain = eng_name != "tensor" and any(
                    d.eng == eng_name and d.kind == "compute"
                    and d.eidx > last_drain for d in op.deps)
                if need_drain:
                    eng_obj.drain()
                    last_drain = op.eidx - 1
                for key in sorted(waits, key=str):
                    val = waits[key]
                    if hw.get(key, 0) >= val:
                        continue
                    hw[key] = val
                    if key[0] == "dma":
                        sem = sems["dma_" + key[1]][key[2]]
                    else:
                        sem = sems["eng_" + key[1]]
                    eng_obj.wait_ge(sem, val)
                ins = op.emit(eng_obj)
                if op.kind == "dma":
                    ins.then_inc(sems["dma_" + op.eng][op.qidx[0]], 16)
                elif op.observed:
                    ins.then_inc(sems["eng_" + op.eng], 1)

        @block.tensor
        def _(t):
            emit_engine(t, "tensor")

        @block.vector
        def _(v):
            emit_engine(v, "vector")

        @block.scalar
        def _(s):
            emit_engine(s, "scalar")

        @block.gpsimd
        def _(g):
            emit_engine(g, "gpsimd")

        @block.sync
        def _(sy):
            emit_engine(sy, "sync")


def build_program(debug=False):
    nc = bass.Bass("TRN2", target_bir_lowering=False)

    d_x = nc.dram_tensor("x_main", [NT, DM], F32, kind="ExternalInput")
    d_xp = nc.dram_tensor("x_prefix", [NT, DM], F32, kind="ExternalInput")
    d_wqT = nc.dram_tensor("wqT", [DM, H * HD], F32, kind="ExternalInput")
    d_wkT = nc.dram_tensor("wkT", [DM, KVH * HD], F32, kind="ExternalInput")
    d_wvT = nc.dram_tensor("wvT", [DM, KVH * HD], F32, kind="ExternalInput")
    d_woT = nc.dram_tensor("woT", [H * HD, DM], F32, kind="ExternalInput")
    d_rtT = nc.dram_tensor("rtT", [128, ICH * NE], F32, kind="ExternalInput")
    d_wg = nc.dram_tensor("wg", [NE, DM, IFF], F32, kind="ExternalInput")
    d_wu = nc.dram_tensor("wu", [NE, DM, IFF], F32, kind="ExternalInput")
    d_wd = nc.dram_tensor("wd", [NE, IFF, DM], F32, kind="ExternalInput")
    d_swg = nc.dram_tensor("swg", [DM, SI], F32, kind="ExternalInput")
    d_swu = nc.dram_tensor("swu", [DM, SI], F32, kind="ExternalInput")
    d_swd = nc.dram_tensor("swd", [SI, DM], F32, kind="ExternalInput")
    d_sgT = nc.dram_tensor("sgT", [128, ICH], F32, kind="ExternalInput")
    d_cos = nc.dram_tensor("cosT", [128, NT], F32, kind="ExternalInput")
    d_sin = nc.dram_tensor("sinT", [128, NT], F32, kind="ExternalInput")
    d_cosp = nc.dram_tensor("cosTp", [128, NT], F32, kind="ExternalInput")
    d_sinp = nc.dram_tensor("sinTp", [128, NT], F32, kind="ExternalInput")
    d_mask = nc.dram_tensor("maskJI", [128, 128], F32, kind="ExternalInput")
    d_id32 = nc.dram_tensor("id32", [128, 128], F32, kind="ExternalInput")
    d_idb = nc.dram_tensor("idb", [128, 128], F32, kind="ExternalInput")
    d_rmat = nc.dram_tensor("rmat", [128, 128], F32, kind="ExternalInput")
    d_ones = nc.dram_tensor("onesb", [128, 1], F32, kind="ExternalInput")
    d_pmask = nc.dram_tensor("pmask", [128, 1], F32, kind="ExternalInput")
    d_out = nc.dram_tensor("out", [NT, DM], F32, kind="ExternalOutput")

    dbg = {}
    if debug:
        dbg["xn"] = nc.dram_tensor("dbg_xn", [128, ICH * NT], BF16, kind="ExternalOutput")
        dbg["qf"] = nc.dram_tensor("dbg_qf", [128, H * NT], BF16, kind="ExternalOutput")
        dbg["kf"] = nc.dram_tensor("dbg_kf", [128, KVH * NT], BF16, kind="ExternalOutput")
        dbg["vt"] = nc.dram_tensor("dbg_vt", [128, TCH * KVH * 129], BF16, kind="ExternalOutput")
        dbg["ao"] = nc.dram_tensor("dbg_ao", [128, H * NT], BF16, kind="ExternalOutput")
        dbg["cmb"] = nc.dram_tensor("dbg_cmb", [128, TCH * NE], F32, kind="ExternalOutput")
        dbg["sig"] = nc.dram_tensor("dbg_sig", [128, TCH], F32, kind="ExternalOutput")
        dbg["oa1"] = nc.dram_tensor("dbg_oa1", [NT, DM], F32, kind="ExternalOutput")
        dbg["pkf"] = nc.dram_tensor("dbg_pkf", [128, KVH * NT], BF16, kind="ExternalOutput")
        dbg["pvt"] = nc.dram_tensor("dbg_pvt", [128, TCH * KVH * 129], BF16, kind="ExternalOutput")
        dbg["pst"] = nc.dram_tensor("dbg_pst", [128, KVH * 129], F32, kind="ExternalOutput")
        dbg["h0"] = nc.dram_tensor("dbg_h0", [128, IFCH * NT], BF16, kind="ExternalOutput")
        dbg["dsl0"] = nc.dram_tensor("dbg_dsl0", [128, IFCH * 512], BF16, kind="ExternalOutput")
        dbg["oau0"] = nc.dram_tensor("dbg_oau0", [NT, DM], F32, kind="ExternalOutput")

    OA = nc.alloc_sbuf_tensor("OA", [128, TCH * DM], F32)
    XN = nc.alloc_sbuf_tensor("XN", [128, ICH * NT], BF16)
    A1 = nc.alloc_sbuf_tensor("A1", [128, 16 * NT], BF16)
    KF = nc.alloc_sbuf_tensor("KF", [128, KVH * NT], BF16)
    VT = nc.alloc_sbuf_tensor("VT", [128, TCH * KVH * 129], BF16)
    KTOK = nc.alloc_sbuf_tensor("KTOK", [128, KVH * 128], BF16)
    STS = nc.alloc_sbuf_tensor("STS", [128, KVH * 129], BF16)
    STS32 = nc.alloc_sbuf_tensor("STS32", [128, KVH * 129], F32)
    U = nc.alloc_sbuf_tensor("U", [128, 5632], F32)
    UB = U.bitcast(BF16)

    WA = nc.alloc_sbuf_tensor("WA", [128, 12288], BF16)
    GS = nc.alloc_sbuf_tensor("GS", [128, 2 * 512], F32)
    MASK = nc.alloc_sbuf_tensor("MASK", [128, 128], F32)
    ID32 = nc.alloc_sbuf_tensor("ID32", [128, 128], F32)
    IDB = nc.alloc_sbuf_tensor("IDB", [128, 128], BF16)
    RMAT = nc.alloc_sbuf_tensor("RMAT", [128, 128], BF16)
    ONESB = nc.alloc_sbuf_tensor("ONESB", [128, 1], BF16)
    PMASK = nc.alloc_sbuf_tensor("PMASK", [128, 1], F32)
    RT32 = nc.alloc_sbuf_tensor("RT32", [128, ICH * NE], F32)
    SGT = nc.alloc_sbuf_tensor("SGT", [128, ICH], BF16)
    CMB = nc.alloc_sbuf_tensor("CMB", [128, TCH * NE], F32)
    SIG = nc.alloc_sbuf_tensor("SIG", [128, TCH], F32)
    AM = nc.alloc_sbuf_tensor("AM", [128, 2 * 128], BF16)
    STAT = nc.alloc_sbuf_tensor("STAT", [128, 16 * 8], F32)
    T2 = nc.alloc_sbuf_tensor("T2", [128, 32], F32)
    DEN = nc.alloc_sbuf_tensor("DEN", [128, 16], F32)
    LG = nc.alloc_sbuf_tensor("LG", [128, TCH * NE], F32)
    TOP = nc.alloc_sbuf_tensor("TOP", [128, 192], F32)
    TRASH = nc.alloc_sbuf_tensor("TRASH", [128, 2048], BF16)
    CSB = TRASH

    PS = [nc.alloc_psum_tensor("ps%d" % i, [128, 512], F32) for i in range(7)]
    PT = nc.alloc_psum_tensor("pt", [128, 1024], BF16)

    p = Prog()

    b_oa = [Buf("oa%d" % i) for i in range(TCH)]
    b_xn = Buf("xn")
    b_a1 = [Buf("a1_%d" % h) for h in range(16)]
    b_kf = Buf("kf")
    b_vt = Buf("vt")
    b_ktok = Buf("ktok")
    b_sts = Buf("sts")
    b_sts32 = [Buf("sts32_%d" % g) for g in range(KVH)]
    b_gs = [Buf("gs0"), Buf("gs1"), Buf("gs2")]
    b_ps = [Buf("ps%d" % i, psum=True) for i in range(7)]
    b_pt = Buf("pt", psum=True)
    b_const = Buf("const")
    b_rt = Buf("rt")
    b_cmb = Buf("cmb")
    b_am = [Buf("am0"), Buf("am1")]
    b_stat = [Buf("stat%d" % i) for i in range(16)]
    b_stat2 = [Buf("statq%d" % i) for i in range(16)]
    b_t2 = Buf("t2")
    b_den = [Buf("den%d" % i) for i in range(4)]
    b_denr = Buf("denr")
    b_lg = Buf("lg")
    b_csb = b_trash = Buf("trash")
    b_top = Buf("top")
    b_wa = [Buf("wa%d" % i) for i in range(24)]
    b_u = [Buf("u%d" % i) for i in range(11)]

    def wa_bufs(c0, c1):
        return b_wa[c0 // 512: (c1 + 511) // 512]

    def u_bufs(byte0, byte1):
        return b_u[byte0 // 2048: (byte1 + 2047) // 2048]

    def bcast(t, col, n):
        return bass.AP(t, col, [[t.shape[1], 128], [0, n]])

    def dma(eng, dst_ap, src_ap, reads=(), writes=(), deps=(), tag=""):
        return p.add(eng, lambda e, d=dst_ap, s=src_ap: e.dma_start(d, s),
                     reads=reads, writes=writes, deps=deps, kind="dma", tag=tag)

    dma("sync", MASK[:, :], d_mask.ap(), writes=[b_const], tag="mask")
    dma("sync", ID32[:, :], d_id32.ap(), writes=[b_const], tag="id32")
    dma("gpsimd", IDB[:, :], d_idb.ap(), writes=[b_const], tag="idb")
    dma("gpsimd", RMAT[:, :], d_rmat.ap(), writes=[b_const], tag="rmat")
    dma("gpsimd", ONESB[:, :], d_ones.ap(), writes=[b_const], tag="ones")
    dma("sync", PMASK[:, :], d_pmask.ap(), writes=[b_const], tag="pmask")
    dma("sync", RT32[:, :], d_rtT.ap(), writes=[b_rt], tag="rt")
    dma("gpsimd", SGT[:, :], d_sgT.ap(), writes=[b_rt], tag="sgt")
    for ct in range(TCH):
        dma("sync", OA[:, ct * DM:(ct + 1) * DM], d_x[ct * 128:(ct + 1) * 128, :],
            writes=[b_oa[ct]], tag="xmain%d" % ct)

    ones_ap = bass.AP(VT, 128, [[TCH * KVH * 129, 128], [129, TCH * KVH], [1, 1]])
    p.add("vector", lambda e, a=ones_ap: e.memset(a, 1.0), writes=[b_vt], tag="vtones")
    p.add("vector", lambda e: e.memset(STS32[:, :], 0.0), writes=b_sts32, tag="sts0")

    def ln_statsA(uid, idx, src_ap, src_bufs, sum_eng="vector"):
        st = idx * 8
        sb = b_stat[idx]
        sb2 = b_stat2[idx]
        if sum_eng == "vector":
            p.add("vector", lambda e, s=src_ap, st=st: e.reduce_sum(
                STAT[:, st:st + 1], s, axis=AX.X),
                reads=src_bufs, writes=[sb], tag="lnsum%s" % uid)
        else:
            p.add("scalar", lambda e, s=src_ap, st=st: e.activation(
                TRASH[:, :], s, AF.Copy, accum_out=STAT[:, st:st + 1]),
                reads=src_bufs, writes=[sb, b_trash], tag="lnsum%s" % uid)
        p.add("scalar", lambda e, s=src_ap, st=st: e.activation(
            GS.bitcast(BF16)[:, :], s, AF.Square, accum_out=STAT[:, st + 1:st + 2]),
            reads=src_bufs, writes=[sb2] + b_gs, tag="lnsq%s" % uid)

    def ln_statsB(uid, idx):
        st = idx * 8
        sb = b_stat[idx]
        sb2 = b_stat2[idx]

        def stats(e, st=st):
            e.tensor_scalar(STAT[:, st + 2:st + 3], STAT[:, st:st + 1], 1.0 / DM, None, ALU.mult)
            e.tensor_scalar(STAT[:, st + 4:st + 5], STAT[:, st + 1:st + 2], 1.0 / DM, None, ALU.mult)
            e.drain()
            e.tensor_tensor(STAT[:, st + 3:st + 4], STAT[:, st + 2:st + 3],
                            STAT[:, st + 2:st + 3], ALU.mult)
            e.drain()
            e.tensor_tensor(STAT[:, st + 4:st + 5], STAT[:, st + 4:st + 5],
                            STAT[:, st + 3:st + 4], ALU.subtract)
            e.drain()
            return e.tensor_scalar(STAT[:, st + 4:st + 5], STAT[:, st + 4:st + 5],
                                   LN_EPS, None, ALU.add)
        p.add("vector", stats, reads=[sb, sb2], writes=[sb], tag="lnstat%s" % uid)

        def rstd(e, st=st):
            e.activation(STAT[:, st + 7:st + 8], STAT[:, st + 4:st + 5], AF.Ln)
            e.drain()
            return e.activation(STAT[:, st + 6:st + 7], STAT[:, st + 7:st + 8],
                                AF.Exp, scale=-0.5)
        p.add("scalar", rstd, reads=[sb], writes=[sb], tag="lnrstd%s" % uid)

    def ln_norm(uid, idx, src_ap, src_bufs, xr_slot, dstT, dst_bufs, keep32=False,
                xs=None, xr_b=None):
        st = idx * 8
        sb = b_stat[idx]
        if xs is None:
            xs = U[:, xr_slot * 2048:(xr_slot + 1) * 2048]
            xr_b = u_bufs(xr_slot * 8192, (xr_slot + 1) * 8192)
        p.add("vector", lambda e, s=src_ap, x=xs, st=st: e.tensor_scalar(
            x, s, STAT[:, st + 2:st + 3], STAT[:, st + 6:st + 7], ALU.subtract, ALU.mult),
            reads=src_bufs + [sb], writes=xr_b, tag="lnxn%s" % uid)
        ct = int(uid.split("_")[-1])
        for q in range(4):
            bank = q % 2

            def tp(e, q=q, bank=bank, x=xs):
                ins = None
                for k in range(4):
                    ic = q * 4 + k
                    ins = e.transpose(PS[bank][:, k * 128:(k + 1) * 128],
                                      x[:, ic * 128:(ic + 1) * 128], ID32[:, :])
                return ins
            p.add("tensor", tp, reads=xr_b + [b_const], writes=[b_ps[bank]],
                  tag="lntp%s_%d" % (uid, q))
            dap = bass.AP(dstT, (q * 4) * NT + ct * 128, [[dstT.shape[1], 128], [NT, 4], [1, 128]])
            ev_eng = "vector" if q % 2 == 0 else "scalar"

            def ev(e, a=dap, bank=bank):
                if hasattr(e, "tensor_copy"):
                    return e.tensor_copy(a, PS[bank][:, 0:512])
                return e.copy(a, PS[bank][:, 0:512])
            p.add(ev_eng, ev, reads=[b_ps[bank]],
                  writes=[dst_bufs[q * 4 + k2] for k2 in range(4)],
                  tag="lnev%s_%d" % (uid, q))
            if keep32:
                oslot = 1 - xr_slot
                d32 = U[:, oslot * 2048 + q * 512: oslot * 2048 + (q + 1) * 512]
                k_eng = "scalar" if q % 2 == 0 else "vector"

                def k32(e, d=d32, bank=bank):
                    if hasattr(e, "tensor_copy"):
                        return e.tensor_copy(d, PS[bank][:, 0:512])
                    return e.copy(d, PS[bank][:, 0:512])
                p.add(k_eng, k32,
                      reads=[b_ps[bank]], writes=u_bufs(oslot * 8192 + q * 2048,
                                                        oslot * 8192 + (q + 1) * 2048),
                      tag="lnev32%s_%d" % (uid, q))

    def kq_proj(name, wT_d, dout_ch, srcT, src_bufs, dst, dst_bufs, pm, drip=None):
        pend = []

        GSB = GS.bitcast(BF16)

        def emit_rope(oc, tt, bank, rbank, dslice):
            cosap = CSB[:, tt * 512: tt * 512 + 512]
            sinap = CSB[:, NT + tt * 512: NT + tt * 512 + 512]
            p.add("tensor", lambda e, d=dslice, rbank=rbank: e.matmul(
                PS[rbank][:, :], RMAT[:, :], d, start=True, stop=True),
                reads=[dst_bufs[oc], b_const], writes=[b_ps[rbank]],
                tag="%srot%d_%d" % (name, oc, tt))
            p.add("scalar", lambda e, rbank=rbank: e.copy(GSB[:, 1024:1536], PS[rbank][:, :]),
                  reads=[b_ps[rbank]], writes=[b_gs[2]], tag="%srotev%d_%d" % (name, oc, tt))
            p.add("vector", lambda e, c2=cosap, d=dslice: e.tensor_tensor(
                GSB[:, 0:512], d, c2, ALU.mult),
                reads=[dst_bufs[oc], b_csb], writes=[b_gs[0]], tag="%st1%d_%d" % (name, oc, tt))
            p.add("vector", lambda e, s2=sinap: e.tensor_tensor(
                GSB[:, 512:1024], GSB[:, 1024:1536], s2, ALU.mult),
                reads=[b_gs[2], b_csb], writes=[b_gs[1]], tag="%st2%d_%d" % (name, oc, tt))
            p.add("vector", lambda e: e.tensor_tensor(GSB[:, 0:512], GSB[:, 0:512],
                                                      GSB[:, 512:1024], ALU.add),
                  reads=[b_gs[0], b_gs[1]], writes=[b_gs[0]], tag="%st3%d_%d" % (name, oc, tt))
            p.add("vector", lambda e: e.tensor_scalar(GSB[:, 512:1024], GSB[:, 0:512],
                                                      0.0, None, ALU.min),
                  reads=[b_gs[0]], writes=[b_gs[1]], tag="%smin%d_%d" % (name, oc, tt))
            p.add("scalar", lambda e: e.activation(GSB[:, 512:1024], GSB[:, 512:1024], AF.Exp),
                  reads=[b_gs[1]], writes=[b_gs[1]], tag="%sexp%d_%d" % (name, oc, tt))
            p.add("vector", lambda e: e.tensor_scalar(GSB[:, 0:512], GSB[:, 0:512],
                                                      0.0, None, ALU.max),
                  reads=[b_gs[0]], writes=[b_gs[0]], tag="%smax%d_%d" % (name, oc, tt))
            if pm:
                def fin(e, d=dslice):
                    e.tensor_tensor(GSB[:, 0:512], GSB[:, 0:512], GSB[:, 512:1024], ALU.add)
                    e.drain()
                    return e.tensor_scalar(d, GSB[:, 0:512], PMASK[:, 0:1], None, ALU.mult)
                p.add("vector", fin, reads=[b_gs[0], b_gs[1], b_const],
                      writes=[dst_bufs[oc], b_gs[0]], tag="%sfin%d_%d" % (name, oc, tt))
            else:
                p.add("vector", lambda e, d=dslice: e.tensor_tensor(
                    d, GSB[:, 0:512], GSB[:, 512:1024], ALU.add),
                    reads=[b_gs[0], b_gs[1]], writes=[dst_bufs[oc]],
                    tag="%sfin%d_%d" % (name, oc, tt))

        for oc in range(dout_ch):
            if drip:
                drip(oc)
            wslot = oc % 2
            wbase = wslot * 2048
            wview = WA[:, wbase:wbase + 2048].rearrange("p (c f) -> p c f", f=128)
            dma("gpsimd", wview,
                wT_d[:, oc * 128:(oc + 1) * 128].rearrange("(c p) f -> p c f", p=128),
                writes=wa_bufs(wbase, wbase + 2048), tag="%sw%d" % (name, oc))
            for tt in range(2):
                k_idx = oc * 2 + tt
                bank = k_idx % 4

                def mm(e, tt=tt, wbase=wbase, bank=bank, srcT=srcT):
                    ins = None
                    for ic in range(ICH):
                        ins = e.matmul(PS[bank][:, :],
                                       WA[:, wbase + ic * 128: wbase + (ic + 1) * 128],
                                       srcT[:, ic * NT + tt * 512: ic * NT + tt * 512 + 512],
                                       start=(ic == 0), stop=(ic == ICH - 1))
                    return ins
                p.add("tensor", mm, reads=wa_bufs(wbase, wbase + 2048) + src_bufs,
                      writes=[b_ps[bank]], tag="%smm%d_%d" % (name, oc, tt))
                dslice = dst[:, oc * NT + tt * 512: oc * NT + tt * 512 + 512]
                p.add("scalar", lambda e, d=dslice, bank=bank: e.copy(d, PS[bank][:, :]),
                      reads=[b_ps[bank]], writes=[dst_bufs[oc]], tag="%sev%d_%d" % (name, oc, tt))
                pend.append((oc, tt, bank, 4 + k_idx % 2, dslice))
                if len(pend) > 3:
                    emit_rope(*pend.pop(0))
        while pend:
            emit_rope(*pend.pop(0))

    def v_proj(name, srcT, src_bufs, load_w):
        if load_w:
            wview = WA[:, 4096:4096 + ICH * 512].rearrange("p (c f) -> p c f", f=512)
            dma("gpsimd", wview, d_wvT.ap().rearrange("(c p) f -> p c f", p=128),
                writes=wa_bufs(4096, 4096 + ICH * 512), tag="%swv" % name)
        for ct in range(TCH):
            bank = ct % 2

            def mm(e, ct=ct, bank=bank, srcT=srcT):
                ins = None
                for ic in range(ICH):
                    ins = e.matmul(PS[bank][:, :],
                                   srcT[:, ic * NT + ct * 128: ic * NT + ct * 128 + 128],
                                   WA[:, 4096 + ic * 512: 4096 + (ic + 1) * 512],
                                   start=(ic == 0), stop=(ic == ICH - 1))
                return ins
            p.add("tensor", mm, reads=wa_bufs(4096, 4096 + ICH * 512) + src_bufs,
                  writes=[b_ps[bank]], tag="%svmm%d" % (name, ct))
            dap = bass.AP(VT, ct * KVH * 129, [[VT.shape[1], 128], [129, KVH], [1, 128]])
            p.add("vector", lambda e, a=dap, bank=bank: e.tensor_copy(a, PS[bank][:, :]),
                  reads=[b_ps[bank]], writes=[b_vt], tag="%svev%d" % (name, ct))

    state_banks = [1, 4, 2, 3]

    def state_ktok(name, c):
        def tp(e, c=c):
            ins = None
            for g in range(KVH):
                ins = e.transpose(PT[:, g * 128:(g + 1) * 128],
                                  KF[:, g * NT + c * 128: g * NT + c * 128 + 128], IDB[:, :])
            return ins
        p.add("tensor", tp, reads=[b_kf, b_const], writes=[b_pt], tag="%sktp%d" % (name, c))
        p.add("scalar", lambda e: e.copy(KTOK[:, :], PT[:, 0:KVH * 128]),
              reads=[b_pt], writes=[b_ktok], tag="%sktev%d" % (name, c))

    def state_update(name, c):
        for g in range(KVH):
            bank = state_banks[g]
            p.add("tensor", lambda e, g=g, c=c, bank=bank: e.matmul(
                PS[bank][:, 0:129], KTOK[:, g * 128:(g + 1) * 128],
                VT[:, c * KVH * 129 + g * 129: c * KVH * 129 + (g + 1) * 129],
                start=True, stop=True),
                reads=[b_ktok, b_vt], writes=[b_ps[bank]], tag="%ssmm%d_%d" % (name, c, g))
            p.add("vector", lambda e, g=g, bank=bank: e.tensor_tensor(
                STS32[:, g * 129:(g + 1) * 129], STS32[:, g * 129:(g + 1) * 129],
                PS[bank][:, 0:129], ALU.add),
                reads=[b_ps[bank], b_sts32[g]], writes=[b_sts32[g]],
                tag="%ssacc%d_%d" % (name, c, g))

    WA32 = WA.bitcast(F32)
    main_norms = []

    def emit_main_norm(ct):
        ln_norm("m_%d" % ct, 8 + ct, OA[:, ct * DM:(ct + 1) * DM], [b_oa[ct]], ct % 2,
                XN, [b_xn] * 16, keep32=True)
        oslot = 1 - ct % 2

        def rmm32(e, oslot=oslot):
            ins = None
            for ic in range(ICH):
                ins = e.matmul(PS[4][:, 0:NE],
                               U[:, oslot * 2048 + ic * 128: oslot * 2048 + (ic + 1) * 128],
                               RT32[:, ic * NE:(ic + 1) * NE], start=(ic == 0), stop=(ic == ICH - 1))
            return ins
        p.add("tensor", rmm32, reads=u_bufs(oslot * 8192, (oslot + 1) * 8192) + [b_rt],
              writes=[b_ps[4]], tag="rmm%d" % ct)

        p.add("vector", lambda e, ct=ct: e.tensor_copy(LG[:, ct * NE:(ct + 1) * NE],
                                                       PS[4][:, 0:NE]),
              reads=[b_ps[4]], writes=[b_lg], tag="lgstash%d" % ct)

    def xpre_dma(pc):
        xsl = pc % 3
        dma("scalar", WA32[:, xsl * 2048:(xsl + 1) * 2048], d_xp[pc * 128:(pc + 1) * 128, :],
            writes=wa_bufs(xsl * 4096, (xsl + 1) * 4096), tag="xpre%d" % pc)
    xpre_dma(0)
    xpre_dma(1)
    for pc in range(TCH + 2):
        if pc >= 2:
            qc = pc - 2
            xsl = qc % 3
            xs_ap = WA32[:, xsl * 2048:(xsl + 1) * 2048]
            ln_norm("p_%d" % qc, qc, xs_ap, wa_bufs(xsl * 4096, (xsl + 1) * 4096),
                    0, A1, b_a1, xs=xs_ap, xr_b=wa_bufs(xsl * 4096, (xsl + 1) * 4096))
            emit_main_norm(qc)
        if 1 <= pc and pc + 1 < TCH:
            xpre_dma(pc + 1)
        if pc < TCH:
            xsl = pc % 3
            ln_statsA("p_%d" % pc, pc, WA32[:, xsl * 2048:(xsl + 1) * 2048],
                      wa_bufs(xsl * 4096, (xsl + 1) * 4096), sum_eng="scalar")
            ln_statsA("m_%d" % pc, 8 + pc, OA[:, pc * DM:(pc + 1) * DM], [b_oa[pc]])
        if 1 <= pc <= TCH:
            ln_statsB("p_%d" % (pc - 1), pc - 1)
            ln_statsB("m_%d" % (pc - 1), 8 + pc - 1)

    def g3(t, col):
        return bass.AP(t, col, [[t.shape[1], 128], [NE, TCH], [1, NE]])

    def gbc(t, col):
        return bass.AP(t, col, [[t.shape[1], 128], [1, TCH], [0, NE]])

    def top2all(e):
        EQ1 = TOP[:, 0:64]
        L2 = TOP[:, 64:128]
        T1 = TOP[:, 128:192]
        M1, M2, D, E, W1, W2 = 0, 8, 16, 24, 0, 8
        e.reduce_max(T2[:, 0:8], g3(LG, 0), axis=AX.X)
        e.drain()
        e.tensor_tensor(EQ1, LG[:, :], gbc(T2, 0), ALU.is_equal)
        e.drain()
        e.scalar_tensor_tensor(L2, EQ1, -1e30, LG[:, :], ALU.mult, ALU.add)
        e.drain()
        e.reduce_max(T2[:, 8:16], g3(TOP, 64), axis=AX.X)
        e.drain()
        e.tensor_tensor(L2, L2, gbc(T2, 8), ALU.is_equal)
        e.drain()
        return e.tensor_tensor(T2[:, 16:24], T2[:, 8:16], T2[:, 0:8], ALU.subtract)
    p.add("vector", top2all, reads=[b_lg], writes=[b_top, b_t2], tag="top2all_a")
    p.add("scalar", lambda e: e.activation(T2[:, 24:32], T2[:, 16:24], AF.Exp),
          reads=[b_t2], writes=[b_t2], tag="top2all_e")

    def top2allc(e):
        EQ1 = TOP[:, 0:64]
        EQ2 = TOP[:, 64:128]
        T1 = TOP[:, 128:192]
        e.tensor_scalar(T2[:, 16:24], T2[:, 24:32], 1.0, None, ALU.add)
        e.drain()
        e.reciprocal(T2[:, 16:24], T2[:, 16:24])
        e.drain()
        e.tensor_tensor(T2[:, 24:32], T2[:, 24:32], T2[:, 16:24], ALU.mult)
        e.drain()
        e.tensor_tensor(T1, EQ1, gbc(T2, 16), ALU.mult)
        e.tensor_tensor(EQ2, EQ2, gbc(T2, 24), ALU.mult)
        e.drain()
        return e.tensor_tensor(CMB[:, :], T1, EQ2, ALU.add)
    p.add("vector", top2allc, reads=[b_t2, b_top], writes=[b_cmb, b_top], tag="top2all_c")

    def drip_main_stats(oc):
        for ct in (2 * oc, 2 * oc + 1):
            if ct < TCH:
                ln_stats("m_%d" % ct, 8 + ct, OA[:, ct * DM:(ct + 1) * DM], [b_oa[ct]])
    dma("gpsimd", CSB[:, 0:NT], d_cosp.ap(), writes=[b_csb], tag="cosp")
    dma("gpsimd", CSB[:, NT:2 * NT], d_sinp.ap(), writes=[b_csb], tag="sinp")
    all_a1 = list(b_a1)
    kq_proj("pk", d_wkT.ap(), KVH, A1, all_a1, KF, [b_kf] * KVH, pm=True)
    v_proj("pv", A1, all_a1, load_w=True)
    for c in range(TCH):
        state_ktok("p", c)
        state_update("p", c)
    if debug:
        dma("sync", dbg["pkf"].ap(), KF[:, :], reads=[b_kf], tag="dbgpkf")
        dma("sync", dbg["pvt"].ap(), VT[:, :], reads=[b_vt], tag="dbgpvt")
        dma("sync", dbg["pst"].ap(), STS32[:, :], reads=list(b_sts32), tag="dbgpst")

    if debug:
        for blk in range(ICH):
            dma("sync", dbg["xn"].ap()[:, blk * NT:(blk + 1) * NT], XN[:, blk * NT:(blk + 1) * NT],
                reads=[b_xn], tag="dbgxn%d" % blk)

    dma("gpsimd", CSB[:, 0:NT], d_cos.ap(), writes=[b_csb], tag="cosm")
    dma("gpsimd", CSB[:, NT:2 * NT], d_sin.ap(), writes=[b_csb], tag="sinm")

    kq_proj("q", d_wqT.ap(), H, XN, [b_xn], A1, b_a1, pm=False)
    kq_proj("k", d_wkT.ap(), KVH, XN, [b_xn], KF, [b_kf] * KVH, pm=False)
    v_proj("v", XN, [b_xn], load_w=False)

    if debug:
        for g in range(KVH):
            dma("sync", dbg["kf"].ap()[:, g * NT:(g + 1) * NT], KF[:, g * NT:(g + 1) * NT],
                reads=[b_kf], tag="dbgkf%d" % g)
        for h in range(H):
            dma("sync", dbg["qf"].ap()[:, h * NT:(h + 1) * NT], A1[:, h * NT:(h + 1) * NT],
                reads=[b_a1[h]], tag="dbgqf%d" % h)
        dma("sync", dbg["vt"].ap()[:, :], VT[:, :], reads=[b_vt], tag="dbgvt")

    for c in range(TCH):
        p.add("scalar", lambda e: e.copy(STS[:, :], STS32[:, :]),
              reads=list(b_sts32), writes=[b_sts], tag="stcp%d" % c)
        numrot = [1, 4, 2, 3]

        def emit_abar(h):
            g = h // KVH
            abank = 0 if h % 2 == 0 else 6
            amap = AM[:, (h % 2) * 128:(h % 2) * 128 + 128]
            p.add("tensor", lambda e, g=g, h=h, c=c, ab=abank: e.matmul(
                PS[ab][:, 0:128], KF[:, g * NT + c * 128: g * NT + c * 128 + 128],
                A1[:, h * NT + c * 128: h * NT + c * 128 + 128], start=True, stop=True),
                reads=[b_kf, b_a1[h]], writes=[b_ps[abank]], tag="abar%d_%d" % (c, h))
            p.add("vector", lambda e, a=amap, ab=abank: e.tensor_tensor(
                a, PS[ab][:, 0:128], MASK[:, :], ALU.mult),
                reads=[b_ps[abank], b_const], writes=[b_am[h % 2]], tag="amask%d_%d" % (c, h))

        emit_abar(0)
        for hb in range(4):
            for j in range(4):
                h = hb * 4 + j
                g = h // KVH
                amb = b_am[h % 2]
                amap = AM[:, (h % 2) * 128:(h % 2) * 128 + 128]
                nbank = numrot[j]
                if h + 1 < H:
                    emit_abar(h + 1)

                def nmm(e, g=g, h=h, c=c, a=amap, nbank=nbank):
                    e.matmul(PS[nbank][:, 0:129], a,
                             VT[:, c * KVH * 129 + g * 129: c * KVH * 129 + (g + 1) * 129],
                             start=True, stop=False)
                    return e.matmul(PS[nbank][:, 0:129],
                                    A1[:, h * NT + c * 128: h * NT + c * 128 + 128],
                                    STS[:, g * 129:(g + 1) * 129], start=False, stop=True)
                p.add("tensor", nmm, reads=[amb, b_vt, b_a1[h], b_sts], writes=[b_ps[nbank]],
                      tag="nmm%d_%d" % (c, h))
                p.add("vector", lambda e, j=j, nbank=nbank: e.tensor_scalar(
                    DEN[:, j:j + 1], PS[nbank][:, 128:129], ATT_EPS, None, ALU.add),
                    reads=[b_ps[nbank]], writes=[b_den[j]], tag="dstash%d_%d" % (c, h))
            p.add("vector", lambda e: e.reciprocal(DEN[:, 4:8], DEN[:, 0:4]),
                  reads=list(b_den), writes=[b_denr], tag="drecip%d_%d" % (c, hb))
            for j in range(4):
                h = hb * 4 + j
                nbank = numrot[j]
                p.add("scalar", lambda e, h=h, c=c, j=j, nbank=nbank: e.activation(
                    A1[:, h * NT + c * 128: h * NT + c * 128 + 128], PS[nbank][:, 0:128],
                    AF.Copy, scale=DEN[:, 4 + j:5 + j]),
                    reads=[b_ps[nbank], b_denr], writes=[b_a1[h]], tag="oev%d_%d" % (c, h))
        state_ktok("m", c)
        state_update("m", c)
        for hh in range(2):
            def tp(e, c=c, hh=hh):
                ins = None
                for k in range(8):
                    h = hh * 8 + k
                    ins = e.transpose(PT[:, k * 128:(k + 1) * 128],
                                      A1[:, h * NT + c * 128: h * NT + c * 128 + 128], IDB[:, :])
                return ins
            p.add("tensor", tp, reads=[b_a1[hh * 8 + k2] for k2 in range(8)] + [b_const],
                  writes=[b_pt], tag="flip%d_%d" % (c, hh))
            dap = bass.AP(A1, (hh * 8) * NT + c * 128, [[A1.shape[1], 128], [NT, 8], [1, 128]])
            if hh == 0:
                p.add("vector", lambda e, a=dap: e.tensor_copy(a, PT[:, :]),
                      reads=[b_pt], writes=[b_a1[hh * 8 + k2] for k2 in range(8)],
                      tag="flipev%d_%d" % (c, hh))
            else:
                p.add("scalar", lambda e, a=dap: e.copy(a, PT[:, :]),
                      reads=[b_pt], writes=[b_a1[hh * 8 + k2] for k2 in range(8)],
                      tag="flipev%d_%d" % (c, hh))

    if debug:
        for h in range(H):
            dma("sync", dbg["ao"].ap()[:, h * NT:(h + 1) * NT], A1[:, h * NT:(h + 1) * NT],
                reads=[b_a1[h]], tag="dbgao%d" % h)

    for fq in range(4):
        if fq % 2 == 0:
            sl_t, sl_bufs = UB, u_bufs(0, 16384)
        else:
            sl_t, sl_bufs = WA, wa_bufs(0, 8192)
        for qq in range(4):
            dst = sl_t[:, qq * 2048:(qq + 1) * 2048].rearrange("p (c f) -> p c f", f=512)
            src = d_woT[qq * 512:(qq + 1) * 512, fq * 512:(fq + 1) * 512].rearrange(
                "(c p) f -> p c f", p=128)
            dma("gpsimd", dst, src, writes=sl_bufs, tag="wo%d_%d" % (fq, qq))
        for ctg in range(4):
            cts = [ctg * 2, ctg * 2 + 1]
            pair = [[2, 3], [5, 6]][ctg % 2]
            for cc in range(16):
                def mm(e, cc=cc, cts=cts, sl_t=sl_t, pair=pair):
                    ins = None
                    for k, ct in enumerate(cts):
                        ins = e.matmul(PS[pair[k]][:, :],
                                       A1[:, cc * NT + ct * 128: cc * NT + ct * 128 + 128],
                                       sl_t[:, cc * 512:(cc + 1) * 512],
                                       start=(cc == 0), stop=(cc == 15))
                    return ins
                p.add("tensor", mm, reads=sl_bufs + [b_a1[cc]],
                      writes=[b_ps[k] for k in pair], tag="womm%d_%d_%d" % (fq, ctg, cc))
            for k, ct in enumerate(cts):
                p.add("vector", lambda e, k=k, ct=ct, fq=fq, pair=pair: e.tensor_tensor(
                    OA[:, ct * DM + fq * 512: ct * DM + (fq + 1) * 512],
                    OA[:, ct * DM + fq * 512: ct * DM + (fq + 1) * 512],
                    PS[pair[k]][:, :], ALU.add),
                    reads=[b_ps[pair[k]], b_oa[ct]], writes=[b_oa[ct]],
                    tag="woev%d_%d_%d" % (fq, ctg, ct))

    if debug:
        for ct in range(TCH):
            dma("sync", dbg["oa1"].ap()[ct * 128:(ct + 1) * 128, :], OA[:, ct * DM:(ct + 1) * DM],
                reads=[b_oa[ct]], tag="dbgoa1_%d" % ct)

    for ct in range(TCH):
        def gmm(e, ct=ct):
            ins = None
            for ic in range(ICH):
                ins = e.matmul(PS[4][:, 0:1],
                               XN[:, ic * NT + ct * 128: ic * NT + ct * 128 + 128],
                               SGT[:, ic:ic + 1], start=(ic == 0), stop=(ic == ICH - 1))
            return ins
        p.add("tensor", gmm, reads=[b_xn, b_rt], writes=[b_ps[4]], tag="gmm%d" % ct)
        p.add("scalar", lambda e, ct=ct: e.activation(SIG[:, ct:ct + 1], PS[4][:, 0:1], AF.Sigmoid),
              reads=[b_ps[4]], writes=[b_cmb], tag="sigev%d" % ct)

    if debug:
        dma("sync", dbg["cmb"].ap(), CMB[:, :], reads=[b_cmb], tag="dbgcmb")
        dma("sync", dbg["sig"].ap(), SIG[:, :], reads=[b_cmb], tag="dbgsig")

    def unit_srcs(u):
        if u < NE:
            return d_wg.ap()[u], d_wu.ap()[u], d_wd.ap()[u]
        blk = u - NE
        return (d_swg.ap()[:, blk * IFF:(blk + 1) * IFF],
                d_swu.ap()[:, blk * IFF:(blk + 1) * IFF],
                d_swd.ap()[blk * IFF:(blk + 1) * IFF, :])

    gu_dma_ops = {}
    gu_last_mm = {}

    def emit_gu_dmas(u, lo=0, hi=IFCH):
        wg_src, wu_src, _ = unit_srcs(u)
        if u not in gu_dma_ops:
            gu_dma_ops[u] = {}
        for i in range(lo, hi):
            gbase = (i % 2) * 2048
            ubase = 4096 + (i % 2) * 2048
            wr_g = wa_bufs(gbase, gbase + 2048) if i < 2 else []
            wr_u = wa_bufs(ubase, ubase + 2048) if i < 2 else []
            dep_g = [gu_last_mm[("g", i % 2)]] if (i < 2 and ("g", i % 2) in gu_last_mm) else []
            dep_u = [gu_last_mm[("u", i % 2)]] if (i < 2 and ("u", i % 2) in gu_last_mm) else []
            og = dma("gpsimd", WA[:, gbase:gbase + 2048].rearrange("p (c f) -> p c f", f=128),
                     wg_src[:, i * 128:(i + 1) * 128].rearrange("(c p) f -> p c f", p=128),
                     writes=wr_g, deps=dep_g, tag="u%dgw%d" % (u, i))
            ou = dma("gpsimd", WA[:, ubase:ubase + 2048].rearrange("p (c f) -> p c f", f=128),
                     wu_src[:, i * 128:(i + 1) * 128].rearrange("(c p) f -> p c f", p=128),
                     writes=wr_u, deps=dep_u, tag="u%duw%d" % (u, i))
            gu_dma_ops[u][("g", i)] = og
            gu_dma_ops[u][("u", i)] = ou

    down_mm_last = {0: None, 1: None}
    dw_ops = {}
    dw_pending = {}

    def emit_down_dma(u, ft):
        _, _, wd_src = unit_srcs(u)
        dsl = ft % 2
        dview = UB[:, dsl * 5632:(dsl + 1) * 5632].rearrange("p (c f) -> p c f", f=512)
        wr = u_bufs(dsl * 11264, (dsl + 1) * 11264) if (u == 0 and ft < 2) else []
        op = dma("gpsimd", dview, wd_src[:, ft * 512:(ft + 1) * 512].rearrange(
            "(c p) f -> p c f", p=128),
            writes=wr, tag="u%ddw%d" % (u, ft))
        dw_ops[(u, ft)] = op
        dw_pending.setdefault((u, ft) if ft >= 2 else ("n", u, dsl), op)
        return op

    out_ops = []
    emit_gu_dmas(0)
    emit_down_dma(0, 0)
    emit_down_dma(0, 1)
    for u in range(NUNIT):
        pool = [0, 1, 4]
        for i in range(IFCH):
            gbase = (i % 2) * 2048
            ubase = 4096 + (i % 2) * 2048
            for tt in range(2):
                k = i * 2 + tt
                bg = pool[(2 * k) % 3]
                bu = pool[(2 * k + 1) % 3]

                def gm(e, tt=tt, gbase=gbase, bg=bg):
                    ins = None
                    for ic in range(ICH):
                        ins = e.matmul(PS[bg][:, :],
                                       WA[:, gbase + ic * 128: gbase + (ic + 1) * 128],
                                       XN[:, ic * NT + tt * 512: ic * NT + tt * 512 + 512],
                                       start=(ic == 0), stop=(ic == ICH - 1))
                    return ins
                op_gm = p.add("tensor", gm, reads=[b_xn], writes=[b_ps[bg]],
                              deps=[gu_dma_ops[u][("g", i)]], tag="u%dgm%d_%d" % (u, i, tt))
                if ("g", i + 2) in gu_dma_ops[u]:
                    gu_dma_ops[u][("g", i + 2)].deps.append(op_gm)
                gu_last_mm[("g", i % 2)] = op_gm

                def um(e, tt=tt, ubase=ubase, bu=bu):
                    ins = None
                    for ic in range(ICH):
                        ins = e.matmul(PS[bu][:, :],
                                       WA[:, ubase + ic * 128: ubase + (ic + 1) * 128],
                                       XN[:, ic * NT + tt * 512: ic * NT + tt * 512 + 512],
                                       start=(ic == 0), stop=(ic == ICH - 1))
                    return ins
                op_um = p.add("tensor", um, reads=[b_xn], writes=[b_ps[bu]],
                              deps=[gu_dma_ops[u][("u", i)]], tag="u%dum%d_%d" % (u, i, tt))
                if ("u", i + 2) in gu_dma_ops[u]:
                    gu_dma_ops[u][("u", i + 2)].deps.append(op_um)
                gu_last_mm[("u", i % 2)] = op_um
                gsl = k % 2
                p.add("scalar", lambda e, gsl=gsl, bg=bg: e.activation(
                    GS[:, gsl * 512:(gsl + 1) * 512], PS[bg][:, :], AF.Silu),
                    reads=[b_ps[bg]], writes=[b_gs[gsl]], tag="u%dsil%d_%d" % (u, i, tt))
                hslice = A1[:, i * NT + tt * 512: i * NT + tt * 512 + 512]
                p.add("vector", lambda e, h2=hslice, gsl=gsl, bu=bu: e.tensor_tensor(
                    h2, GS[:, gsl * 512:(gsl + 1) * 512], PS[bu][:, :], ALU.mult),
                    reads=[b_gs[gsl], b_ps[bu]], writes=[b_a1[i]], tag="u%dh%d_%d" % (u, i, tt))
        if u + 1 < NUNIT:
            emit_gu_dmas(u + 1, 0, 2)
        dw23 = [emit_down_dma(u, 2), emit_down_dma(u, 3)]
        if u + 1 < NUNIT:
            dw01n = [emit_down_dma(u + 1, 0), emit_down_dma(u + 1, 1)]
        else:
            dw01n = []
        if u + 1 < NUNIT:
            emit_gu_dmas(u + 1, 2, IFCH)
        for ft in range(4):
            dsl = ft % 2
            dbase = dsl * 5632
            for ctg in range(4):
                cts = [ctg * 2, ctg * 2 + 1]
                pair = [[2, 3], [5, 6]][ctg % 2]

                def dmm(e, cts=cts, dbase=dbase, pair=pair):
                    ins = None
                    for i in range(IFCH):
                        for kk, ct in enumerate(cts):
                            ins = e.matmul(PS[pair[kk]][:, :],
                                           A1[:, i * NT + ct * 128: i * NT + ct * 128 + 128],
                                           UB[:, dbase + i * 512: dbase + (i + 1) * 512],
                                           start=(i == 0), stop=(i == IFCH - 1))
                    return ins
                op_dmm = p.add("tensor", dmm, reads=[b_a1[i2] for i2 in range(IFCH)],
                               writes=[b_ps[k] for k in pair],
                               deps=[dw_ops[(u, ft)]],
                               tag="u%ddm%d_%d" % (u, ft, ctg))
                if ft < 2:
                    dw23[ft].deps.append(op_dmm)
                elif dw01n:
                    dw01n[ft - 2].deps.append(op_dmm)
                down_mm_last[ft % 2] = op_dmm
                for kk, ct in enumerate(cts):
                    oaslice = OA[:, ct * DM + ft * 512: ct * DM + (ft + 1) * 512]
                    sap = CMB[:, ct * NE + u: ct * NE + u + 1] if u < NE else SIG[:, ct:ct + 1]
                    p.add("vector", lambda e, o=oaslice, kk=kk, sap=sap, pair=pair:
                          e.scalar_tensor_tensor(o, PS[pair[kk]][:, :], sap, o,
                                                 ALU.mult, ALU.add),
                          reads=[b_ps[pair[kk]], b_cmb, b_oa[ct]], writes=[b_oa[ct]],
                          tag="u%ddev%d_%d" % (u, ft, ct))
        if debug and u == 0:
            for i2 in range(IFCH):
                dma("sync", dbg["h0"].ap()[:, i2 * NT:(i2 + 1) * NT], A1[:, i2 * NT:(i2 + 1) * NT],
                    reads=[b_a1[i2]], tag="dbgh0_%d" % i2)
            _dmp = dma("sync", dbg["dsl0"].ap(), UB[:, 0:IFCH * 512],
                       deps=[dw_ops[(0, 2)], down_mm_last[0]], tag="dbgdsl0")
            if (1, 0) in dw_ops:
                dw_ops[(1, 0)].deps.append(_dmp)
            for ct in range(TCH):
                dma("sync", dbg["oau0"].ap()[ct * 128:(ct + 1) * 128, :],
                    OA[:, ct * DM:(ct + 1) * DM], reads=[b_oa[ct]], tag="dbgoau0_%d" % ct)
        if u == NUNIT - 1:
            for ct in range(TCH):
                out_ops.append(dma("sync", d_out[ct * 128:(ct + 1) * 128, :],
                                   OA[:, ct * DM:(ct + 1) * DM],
                                   reads=[b_oa[ct]], tag="outst%d" % ct))

    p.add("scalar", lambda e: e.memzero(STAT[:, 0:1]), deps=out_ops, tag="finwait")

    import contextlib
    with contextlib.ExitStack() as st:
        sems = {}
        for e in ENGS:
            sems["eng_" + e] = st.enter_context(nc.semaphore("s_" + e))
        for q in ("gpsimd", "sync", "scalar"):
            sems["dma_" + q] = [st.enter_context(nc.semaphore("sd_%s_%d" % (q, i)))
                                for i in range(NSEM[q])]
        block = st.enter_context(nc.Block())
        p.finalize_and_emit(block, sems)
    return nc


_cached = {}


def _get_nc(debug=False):
    if debug not in _cached:
        _cached[debug] = build_program(debug)
    return _cached[debug]


def _rope_tables():
    inv_freq = 1.0 / ROPE_BASE ** (np.arange(0, HD, 2, dtype=np.float32) / HD)
    t = np.arange(S, dtype=np.float32)
    freqs = np.outer(t, inv_freq).astype(np.float32)
    emb = np.concatenate([freqs, freqs], -1)
    return np.cos(emb).astype(np.float32), np.sin(emb).astype(np.float32)


def make_in_maps(inputs):
    x = np.asarray(inputs["L0_attn_x"], dtype=np.float32)
    wq = np.asarray(inputs["L0_attn_Wq"], dtype=np.float32)
    wk = np.asarray(inputs["L0_attn_Wk"], dtype=np.float32)
    wv = np.asarray(inputs["L0_attn_Wv"], dtype=np.float32)
    wo = np.asarray(inputs["L0_attn_Wo"], dtype=np.float32)
    rw = np.asarray(inputs["L0_ffn_router_w"], dtype=np.float32)
    wg = np.ascontiguousarray(np.asarray(inputs["L0_ffn_w_gate"], dtype=np.float32))
    wu = np.ascontiguousarray(np.asarray(inputs["L0_ffn_w_up"], dtype=np.float32))
    wd = np.ascontiguousarray(np.asarray(inputs["L0_ffn_w_down"], dtype=np.float32))
    swg = np.ascontiguousarray(np.asarray(inputs["L0_ffn_sw_gate"], dtype=np.float32))
    swu = np.ascontiguousarray(np.asarray(inputs["L0_ffn_sw_up"], dtype=np.float32))
    swd = np.ascontiguousarray(np.asarray(inputs["L0_ffn_sw_down"], dtype=np.float32))
    sg = np.asarray(inputs["L0_ffn_shared_gate_w"], dtype=np.float32)

    wqT = np.ascontiguousarray(wq.T)
    wkT = np.ascontiguousarray(wk.T)
    wvT = np.ascontiguousarray(wv.T)
    woT = np.ascontiguousarray(wo.T)
    rtT = np.ascontiguousarray(rw.reshape(NE, ICH, 128).transpose(2, 1, 0).reshape(128, ICH * NE))
    sgT = np.ascontiguousarray(sg.reshape(ICH, 128).T)

    cos, sin = _rope_tables()
    cosT = np.ascontiguousarray(cos.T)
    sinT = np.ascontiguousarray(sin.T)

    j = np.arange(128)
    maskJI = (j[:, None] <= j[None, :]).astype(np.float32)
    id32 = np.eye(128, dtype=np.float32)
    idb = np.eye(128, dtype=np.float32)
    R = np.zeros((128, 128), dtype=np.float32)
    for dp in range(64):
        R[dp + 64, dp] = -1.0
    for dp in range(64, 128):
        R[dp - 64, dp] = 1.0
    rmat = R
    onesb = np.ones((128, 1), dtype=np.float32)

    in_maps = []
    for core in range(8):
        b = core // 2
        hh = core % 2
        in_maps.append({
            "x_main": np.ascontiguousarray(x[b, hh * NT:(hh + 1) * NT, :]),
            "x_prefix": np.ascontiguousarray(x[b, 0:NT, :]),
            "wqT": wqT, "wkT": wkT, "wvT": wvT, "woT": woT, "rtT": rtT,
            "wg": wg, "wu": wu, "wd": wd, "swg": swg, "swu": swu, "swd": swd,
            "sgT": sgT,
            "cosT": np.ascontiguousarray(cosT[:, hh * NT:(hh + 1) * NT]),
            "sinT": np.ascontiguousarray(sinT[:, hh * NT:(hh + 1) * NT]),
            "cosTp": np.ascontiguousarray(cosT[:, 0:NT]),
            "sinTp": np.ascontiguousarray(sinT[:, 0:NT]),
            "maskJI": maskJI, "id32": id32, "idb": idb, "rmat": rmat,
            "onesb": onesb,
            "pmask": np.full((128, 1), float(hh), dtype=np.float32),
        })
    return in_maps


def kernel(**inputs):
    nc = _get_nc(False)
    in_maps = make_in_maps(inputs)
    res = run_bass_kernel_spmd(nc, in_maps, core_ids=list(range(8)))
    out = np.empty((B, S, DM), dtype=np.float32)
    for core in range(8):
        b = core // 2
        hh = core % 2
        out[b, hh * NT:(hh + 1) * NT, :] = res.results[core]["out"]
    return out


# version 20
# speedup vs baseline: 1.0873x; 1.0873x over previous
import numpy as np
import ml_dtypes
import concourse.bass as bass
import concourse.mybir as mybir
from concourse.bass_utils import run_bass_kernel_spmd

F32 = mybir.dt.float32
BF16 = mybir.dt.bfloat16
AF = mybir.ActivationFunctionType
ALU = mybir.AluOpType
AX = mybir.AxisListType

B, S, DM = 4, 2048, 2048
H, KVH, HD = 16, 4, 128
NT = 1024
TCH = NT // 128
ICH = DM // 128
NE, IFF, SI = 8, 1408, 5632
IFCH = IFF // 128
NUNIT = 12
LN_EPS = 1e-5
ATT_EPS = 1e-6
ROPE_BASE = 10000.0

ENGS = ("tensor", "vector", "scalar", "gpsimd", "sync")
NSEM = {"gpsimd": 10, "sync": 6, "scalar": 4}


class Op:
    __slots__ = ("eng", "emit", "deps", "kind", "seq", "observed", "qidx", "tag", "eidx")

    def __init__(self, eng, emit, deps, kind, tag=""):
        self.eng = eng
        self.emit = emit
        self.deps = deps
        self.kind = kind
        self.seq = None
        self.observed = False
        self.qidx = None
        self.tag = tag


class Buf:
    __slots__ = ("writer", "readers", "name", "psum")

    def __init__(self, name="", psum=False):
        self.writer = None
        self.readers = []
        self.name = name
        self.psum = psum


class Prog:
    def __init__(self):
        self.ops = []
        self.dma_counts = {"gpsimd": 0, "sync": 0, "scalar": 0}

    def add(self, eng, emit, reads=(), writes=(), deps=(), kind="compute", tag=""):
        dd = list(deps)
        for b in reads:
            if b.writer is not None:
                dd.append(b.writer)
            if b.psum:
                dd.extend(r for r in b.readers if r.eng != eng)
        for b in writes:
            if b.writer is not None:
                dd.append(b.writer)
            dd.extend(b.readers)
        op = Op(eng, emit, dd, kind, tag)
        if kind == "dma":
            i = self.dma_counts[eng]
            self.dma_counts[eng] += 1
            op.qidx = (i % NSEM[eng], i // NSEM[eng])
        self.ops.append(op)
        for b in reads:
            b.readers.append(op)
        for b in writes:
            b.writer = op
            b.readers = []
        return op

    def finalize_and_emit(self, block, sems):
        for op in self.ops:
            for d in op.deps:
                if d.kind != "dma" and d.eng != op.eng:
                    d.observed = True
        cnt = {e: 0 for e in ENGS}
        for op in self.ops:
            if op.kind == "compute" and op.observed:
                cnt[op.eng] += 1
                op.seq = cnt[op.eng]
        eidx_cnt = {e: 0 for e in ENGS}
        for op in self.ops:
            op.eidx = eidx_cnt[op.eng]
            eidx_cnt[op.eng] += 1
        per_eng = {e: [] for e in ENGS}
        for op in self.ops:
            waits = {}
            if op.kind == "dma" and op.qidx[1] > 0:
                waits[("dma", op.eng, op.qidx[0])] = 16 * op.qidx[1]
            for d in op.deps:
                if d.kind == "dma":
                    key, val = ("dma", d.eng, d.qidx[0]), 16 * (d.qidx[1] + 1)
                elif d.eng != op.eng:
                    key, val = ("eng", d.eng), d.seq
                else:
                    continue
                if waits.get(key, -1) < val:
                    waits[key] = val
            per_eng[op.eng].append((op, waits))

        def emit_engine(eng_obj, eng_name):
            hw = {}
            last_drain = -1
            for op, waits in per_eng[eng_name]:
                need_drain = eng_name != "tensor" and any(
                    d.eng == eng_name and d.kind == "compute"
                    and d.eidx > last_drain for d in op.deps)
                if need_drain:
                    eng_obj.drain()
                    last_drain = op.eidx - 1
                for key in sorted(waits, key=str):
                    val = waits[key]
                    if hw.get(key, 0) >= val:
                        continue
                    hw[key] = val
                    if key[0] == "dma":
                        sem = sems["dma_" + key[1]][key[2]]
                    else:
                        sem = sems["eng_" + key[1]]
                    eng_obj.wait_ge(sem, val)
                ins = op.emit(eng_obj)
                if op.kind == "dma":
                    ins.then_inc(sems["dma_" + op.eng][op.qidx[0]], 16)
                elif op.observed:
                    ins.then_inc(sems["eng_" + op.eng], 1)

        @block.tensor
        def _(t):
            emit_engine(t, "tensor")

        @block.vector
        def _(v):
            emit_engine(v, "vector")

        @block.scalar
        def _(s):
            emit_engine(s, "scalar")

        @block.gpsimd
        def _(g):
            emit_engine(g, "gpsimd")

        @block.sync
        def _(sy):
            emit_engine(sy, "sync")


def build_program(debug=False):
    nc = bass.Bass("TRN2", target_bir_lowering=False)

    d_x = nc.dram_tensor("x_main", [NT, DM], F32, kind="ExternalInput")
    d_xp = nc.dram_tensor("x_prefix", [NT, DM], F32, kind="ExternalInput")
    d_wqT = nc.dram_tensor("wqT", [DM, H * HD], F32, kind="ExternalInput")
    d_wkT = nc.dram_tensor("wkT", [DM, KVH * HD], F32, kind="ExternalInput")
    d_wvT = nc.dram_tensor("wvT", [DM, KVH * HD], F32, kind="ExternalInput")
    d_woT = nc.dram_tensor("woT", [H * HD, DM], F32, kind="ExternalInput")
    d_rtT = nc.dram_tensor("rtT", [128, ICH * NE], F32, kind="ExternalInput")
    d_wg = nc.dram_tensor("wg", [NE, DM, IFF], F32, kind="ExternalInput")
    d_wu = nc.dram_tensor("wu", [NE, DM, IFF], F32, kind="ExternalInput")
    d_wd = nc.dram_tensor("wd", [NE, IFF, DM], F32, kind="ExternalInput")
    d_swg = nc.dram_tensor("swg", [DM, SI], F32, kind="ExternalInput")
    d_swu = nc.dram_tensor("swu", [DM, SI], F32, kind="ExternalInput")
    d_swd = nc.dram_tensor("swd", [SI, DM], F32, kind="ExternalInput")
    d_sgT = nc.dram_tensor("sgT", [128, ICH], F32, kind="ExternalInput")
    d_cos = nc.dram_tensor("cosT", [128, NT], F32, kind="ExternalInput")
    d_sin = nc.dram_tensor("sinT", [128, NT], F32, kind="ExternalInput")
    d_cosp = nc.dram_tensor("cosTp", [128, NT], F32, kind="ExternalInput")
    d_sinp = nc.dram_tensor("sinTp", [128, NT], F32, kind="ExternalInput")
    d_mask = nc.dram_tensor("maskJI", [128, 128], F32, kind="ExternalInput")
    d_id32 = nc.dram_tensor("id32", [128, 128], F32, kind="ExternalInput")
    d_idb = nc.dram_tensor("idb", [128, 128], F32, kind="ExternalInput")
    d_rmat = nc.dram_tensor("rmat", [128, 128], F32, kind="ExternalInput")
    d_ones = nc.dram_tensor("onesb", [128, 1], F32, kind="ExternalInput")
    d_pmask = nc.dram_tensor("pmask", [128, 1], F32, kind="ExternalInput")
    d_out = nc.dram_tensor("out", [NT, DM], F32, kind="ExternalOutput")

    dbg = {}
    if debug:
        dbg["xn"] = nc.dram_tensor("dbg_xn", [128, ICH * NT], BF16, kind="ExternalOutput")
        dbg["qf"] = nc.dram_tensor("dbg_qf", [128, H * NT], BF16, kind="ExternalOutput")
        dbg["kf"] = nc.dram_tensor("dbg_kf", [128, KVH * NT], BF16, kind="ExternalOutput")
        dbg["vt"] = nc.dram_tensor("dbg_vt", [128, TCH * KVH * 129], BF16, kind="ExternalOutput")
        dbg["ao"] = nc.dram_tensor("dbg_ao", [128, H * NT], BF16, kind="ExternalOutput")
        dbg["cmb"] = nc.dram_tensor("dbg_cmb", [128, TCH * NE], F32, kind="ExternalOutput")
        dbg["sig"] = nc.dram_tensor("dbg_sig", [128, TCH], F32, kind="ExternalOutput")
        dbg["oa1"] = nc.dram_tensor("dbg_oa1", [NT, DM], F32, kind="ExternalOutput")
        dbg["pkf"] = nc.dram_tensor("dbg_pkf", [128, KVH * NT], BF16, kind="ExternalOutput")
        dbg["pvt"] = nc.dram_tensor("dbg_pvt", [128, TCH * KVH * 129], BF16, kind="ExternalOutput")
        dbg["pst"] = nc.dram_tensor("dbg_pst", [128, KVH * 129], F32, kind="ExternalOutput")
        dbg["h0"] = nc.dram_tensor("dbg_h0", [128, IFCH * NT], BF16, kind="ExternalOutput")
        dbg["dsl0"] = nc.dram_tensor("dbg_dsl0", [128, IFCH * 512], BF16, kind="ExternalOutput")
        dbg["oau0"] = nc.dram_tensor("dbg_oau0", [NT, DM], F32, kind="ExternalOutput")

    OA = nc.alloc_sbuf_tensor("OA", [128, TCH * DM], F32)
    XN = nc.alloc_sbuf_tensor("XN", [128, ICH * NT], BF16)
    A1 = nc.alloc_sbuf_tensor("A1", [128, 16 * NT], BF16)
    KF = nc.alloc_sbuf_tensor("KF", [128, KVH * NT], BF16)
    VT = nc.alloc_sbuf_tensor("VT", [128, TCH * KVH * 129], BF16)
    KTOK = nc.alloc_sbuf_tensor("KTOK", [128, KVH * 128], BF16)
    STS = nc.alloc_sbuf_tensor("STS", [128, KVH * 129], BF16)
    STS32 = nc.alloc_sbuf_tensor("STS32", [128, KVH * 129], F32)
    U = nc.alloc_sbuf_tensor("U", [128, 5632], F32)
    UB = U.bitcast(BF16)

    WA = nc.alloc_sbuf_tensor("WA", [128, 12288], BF16)
    GS = nc.alloc_sbuf_tensor("GS", [128, 2 * 512], F32)
    MASK = nc.alloc_sbuf_tensor("MASK", [128, 128], F32)
    ID32 = nc.alloc_sbuf_tensor("ID32", [128, 128], F32)
    IDB = nc.alloc_sbuf_tensor("IDB", [128, 128], BF16)
    RMAT = nc.alloc_sbuf_tensor("RMAT", [128, 128], BF16)
    ONESB = nc.alloc_sbuf_tensor("ONESB", [128, 1], BF16)
    PMASK = nc.alloc_sbuf_tensor("PMASK", [128, 1], F32)
    RT32 = nc.alloc_sbuf_tensor("RT32", [128, ICH * NE], F32)
    SGT = nc.alloc_sbuf_tensor("SGT", [128, ICH], BF16)
    CMB = nc.alloc_sbuf_tensor("CMB", [128, TCH * NE], F32)
    SIG = nc.alloc_sbuf_tensor("SIG", [128, TCH], F32)
    AM = nc.alloc_sbuf_tensor("AM", [128, 2 * 128], BF16)
    STAT = nc.alloc_sbuf_tensor("STAT", [128, 16 * 8], F32)
    T2 = nc.alloc_sbuf_tensor("T2", [128, 32], F32)
    DEN = nc.alloc_sbuf_tensor("DEN", [128, 16], F32)
    LG = nc.alloc_sbuf_tensor("LG", [128, TCH * NE], F32)
    TOP = nc.alloc_sbuf_tensor("TOP", [128, 192], F32)
    TRASH = nc.alloc_sbuf_tensor("TRASH", [128, 2048], BF16)
    CSB = TRASH

    PS = [nc.alloc_psum_tensor("ps%d" % i, [128, 512], F32) for i in range(7)]
    PT = nc.alloc_psum_tensor("pt", [128, 1024], BF16)

    p = Prog()

    b_oa = [Buf("oa%d" % i) for i in range(TCH)]
    b_xn = Buf("xn")
    b_a1 = [Buf("a1_%d" % h) for h in range(16)]
    b_kf_h = [Buf("kf%d" % g) for g in range(KVH)]
    b_vt = Buf("vt")
    b_ktok = Buf("ktok")
    b_sts = Buf("sts")
    b_sts32 = [Buf("sts32_%d" % g) for g in range(KVH)]
    b_gs = [Buf("gs0"), Buf("gs1"), Buf("gs2")]
    b_ps = [Buf("ps%d" % i, psum=True) for i in range(7)]
    b_pt = Buf("pt", psum=True)
    b_const = Buf("const")
    b_rt = Buf("rt")
    b_cmb = Buf("cmb")
    b_am = [Buf("am0"), Buf("am1")]
    b_stat = [Buf("stat%d" % i) for i in range(16)]
    b_stat2 = [Buf("statq%d" % i) for i in range(16)]
    b_t2 = Buf("t2")
    b_den = [Buf("den%d" % i) for i in range(4)]
    b_denr = Buf("denr")
    b_lg = Buf("lg")
    b_csb = b_trash = Buf("trash")
    b_top = Buf("top")
    b_wa = [Buf("wa%d" % i) for i in range(24)]
    b_u = [Buf("u%d" % i) for i in range(11)]

    def wa_bufs(c0, c1):
        return b_wa[c0 // 512: (c1 + 511) // 512]

    def u_bufs(byte0, byte1):
        return b_u[byte0 // 2048: (byte1 + 2047) // 2048]

    def bcast(t, col, n):
        return bass.AP(t, col, [[t.shape[1], 128], [0, n]])

    def dma(eng, dst_ap, src_ap, reads=(), writes=(), deps=(), tag=""):
        return p.add(eng, lambda e, d=dst_ap, s=src_ap: e.dma_start(d, s),
                     reads=reads, writes=writes, deps=deps, kind="dma", tag=tag)

    dma("sync", MASK[:, :], d_mask.ap(), writes=[b_const], tag="mask")
    dma("sync", ID32[:, :], d_id32.ap(), writes=[b_const], tag="id32")
    dma("gpsimd", IDB[:, :], d_idb.ap(), writes=[b_const], tag="idb")
    dma("gpsimd", RMAT[:, :], d_rmat.ap(), writes=[b_const], tag="rmat")
    dma("gpsimd", ONESB[:, :], d_ones.ap(), writes=[b_const], tag="ones")
    dma("sync", PMASK[:, :], d_pmask.ap(), writes=[b_const], tag="pmask")
    dma("sync", RT32[:, :], d_rtT.ap(), writes=[b_rt], tag="rt")
    dma("gpsimd", SGT[:, :], d_sgT.ap(), writes=[b_rt], tag="sgt")
    for ct in range(TCH):
        dma("sync", OA[:, ct * DM:(ct + 1) * DM], d_x[ct * 128:(ct + 1) * 128, :],
            writes=[b_oa[ct]], tag="xmain%d" % ct)

    ones_ap = bass.AP(VT, 128, [[TCH * KVH * 129, 128], [129, TCH * KVH], [1, 1]])
    p.add("vector", lambda e, a=ones_ap: e.memset(a, 1.0), writes=[b_vt], tag="vtones")
    p.add("vector", lambda e: e.memset(STS32[:, :], 0.0), writes=b_sts32, tag="sts0")

    def ln_statsA(uid, idx, src_ap, src_bufs, sum_eng="vector"):
        st = idx * 8
        sb = b_stat[idx]
        sb2 = b_stat2[idx]
        if sum_eng == "vector":
            p.add("vector", lambda e, s=src_ap, st=st: e.reduce_sum(
                STAT[:, st:st + 1], s, axis=AX.X),
                reads=src_bufs, writes=[sb], tag="lnsum%s" % uid)
        else:
            p.add("scalar", lambda e, s=src_ap, st=st: e.activation(
                TRASH[:, :], s, AF.Copy, accum_out=STAT[:, st:st + 1]),
                reads=src_bufs, writes=[sb, b_trash], tag="lnsum%s" % uid)
        p.add("scalar", lambda e, s=src_ap, st=st: e.activation(
            GS.bitcast(BF16)[:, :], s, AF.Square, accum_out=STAT[:, st + 1:st + 2]),
            reads=src_bufs, writes=[sb2] + b_gs, tag="lnsq%s" % uid)

    def ln_statsB(uid, idx):
        st = idx * 8
        sb = b_stat[idx]
        sb2 = b_stat2[idx]

        def stats(e, st=st):
            e.tensor_scalar(STAT[:, st + 2:st + 3], STAT[:, st:st + 1], 1.0 / DM, None, ALU.mult)
            e.tensor_scalar(STAT[:, st + 4:st + 5], STAT[:, st + 1:st + 2], 1.0 / DM, None, ALU.mult)
            e.drain()
            e.tensor_tensor(STAT[:, st + 3:st + 4], STAT[:, st + 2:st + 3],
                            STAT[:, st + 2:st + 3], ALU.mult)
            e.drain()
            e.tensor_tensor(STAT[:, st + 4:st + 5], STAT[:, st + 4:st + 5],
                            STAT[:, st + 3:st + 4], ALU.subtract)
            e.drain()
            return e.tensor_scalar(STAT[:, st + 4:st + 5], STAT[:, st + 4:st + 5],
                                   LN_EPS, None, ALU.add)
        p.add("vector", stats, reads=[sb, sb2], writes=[sb], tag="lnstat%s" % uid)

        def rstd(e, st=st):
            e.activation(STAT[:, st + 7:st + 8], STAT[:, st + 4:st + 5], AF.Ln)
            e.drain()
            return e.activation(STAT[:, st + 6:st + 7], STAT[:, st + 7:st + 8],
                                AF.Exp, scale=-0.5)
        p.add("scalar", rstd, reads=[sb], writes=[sb], tag="lnrstd%s" % uid)

    def ln_norm(uid, idx, src_ap, src_bufs, xr_slot, dstT, dst_bufs, keep32=False,
                xs=None, xr_b=None):
        st = idx * 8
        sb = b_stat[idx]
        if xs is None:
            xs = U[:, xr_slot * 2048:(xr_slot + 1) * 2048]
            xr_b = u_bufs(xr_slot * 8192, (xr_slot + 1) * 8192)
        p.add("vector", lambda e, s=src_ap, x=xs, st=st: e.tensor_scalar(
            x, s, STAT[:, st + 2:st + 3], STAT[:, st + 6:st + 7], ALU.subtract, ALU.mult),
            reads=src_bufs + [sb], writes=xr_b, tag="lnxn%s" % uid)
        ct = int(uid.split("_")[-1])
        for q in range(4):
            bank = q % 2

            def tp(e, q=q, bank=bank, x=xs):
                ins = None
                for k in range(4):
                    ic = q * 4 + k
                    ins = e.transpose(PS[bank][:, k * 128:(k + 1) * 128],
                                      x[:, ic * 128:(ic + 1) * 128], ID32[:, :])
                return ins
            p.add("tensor", tp, reads=xr_b + [b_const], writes=[b_ps[bank]],
                  tag="lntp%s_%d" % (uid, q))
            dap = bass.AP(dstT, (q * 4) * NT + ct * 128, [[dstT.shape[1], 128], [NT, 4], [1, 128]])
            ev_eng = "vector" if q % 2 == 0 else "scalar"

            def ev(e, a=dap, bank=bank):
                if hasattr(e, "tensor_copy"):
                    return e.tensor_copy(a, PS[bank][:, 0:512])
                return e.copy(a, PS[bank][:, 0:512])
            p.add(ev_eng, ev, reads=[b_ps[bank]],
                  writes=[dst_bufs[q * 4 + k2] for k2 in range(4)],
                  tag="lnev%s_%d" % (uid, q))
            if keep32:
                oslot = 1 - xr_slot
                d32 = U[:, oslot * 2048 + q * 512: oslot * 2048 + (q + 1) * 512]
                k_eng = "scalar" if q % 2 == 0 else "vector"

                def k32(e, d=d32, bank=bank):
                    if hasattr(e, "tensor_copy"):
                        return e.tensor_copy(d, PS[bank][:, 0:512])
                    return e.copy(d, PS[bank][:, 0:512])
                p.add(k_eng, k32,
                      reads=[b_ps[bank]], writes=u_bufs(oslot * 8192 + q * 2048,
                                                        oslot * 8192 + (q + 1) * 2048),
                      tag="lnev32%s_%d" % (uid, q))

    def kq_proj(name, wT_d, dout_ch, srcT, src_bufs, dst, dst_bufs, pm, drip=None):
        pend = []

        GSB = GS.bitcast(BF16)

        def emit_rope(oc, tt, bank, rbank, dslice):
            cosap = CSB[:, tt * 512: tt * 512 + 512]
            sinap = CSB[:, NT + tt * 512: NT + tt * 512 + 512]
            p.add("tensor", lambda e, d=dslice, rbank=rbank: e.matmul(
                PS[rbank][:, :], RMAT[:, :], d, start=True, stop=True),
                reads=[dst_bufs[oc], b_const], writes=[b_ps[rbank]],
                tag="%srot%d_%d" % (name, oc, tt))
            p.add("scalar", lambda e, rbank=rbank: e.copy(GSB[:, 1024:1536], PS[rbank][:, :]),
                  reads=[b_ps[rbank]], writes=[b_gs[2]], tag="%srotev%d_%d" % (name, oc, tt))
            p.add("vector", lambda e, c2=cosap, d=dslice: e.tensor_tensor(
                GSB[:, 0:512], d, c2, ALU.mult),
                reads=[dst_bufs[oc], b_csb], writes=[b_gs[0]], tag="%st1%d_%d" % (name, oc, tt))
            p.add("vector", lambda e, s2=sinap: e.tensor_tensor(
                GSB[:, 512:1024], GSB[:, 1024:1536], s2, ALU.mult),
                reads=[b_gs[2], b_csb], writes=[b_gs[1]], tag="%st2%d_%d" % (name, oc, tt))
            p.add("vector", lambda e: e.tensor_tensor(GSB[:, 0:512], GSB[:, 0:512],
                                                      GSB[:, 512:1024], ALU.add),
                  reads=[b_gs[0], b_gs[1]], writes=[b_gs[0]], tag="%st3%d_%d" % (name, oc, tt))
            p.add("vector", lambda e: e.tensor_scalar(GSB[:, 512:1024], GSB[:, 0:512],
                                                      0.0, None, ALU.min),
                  reads=[b_gs[0]], writes=[b_gs[1]], tag="%smin%d_%d" % (name, oc, tt))
            p.add("scalar", lambda e: e.activation(GSB[:, 512:1024], GSB[:, 512:1024], AF.Exp),
                  reads=[b_gs[1]], writes=[b_gs[1]], tag="%sexp%d_%d" % (name, oc, tt))
            p.add("vector", lambda e: e.tensor_scalar(GSB[:, 0:512], GSB[:, 0:512],
                                                      0.0, None, ALU.max),
                  reads=[b_gs[0]], writes=[b_gs[0]], tag="%smax%d_%d" % (name, oc, tt))
            if pm:
                def fin(e, d=dslice):
                    e.tensor_tensor(GSB[:, 0:512], GSB[:, 0:512], GSB[:, 512:1024], ALU.add)
                    e.drain()
                    return e.tensor_scalar(d, GSB[:, 0:512], PMASK[:, 0:1], None, ALU.mult)
                p.add("vector", fin, reads=[b_gs[0], b_gs[1], b_const],
                      writes=[dst_bufs[oc], b_gs[0]], tag="%sfin%d_%d" % (name, oc, tt))
            else:
                p.add("vector", lambda e, d=dslice: e.tensor_tensor(
                    d, GSB[:, 0:512], GSB[:, 512:1024], ALU.add),
                    reads=[b_gs[0], b_gs[1]], writes=[dst_bufs[oc]],
                    tag="%sfin%d_%d" % (name, oc, tt))

        for oc in range(dout_ch):
            if drip:
                drip(oc)
            wslot = oc % 2
            wbase = wslot * 2048
            wview = WA[:, wbase:wbase + 2048].rearrange("p (c f) -> p c f", f=128)
            dma("gpsimd", wview,
                wT_d[:, oc * 128:(oc + 1) * 128].rearrange("(c p) f -> p c f", p=128),
                writes=wa_bufs(wbase, wbase + 2048), tag="%sw%d" % (name, oc))
            for tt in range(2):
                k_idx = oc * 2 + tt
                bank = k_idx % 4

                def mm(e, tt=tt, wbase=wbase, bank=bank, srcT=srcT):
                    ins = None
                    for ic in range(ICH):
                        ins = e.matmul(PS[bank][:, :],
                                       WA[:, wbase + ic * 128: wbase + (ic + 1) * 128],
                                       srcT[:, ic * NT + tt * 512: ic * NT + tt * 512 + 512],
                                       start=(ic == 0), stop=(ic == ICH - 1))
                    return ins
                p.add("tensor", mm, reads=wa_bufs(wbase, wbase + 2048) + src_bufs,
                      writes=[b_ps[bank]], tag="%smm%d_%d" % (name, oc, tt))
                dslice = dst[:, oc * NT + tt * 512: oc * NT + tt * 512 + 512]
                p.add("scalar", lambda e, d=dslice, bank=bank: e.copy(d, PS[bank][:, :]),
                      reads=[b_ps[bank]], writes=[dst_bufs[oc]], tag="%sev%d_%d" % (name, oc, tt))
                pend.append((oc, tt, bank, 4 + k_idx % 2, dslice))
                if len(pend) > 3:
                    emit_rope(*pend.pop(0))
        while pend:
            emit_rope(*pend.pop(0))

    def v_proj(name, srcT, src_bufs, load_w):
        if load_w:
            wview = WA[:, 4096:4096 + ICH * 512].rearrange("p (c f) -> p c f", f=512)
            dma("gpsimd", wview, d_wvT.ap().rearrange("(c p) f -> p c f", p=128),
                writes=wa_bufs(4096, 4096 + ICH * 512), tag="%swv" % name)
        for ct in range(TCH):
            bank = ct % 2

            def mm(e, ct=ct, bank=bank, srcT=srcT):
                ins = None
                for ic in range(ICH):
                    ins = e.matmul(PS[bank][:, :],
                                   srcT[:, ic * NT + ct * 128: ic * NT + ct * 128 + 128],
                                   WA[:, 4096 + ic * 512: 4096 + (ic + 1) * 512],
                                   start=(ic == 0), stop=(ic == ICH - 1))
                return ins
            p.add("tensor", mm, reads=wa_bufs(4096, 4096 + ICH * 512) + src_bufs,
                  writes=[b_ps[bank]], tag="%svmm%d" % (name, ct))
            dap = bass.AP(VT, ct * KVH * 129, [[VT.shape[1], 128], [129, KVH], [1, 128]])
            p.add("vector", lambda e, a=dap, bank=bank: e.tensor_copy(a, PS[bank][:, :]),
                  reads=[b_ps[bank]], writes=[b_vt], tag="%svev%d" % (name, ct))

    state_banks = [1, 4, 2, 3]

    def state_ktok(name, c):
        def tp(e, c=c):
            ins = None
            for g in range(KVH):
                ins = e.transpose(PT[:, g * 128:(g + 1) * 128],
                                  KF[:, g * NT + c * 128: g * NT + c * 128 + 128], IDB[:, :])
            return ins
        p.add("tensor", tp, reads=b_kf_h + [b_const], writes=[b_pt], tag="%sktp%d" % (name, c))
        p.add("scalar", lambda e: e.copy(KTOK[:, :], PT[:, 0:KVH * 128]),
              reads=[b_pt], writes=[b_ktok], tag="%sktev%d" % (name, c))

    def state_update(name, c):
        for g in range(KVH):
            bank = state_banks[g]
            p.add("tensor", lambda e, g=g, c=c, bank=bank: e.matmul(
                PS[bank][:, 0:129], KTOK[:, g * 128:(g + 1) * 128],
                VT[:, c * KVH * 129 + g * 129: c * KVH * 129 + (g + 1) * 129],
                start=True, stop=True),
                reads=[b_ktok, b_vt], writes=[b_ps[bank]], tag="%ssmm%d_%d" % (name, c, g))
            p.add("vector", lambda e, g=g, bank=bank: e.tensor_tensor(
                STS32[:, g * 129:(g + 1) * 129], STS32[:, g * 129:(g + 1) * 129],
                PS[bank][:, 0:129], ALU.add),
                reads=[b_ps[bank], b_sts32[g]], writes=[b_sts32[g]],
                tag="%ssacc%d_%d" % (name, c, g))

    WA32 = WA.bitcast(F32)
    main_norms = []

    def emit_main_norm(ct):
        ln_norm("m_%d" % ct, 8 + ct, OA[:, ct * DM:(ct + 1) * DM], [b_oa[ct]], ct % 2,
                XN, [b_xn] * 16, keep32=True)
        oslot = 1 - ct % 2

        def rmm32(e, oslot=oslot):
            ins = None
            for ic in range(ICH):
                ins = e.matmul(PS[4][:, 0:NE],
                               U[:, oslot * 2048 + ic * 128: oslot * 2048 + (ic + 1) * 128],
                               RT32[:, ic * NE:(ic + 1) * NE], start=(ic == 0), stop=(ic == ICH - 1))
            return ins
        p.add("tensor", rmm32, reads=u_bufs(oslot * 8192, (oslot + 1) * 8192) + [b_rt],
              writes=[b_ps[4]], tag="rmm%d" % ct)

        p.add("vector", lambda e, ct=ct: e.tensor_copy(LG[:, ct * NE:(ct + 1) * NE],
                                                       PS[4][:, 0:NE]),
              reads=[b_ps[4]], writes=[b_lg], tag="lgstash%d" % ct)

    def xpre_dma(pc):
        xsl = pc % 3
        dma("scalar", WA32[:, xsl * 2048:(xsl + 1) * 2048], d_xp[pc * 128:(pc + 1) * 128, :],
            writes=wa_bufs(xsl * 4096, (xsl + 1) * 4096), tag="xpre%d" % pc)
    xpre_dma(0)
    xpre_dma(1)
    for pc in range(TCH + 2):
        if pc >= 2:
            qc = pc - 2
            xsl = qc % 3
            xs_ap = WA32[:, xsl * 2048:(xsl + 1) * 2048]
            ln_norm("p_%d" % qc, qc, xs_ap, wa_bufs(xsl * 4096, (xsl + 1) * 4096),
                    0, A1, b_a1, xs=xs_ap, xr_b=wa_bufs(xsl * 4096, (xsl + 1) * 4096))
            emit_main_norm(qc)
        if 1 <= pc and pc + 1 < TCH:
            xpre_dma(pc + 1)
        if pc < TCH:
            xsl = pc % 3
            ln_statsA("p_%d" % pc, pc, WA32[:, xsl * 2048:(xsl + 1) * 2048],
                      wa_bufs(xsl * 4096, (xsl + 1) * 4096), sum_eng="scalar")
            ln_statsA("m_%d" % pc, 8 + pc, OA[:, pc * DM:(pc + 1) * DM], [b_oa[pc]])
        if 1 <= pc <= TCH:
            ln_statsB("p_%d" % (pc - 1), pc - 1)
            ln_statsB("m_%d" % (pc - 1), 8 + pc - 1)

    def g3(t, col):
        return bass.AP(t, col, [[t.shape[1], 128], [NE, TCH], [1, NE]])

    def gbc(t, col):
        return bass.AP(t, col, [[t.shape[1], 128], [1, TCH], [0, NE]])

    def top2all(e):
        EQ1 = TOP[:, 0:64]
        L2 = TOP[:, 64:128]
        T1 = TOP[:, 128:192]
        M1, M2, D, E, W1, W2 = 0, 8, 16, 24, 0, 8
        e.reduce_max(T2[:, 0:8], g3(LG, 0), axis=AX.X)
        e.drain()
        e.tensor_tensor(EQ1, LG[:, :], gbc(T2, 0), ALU.is_equal)
        e.drain()
        e.scalar_tensor_tensor(L2, EQ1, -1e30, LG[:, :], ALU.mult, ALU.add)
        e.drain()
        e.reduce_max(T2[:, 8:16], g3(TOP, 64), axis=AX.X)
        e.drain()
        e.tensor_tensor(L2, L2, gbc(T2, 8), ALU.is_equal)
        e.drain()
        return e.tensor_tensor(T2[:, 16:24], T2[:, 8:16], T2[:, 0:8], ALU.subtract)
    p.add("vector", top2all, reads=[b_lg], writes=[b_top, b_t2], tag="top2all_a")
    p.add("scalar", lambda e: e.activation(T2[:, 24:32], T2[:, 16:24], AF.Exp),
          reads=[b_t2], writes=[b_t2], tag="top2all_e")

    def top2allc(e):
        EQ1 = TOP[:, 0:64]
        EQ2 = TOP[:, 64:128]
        T1 = TOP[:, 128:192]
        e.tensor_scalar(T2[:, 16:24], T2[:, 24:32], 1.0, None, ALU.add)
        e.drain()
        e.reciprocal(T2[:, 16:24], T2[:, 16:24])
        e.drain()
        e.tensor_tensor(T2[:, 24:32], T2[:, 24:32], T2[:, 16:24], ALU.mult)
        e.drain()
        e.tensor_tensor(T1, EQ1, gbc(T2, 16), ALU.mult)
        e.tensor_tensor(EQ2, EQ2, gbc(T2, 24), ALU.mult)
        e.drain()
        return e.tensor_tensor(CMB[:, :], T1, EQ2, ALU.add)
    p.add("vector", top2allc, reads=[b_t2, b_top], writes=[b_cmb, b_top], tag="top2all_c")

    def drip_main_stats(oc):
        for ct in (2 * oc, 2 * oc + 1):
            if ct < TCH:
                ln_stats("m_%d" % ct, 8 + ct, OA[:, ct * DM:(ct + 1) * DM], [b_oa[ct]])
    dma("gpsimd", CSB[:, 0:NT], d_cosp.ap(), writes=[b_csb], tag="cosp")
    dma("gpsimd", CSB[:, NT:2 * NT], d_sinp.ap(), writes=[b_csb], tag="sinp")
    all_a1 = list(b_a1)
    kq_proj("pk", d_wkT.ap(), KVH, A1, all_a1, KF, b_kf_h, pm=True)
    v_proj("pv", A1, all_a1, load_w=True)
    for c in range(TCH):
        state_ktok("p", c)
        state_update("p", c)
    if debug:
        dma("sync", dbg["pkf"].ap(), KF[:, :], reads=b_kf_h, tag="dbgpkf")
        dma("sync", dbg["pvt"].ap(), VT[:, :], reads=[b_vt], tag="dbgpvt")
        dma("sync", dbg["pst"].ap(), STS32[:, :], reads=list(b_sts32), tag="dbgpst")

    if debug:
        for blk in range(ICH):
            dma("sync", dbg["xn"].ap()[:, blk * NT:(blk + 1) * NT], XN[:, blk * NT:(blk + 1) * NT],
                reads=[b_xn], tag="dbgxn%d" % blk)

    dma("gpsimd", CSB[:, 0:NT], d_cos.ap(), writes=[b_csb], tag="cosm")
    dma("gpsimd", CSB[:, NT:2 * NT], d_sin.ap(), writes=[b_csb], tag="sinm")

    kq_proj("q", d_wqT.ap(), H, XN, [b_xn], A1, b_a1, pm=False)
    kq_proj("k", d_wkT.ap(), KVH, XN, [b_xn], KF, b_kf_h, pm=False)
    v_proj("v", XN, [b_xn], load_w=False)

    if debug:
        for g in range(KVH):
            dma("sync", dbg["kf"].ap()[:, g * NT:(g + 1) * NT], KF[:, g * NT:(g + 1) * NT],
                reads=[b_kf_h[g]], tag="dbgkf%d" % g)
        for h in range(H):
            dma("sync", dbg["qf"].ap()[:, h * NT:(h + 1) * NT], A1[:, h * NT:(h + 1) * NT],
                reads=[b_a1[h]], tag="dbgqf%d" % h)
        dma("sync", dbg["vt"].ap()[:, :], VT[:, :], reads=[b_vt], tag="dbgvt")

    for c in range(TCH):
        p.add("scalar", lambda e: e.copy(STS[:, :], STS32[:, :]),
              reads=list(b_sts32), writes=[b_sts], tag="stcp%d" % c)
        numrot = [1, 4, 2, 3]

        def emit_abar(h):
            g = h // KVH
            abank = 0 if h % 2 == 0 else 6
            amap = AM[:, (h % 2) * 128:(h % 2) * 128 + 128]
            p.add("tensor", lambda e, g=g, h=h, c=c, ab=abank: e.matmul(
                PS[ab][:, 0:128], KF[:, g * NT + c * 128: g * NT + c * 128 + 128],
                A1[:, h * NT + c * 128: h * NT + c * 128 + 128], start=True, stop=True),
                reads=[b_kf_h[g], b_a1[h]], writes=[b_ps[abank]], tag="abar%d_%d" % (c, h))
            p.add("vector", lambda e, a=amap, ab=abank: e.tensor_tensor(
                a, PS[ab][:, 0:128], MASK[:, :], ALU.mult),
                reads=[b_ps[abank], b_const], writes=[b_am[h % 2]], tag="amask%d_%d" % (c, h))

        emit_abar(0)
        for hb in range(4):
            for j in range(4):
                h = hb * 4 + j
                g = h // KVH
                amb = b_am[h % 2]
                amap = AM[:, (h % 2) * 128:(h % 2) * 128 + 128]
                nbank = numrot[j]
                if h + 1 < H:
                    emit_abar(h + 1)

                def nmm(e, g=g, h=h, c=c, a=amap, nbank=nbank):
                    e.matmul(PS[nbank][:, 0:129], a,
                             VT[:, c * KVH * 129 + g * 129: c * KVH * 129 + (g + 1) * 129],
                             start=True, stop=False)
                    return e.matmul(PS[nbank][:, 0:129],
                                    A1[:, h * NT + c * 128: h * NT + c * 128 + 128],
                                    STS[:, g * 129:(g + 1) * 129], start=False, stop=True)
                p.add("tensor", nmm, reads=[amb, b_vt, b_a1[h], b_sts], writes=[b_ps[nbank]],
                      tag="nmm%d_%d" % (c, h))
                p.add("vector", lambda e, j=j, nbank=nbank: e.tensor_scalar(
                    DEN[:, j:j + 1], PS[nbank][:, 128:129], ATT_EPS, None, ALU.add),
                    reads=[b_ps[nbank]], writes=[b_den[j]], tag="dstash%d_%d" % (c, h))
            p.add("vector", lambda e: e.reciprocal(DEN[:, 4:8], DEN[:, 0:4]),
                  reads=list(b_den), writes=[b_denr], tag="drecip%d_%d" % (c, hb))
            for j in range(4):
                h = hb * 4 + j
                nbank = numrot[j]
                p.add("scalar", lambda e, h=h, c=c, j=j, nbank=nbank: e.activation(
                    A1[:, h * NT + c * 128: h * NT + c * 128 + 128], PS[nbank][:, 0:128],
                    AF.Copy, scale=DEN[:, 4 + j:5 + j]),
                    reads=[b_ps[nbank], b_denr], writes=[b_a1[h]], tag="oev%d_%d" % (c, h))
        state_ktok("m", c)
        state_update("m", c)
        for hh in range(2):
            def tp(e, c=c, hh=hh):
                ins = None
                for k in range(8):
                    h = hh * 8 + k
                    ins = e.transpose(PT[:, k * 128:(k + 1) * 128],
                                      A1[:, h * NT + c * 128: h * NT + c * 128 + 128], IDB[:, :])
                return ins
            p.add("tensor", tp, reads=[b_a1[hh * 8 + k2] for k2 in range(8)] + [b_const],
                  writes=[b_pt], tag="flip%d_%d" % (c, hh))
            dap = bass.AP(A1, (hh * 8) * NT + c * 128, [[A1.shape[1], 128], [NT, 8], [1, 128]])
            if hh == 0:
                p.add("vector", lambda e, a=dap: e.tensor_copy(a, PT[:, :]),
                      reads=[b_pt], writes=[b_a1[hh * 8 + k2] for k2 in range(8)],
                      tag="flipev%d_%d" % (c, hh))
            else:
                p.add("scalar", lambda e, a=dap: e.copy(a, PT[:, :]),
                      reads=[b_pt], writes=[b_a1[hh * 8 + k2] for k2 in range(8)],
                      tag="flipev%d_%d" % (c, hh))

    if debug:
        for h in range(H):
            dma("sync", dbg["ao"].ap()[:, h * NT:(h + 1) * NT], A1[:, h * NT:(h + 1) * NT],
                reads=[b_a1[h]], tag="dbgao%d" % h)

    for fq in range(4):
        if fq % 2 == 0:
            sl_t, sl_bufs = UB, u_bufs(0, 16384)
        else:
            sl_t, sl_bufs = WA, wa_bufs(0, 8192)
        for qq in range(4):
            dst = sl_t[:, qq * 2048:(qq + 1) * 2048].rearrange("p (c f) -> p c f", f=512)
            src = d_woT[qq * 512:(qq + 1) * 512, fq * 512:(fq + 1) * 512].rearrange(
                "(c p) f -> p c f", p=128)
            dma("gpsimd", dst, src, writes=sl_bufs, tag="wo%d_%d" % (fq, qq))
        for ctg in range(4):
            cts = [ctg * 2, ctg * 2 + 1]
            pair = [[2, 3], [5, 6]][ctg % 2]
            for cc in range(16):
                def mm(e, cc=cc, cts=cts, sl_t=sl_t, pair=pair):
                    ins = None
                    for k, ct in enumerate(cts):
                        ins = e.matmul(PS[pair[k]][:, :],
                                       A1[:, cc * NT + ct * 128: cc * NT + ct * 128 + 128],
                                       sl_t[:, cc * 512:(cc + 1) * 512],
                                       start=(cc == 0), stop=(cc == 15))
                    return ins
                p.add("tensor", mm, reads=sl_bufs + [b_a1[cc]],
                      writes=[b_ps[k] for k in pair], tag="womm%d_%d_%d" % (fq, ctg, cc))
            for k, ct in enumerate(cts):
                p.add("vector", lambda e, k=k, ct=ct, fq=fq, pair=pair: e.tensor_tensor(
                    OA[:, ct * DM + fq * 512: ct * DM + (fq + 1) * 512],
                    OA[:, ct * DM + fq * 512: ct * DM + (fq + 1) * 512],
                    PS[pair[k]][:, :], ALU.add),
                    reads=[b_ps[pair[k]], b_oa[ct]], writes=[b_oa[ct]],
                    tag="woev%d_%d_%d" % (fq, ctg, ct))

    if debug:
        for ct in range(TCH):
            dma("sync", dbg["oa1"].ap()[ct * 128:(ct + 1) * 128, :], OA[:, ct * DM:(ct + 1) * DM],
                reads=[b_oa[ct]], tag="dbgoa1_%d" % ct)

    for ct in range(TCH):
        def gmm(e, ct=ct):
            ins = None
            for ic in range(ICH):
                ins = e.matmul(PS[4][:, 0:1],
                               XN[:, ic * NT + ct * 128: ic * NT + ct * 128 + 128],
                               SGT[:, ic:ic + 1], start=(ic == 0), stop=(ic == ICH - 1))
            return ins
        p.add("tensor", gmm, reads=[b_xn, b_rt], writes=[b_ps[4]], tag="gmm%d" % ct)
        p.add("scalar", lambda e, ct=ct: e.activation(SIG[:, ct:ct + 1], PS[4][:, 0:1], AF.Sigmoid),
              reads=[b_ps[4]], writes=[b_cmb], tag="sigev%d" % ct)

    if debug:
        dma("sync", dbg["cmb"].ap(), CMB[:, :], reads=[b_cmb], tag="dbgcmb")
        dma("sync", dbg["sig"].ap(), SIG[:, :], reads=[b_cmb], tag="dbgsig")

    def unit_srcs(u):
        if u < NE:
            return d_wg.ap()[u], d_wu.ap()[u], d_wd.ap()[u]
        blk = u - NE
        return (d_swg.ap()[:, blk * IFF:(blk + 1) * IFF],
                d_swu.ap()[:, blk * IFF:(blk + 1) * IFF],
                d_swd.ap()[blk * IFF:(blk + 1) * IFF, :])

    gu_dma_ops = {}
    gu_last_mm = {}

    def emit_gu_dmas(u, lo=0, hi=IFCH):
        wg_src, wu_src, _ = unit_srcs(u)
        if u not in gu_dma_ops:
            gu_dma_ops[u] = {}
        for i in range(lo, hi):
            gbase = (i % 2) * 2048
            ubase = 4096 + (i % 2) * 2048
            wr_g = wa_bufs(gbase, gbase + 2048) if i < 2 else []
            wr_u = wa_bufs(ubase, ubase + 2048) if i < 2 else []
            dep_g = [gu_last_mm[("g", i % 2)]] if (i < 2 and ("g", i % 2) in gu_last_mm) else []
            dep_u = [gu_last_mm[("u", i % 2)]] if (i < 2 and ("u", i % 2) in gu_last_mm) else []
            og = dma("gpsimd", WA[:, gbase:gbase + 2048].rearrange("p (c f) -> p c f", f=128),
                     wg_src[:, i * 128:(i + 1) * 128].rearrange("(c p) f -> p c f", p=128),
                     writes=wr_g, deps=dep_g, tag="u%dgw%d" % (u, i))
            ou = dma("gpsimd", WA[:, ubase:ubase + 2048].rearrange("p (c f) -> p c f", f=128),
                     wu_src[:, i * 128:(i + 1) * 128].rearrange("(c p) f -> p c f", p=128),
                     writes=wr_u, deps=dep_u, tag="u%duw%d" % (u, i))
            gu_dma_ops[u][("g", i)] = og
            gu_dma_ops[u][("u", i)] = ou

    down_mm_last = {0: None, 1: None}
    dw_ops = {}
    dw_pending = {}

    def emit_down_dma(u, ft):
        _, _, wd_src = unit_srcs(u)
        dsl = ft % 2
        dview = UB[:, dsl * 5632:(dsl + 1) * 5632].rearrange("p (c f) -> p c f", f=512)
        wr = u_bufs(dsl * 11264, (dsl + 1) * 11264) if (u == 0 and ft < 2) else []
        op = dma("gpsimd", dview, wd_src[:, ft * 512:(ft + 1) * 512].rearrange(
            "(c p) f -> p c f", p=128),
            writes=wr, tag="u%ddw%d" % (u, ft))
        dw_ops[(u, ft)] = op
        dw_pending.setdefault((u, ft) if ft >= 2 else ("n", u, dsl), op)
        return op

    out_ops = []
    emit_gu_dmas(0)
    emit_down_dma(0, 0)
    emit_down_dma(0, 1)
    for u in range(NUNIT):
        pool = [0, 1, 4]
        for i in range(IFCH):
            gbase = (i % 2) * 2048
            ubase = 4096 + (i % 2) * 2048
            for tt in range(2):
                k = i * 2 + tt
                bg = pool[(2 * k) % 3]
                bu = pool[(2 * k + 1) % 3]

                def gm(e, tt=tt, gbase=gbase, bg=bg):
                    ins = None
                    for ic in range(ICH):
                        ins = e.matmul(PS[bg][:, :],
                                       WA[:, gbase + ic * 128: gbase + (ic + 1) * 128],
                                       XN[:, ic * NT + tt * 512: ic * NT + tt * 512 + 512],
                                       start=(ic == 0), stop=(ic == ICH - 1))
                    return ins
                op_gm = p.add("tensor", gm, reads=[b_xn], writes=[b_ps[bg]],
                              deps=[gu_dma_ops[u][("g", i)]], tag="u%dgm%d_%d" % (u, i, tt))
                if ("g", i + 2) in gu_dma_ops[u]:
                    gu_dma_ops[u][("g", i + 2)].deps.append(op_gm)
                gu_last_mm[("g", i % 2)] = op_gm

                def um(e, tt=tt, ubase=ubase, bu=bu):
                    ins = None
                    for ic in range(ICH):
                        ins = e.matmul(PS[bu][:, :],
                                       WA[:, ubase + ic * 128: ubase + (ic + 1) * 128],
                                       XN[:, ic * NT + tt * 512: ic * NT + tt * 512 + 512],
                                       start=(ic == 0), stop=(ic == ICH - 1))
                    return ins
                op_um = p.add("tensor", um, reads=[b_xn], writes=[b_ps[bu]],
                              deps=[gu_dma_ops[u][("u", i)]], tag="u%dum%d_%d" % (u, i, tt))
                if ("u", i + 2) in gu_dma_ops[u]:
                    gu_dma_ops[u][("u", i + 2)].deps.append(op_um)
                gu_last_mm[("u", i % 2)] = op_um
                gsl = k % 2
                p.add("scalar", lambda e, gsl=gsl, bg=bg: e.activation(
                    GS[:, gsl * 512:(gsl + 1) * 512], PS[bg][:, :], AF.Silu),
                    reads=[b_ps[bg]], writes=[b_gs[gsl]], tag="u%dsil%d_%d" % (u, i, tt))
                hslice = A1[:, i * NT + tt * 512: i * NT + tt * 512 + 512]
                p.add("vector", lambda e, h2=hslice, gsl=gsl, bu=bu: e.tensor_tensor(
                    h2, GS[:, gsl * 512:(gsl + 1) * 512], PS[bu][:, :], ALU.mult),
                    reads=[b_gs[gsl], b_ps[bu]], writes=[b_a1[i]], tag="u%dh%d_%d" % (u, i, tt))
        if u + 1 < NUNIT:
            emit_gu_dmas(u + 1, 0, 2)
        dw23 = [emit_down_dma(u, 2), emit_down_dma(u, 3)]
        if u + 1 < NUNIT:
            dw01n = [emit_down_dma(u + 1, 0), emit_down_dma(u + 1, 1)]
        else:
            dw01n = []
        if u + 1 < NUNIT:
            emit_gu_dmas(u + 1, 2, IFCH)
        for ft in range(4):
            dsl = ft % 2
            dbase = dsl * 5632
            for ctg in range(4):
                cts = [ctg * 2, ctg * 2 + 1]
                pair = [[2, 3], [5, 6]][ctg % 2]

                def dmm(e, cts=cts, dbase=dbase, pair=pair):
                    ins = None
                    for i in range(IFCH):
                        for kk, ct in enumerate(cts):
                            ins = e.matmul(PS[pair[kk]][:, :],
                                           A1[:, i * NT + ct * 128: i * NT + ct * 128 + 128],
                                           UB[:, dbase + i * 512: dbase + (i + 1) * 512],
                                           start=(i == 0), stop=(i == IFCH - 1))
                    return ins
                op_dmm = p.add("tensor", dmm, reads=[b_a1[i2] for i2 in range(IFCH)],
                               writes=[b_ps[k] for k in pair],
                               deps=[dw_ops[(u, ft)]],
                               tag="u%ddm%d_%d" % (u, ft, ctg))
                if ft < 2:
                    dw23[ft].deps.append(op_dmm)
                elif dw01n:
                    dw01n[ft - 2].deps.append(op_dmm)
                down_mm_last[ft % 2] = op_dmm
                for kk, ct in enumerate(cts):
                    oaslice = OA[:, ct * DM + ft * 512: ct * DM + (ft + 1) * 512]
                    sap = CMB[:, ct * NE + u: ct * NE + u + 1] if u < NE else SIG[:, ct:ct + 1]
                    p.add("vector", lambda e, o=oaslice, kk=kk, sap=sap, pair=pair:
                          e.scalar_tensor_tensor(o, PS[pair[kk]][:, :], sap, o,
                                                 ALU.mult, ALU.add),
                          reads=[b_ps[pair[kk]], b_cmb, b_oa[ct]], writes=[b_oa[ct]],
                          tag="u%ddev%d_%d" % (u, ft, ct))
        if debug and u == 0:
            for i2 in range(IFCH):
                dma("sync", dbg["h0"].ap()[:, i2 * NT:(i2 + 1) * NT], A1[:, i2 * NT:(i2 + 1) * NT],
                    reads=[b_a1[i2]], tag="dbgh0_%d" % i2)
            _dmp = dma("sync", dbg["dsl0"].ap(), UB[:, 0:IFCH * 512],
                       deps=[dw_ops[(0, 2)], down_mm_last[0]], tag="dbgdsl0")
            if (1, 0) in dw_ops:
                dw_ops[(1, 0)].deps.append(_dmp)
            for ct in range(TCH):
                dma("sync", dbg["oau0"].ap()[ct * 128:(ct + 1) * 128, :],
                    OA[:, ct * DM:(ct + 1) * DM], reads=[b_oa[ct]], tag="dbgoau0_%d" % ct)
        if u == NUNIT - 1:
            for ct in range(TCH):
                out_ops.append(dma("sync", d_out[ct * 128:(ct + 1) * 128, :],
                                   OA[:, ct * DM:(ct + 1) * DM],
                                   reads=[b_oa[ct]], tag="outst%d" % ct))

    p.add("scalar", lambda e: e.memzero(STAT[:, 0:1]), deps=out_ops, tag="finwait")

    import contextlib
    with contextlib.ExitStack() as st:
        sems = {}
        for e in ENGS:
            sems["eng_" + e] = st.enter_context(nc.semaphore("s_" + e))
        for q in ("gpsimd", "sync", "scalar"):
            sems["dma_" + q] = [st.enter_context(nc.semaphore("sd_%s_%d" % (q, i)))
                                for i in range(NSEM[q])]
        block = st.enter_context(nc.Block())
        p.finalize_and_emit(block, sems)
    return nc


_cached = {}


def _get_nc(debug=False):
    if debug not in _cached:
        _cached[debug] = build_program(debug)
    return _cached[debug]


def _rope_tables():
    inv_freq = 1.0 / ROPE_BASE ** (np.arange(0, HD, 2, dtype=np.float32) / HD)
    t = np.arange(S, dtype=np.float32)
    freqs = np.outer(t, inv_freq).astype(np.float32)
    emb = np.concatenate([freqs, freqs], -1)
    return np.cos(emb).astype(np.float32), np.sin(emb).astype(np.float32)


def make_in_maps(inputs):
    x = np.asarray(inputs["L0_attn_x"], dtype=np.float32)
    wq = np.asarray(inputs["L0_attn_Wq"], dtype=np.float32)
    wk = np.asarray(inputs["L0_attn_Wk"], dtype=np.float32)
    wv = np.asarray(inputs["L0_attn_Wv"], dtype=np.float32)
    wo = np.asarray(inputs["L0_attn_Wo"], dtype=np.float32)
    rw = np.asarray(inputs["L0_ffn_router_w"], dtype=np.float32)
    wg = np.ascontiguousarray(np.asarray(inputs["L0_ffn_w_gate"], dtype=np.float32))
    wu = np.ascontiguousarray(np.asarray(inputs["L0_ffn_w_up"], dtype=np.float32))
    wd = np.ascontiguousarray(np.asarray(inputs["L0_ffn_w_down"], dtype=np.float32))
    swg = np.ascontiguousarray(np.asarray(inputs["L0_ffn_sw_gate"], dtype=np.float32))
    swu = np.ascontiguousarray(np.asarray(inputs["L0_ffn_sw_up"], dtype=np.float32))
    swd = np.ascontiguousarray(np.asarray(inputs["L0_ffn_sw_down"], dtype=np.float32))
    sg = np.asarray(inputs["L0_ffn_shared_gate_w"], dtype=np.float32)

    wqT = np.ascontiguousarray(wq.T)
    wkT = np.ascontiguousarray(wk.T)
    wvT = np.ascontiguousarray(wv.T)
    woT = np.ascontiguousarray(wo.T)
    rtT = np.ascontiguousarray(rw.reshape(NE, ICH, 128).transpose(2, 1, 0).reshape(128, ICH * NE))
    sgT = np.ascontiguousarray(sg.reshape(ICH, 128).T)

    cos, sin = _rope_tables()
    cosT = np.ascontiguousarray(cos.T)
    sinT = np.ascontiguousarray(sin.T)

    j = np.arange(128)
    maskJI = (j[:, None] <= j[None, :]).astype(np.float32)
    id32 = np.eye(128, dtype=np.float32)
    idb = np.eye(128, dtype=np.float32)
    R = np.zeros((128, 128), dtype=np.float32)
    for dp in range(64):
        R[dp + 64, dp] = -1.0
    for dp in range(64, 128):
        R[dp - 64, dp] = 1.0
    rmat = R
    onesb = np.ones((128, 1), dtype=np.float32)

    in_maps = []
    for core in range(8):
        b = core // 2
        hh = core % 2
        in_maps.append({
            "x_main": np.ascontiguousarray(x[b, hh * NT:(hh + 1) * NT, :]),
            "x_prefix": np.ascontiguousarray(x[b, 0:NT, :]),
            "wqT": wqT, "wkT": wkT, "wvT": wvT, "woT": woT, "rtT": rtT,
            "wg": wg, "wu": wu, "wd": wd, "swg": swg, "swu": swu, "swd": swd,
            "sgT": sgT,
            "cosT": np.ascontiguousarray(cosT[:, hh * NT:(hh + 1) * NT]),
            "sinT": np.ascontiguousarray(sinT[:, hh * NT:(hh + 1) * NT]),
            "cosTp": np.ascontiguousarray(cosT[:, 0:NT]),
            "sinTp": np.ascontiguousarray(sinT[:, 0:NT]),
            "maskJI": maskJI, "id32": id32, "idb": idb, "rmat": rmat,
            "onesb": onesb,
            "pmask": np.full((128, 1), float(hh), dtype=np.float32),
        })
    return in_maps


def kernel(**inputs):
    nc = _get_nc(False)
    in_maps = make_in_maps(inputs)
    res = run_bass_kernel_spmd(nc, in_maps, core_ids=list(range(8)))
    out = np.empty((B, S, DM), dtype=np.float32)
    for core in range(8):
        b = core // 2
        hh = core % 2
        out[b, hh * NT:(hh + 1) * NT, :] = res.results[core]["out"]
    return out


# version 21
# speedup vs baseline: 1.0914x; 1.0037x over previous
import numpy as np
import ml_dtypes
import concourse.bass as bass
import concourse.mybir as mybir
from concourse.bass_utils import run_bass_kernel_spmd

F32 = mybir.dt.float32
BF16 = mybir.dt.bfloat16
AF = mybir.ActivationFunctionType
ALU = mybir.AluOpType
AX = mybir.AxisListType

B, S, DM = 4, 2048, 2048
H, KVH, HD = 16, 4, 128
NT = 1024
TCH = NT // 128
ICH = DM // 128
NE, IFF, SI = 8, 1408, 5632
IFCH = IFF // 128
NUNIT = 12
LN_EPS = 1e-5
ATT_EPS = 1e-6
ROPE_BASE = 10000.0

ENGS = ("tensor", "vector", "scalar", "gpsimd", "sync")
NSEM = {"gpsimd": 10, "sync": 6, "scalar": 4}


class Op:
    __slots__ = ("eng", "emit", "deps", "kind", "seq", "observed", "qidx", "tag", "eidx")

    def __init__(self, eng, emit, deps, kind, tag=""):
        self.eng = eng
        self.emit = emit
        self.deps = deps
        self.kind = kind
        self.seq = None
        self.observed = False
        self.qidx = None
        self.tag = tag


class Buf:
    __slots__ = ("writer", "readers", "name", "psum")

    def __init__(self, name="", psum=False):
        self.writer = None
        self.readers = []
        self.name = name
        self.psum = psum


class Prog:
    def __init__(self):
        self.ops = []
        self.dma_counts = {"gpsimd": 0, "sync": 0, "scalar": 0}

    def add(self, eng, emit, reads=(), writes=(), deps=(), kind="compute", tag=""):
        dd = list(deps)
        for b in reads:
            if b.writer is not None:
                dd.append(b.writer)
            if b.psum:
                dd.extend(r for r in b.readers if r.eng != eng)
        for b in writes:
            if b.writer is not None:
                dd.append(b.writer)
            dd.extend(b.readers)
        op = Op(eng, emit, dd, kind, tag)
        if kind == "dma":
            i = self.dma_counts[eng]
            self.dma_counts[eng] += 1
            op.qidx = (i % NSEM[eng], i // NSEM[eng])
        self.ops.append(op)
        for b in reads:
            b.readers.append(op)
        for b in writes:
            b.writer = op
            b.readers = []
        return op

    def finalize_and_emit(self, block, sems):
        for op in self.ops:
            for d in op.deps:
                if d.kind != "dma" and d.eng != op.eng:
                    d.observed = True
        cnt = {e: 0 for e in ENGS}
        for op in self.ops:
            if op.kind == "compute" and op.observed:
                cnt[op.eng] += 1
                op.seq = cnt[op.eng]
        eidx_cnt = {e: 0 for e in ENGS}
        for op in self.ops:
            op.eidx = eidx_cnt[op.eng]
            eidx_cnt[op.eng] += 1
        per_eng = {e: [] for e in ENGS}
        for op in self.ops:
            waits = {}
            if op.kind == "dma" and op.qidx[1] > 0:
                waits[("dma", op.eng, op.qidx[0])] = 16 * op.qidx[1]
            for d in op.deps:
                if d.kind == "dma":
                    key, val = ("dma", d.eng, d.qidx[0]), 16 * (d.qidx[1] + 1)
                elif d.eng != op.eng:
                    key, val = ("eng", d.eng), d.seq
                else:
                    continue
                if waits.get(key, -1) < val:
                    waits[key] = val
            per_eng[op.eng].append((op, waits))

        def emit_engine(eng_obj, eng_name):
            hw = {}
            last_drain = -1
            for op, waits in per_eng[eng_name]:
                need_drain = eng_name != "tensor" and any(
                    d.eng == eng_name and d.kind == "compute"
                    and d.eidx > last_drain for d in op.deps)
                if need_drain:
                    eng_obj.drain()
                    last_drain = op.eidx - 1
                for key in sorted(waits, key=str):
                    val = waits[key]
                    if hw.get(key, 0) >= val:
                        continue
                    hw[key] = val
                    if key[0] == "dma":
                        sem = sems["dma_" + key[1]][key[2]]
                    else:
                        sem = sems["eng_" + key[1]]
                    eng_obj.wait_ge(sem, val)
                ins = op.emit(eng_obj)
                if op.kind == "dma":
                    ins.then_inc(sems["dma_" + op.eng][op.qidx[0]], 16)
                elif op.observed:
                    ins.then_inc(sems["eng_" + op.eng], 1)

        @block.tensor
        def _(t):
            emit_engine(t, "tensor")

        @block.vector
        def _(v):
            emit_engine(v, "vector")

        @block.scalar
        def _(s):
            emit_engine(s, "scalar")

        @block.gpsimd
        def _(g):
            emit_engine(g, "gpsimd")

        @block.sync
        def _(sy):
            emit_engine(sy, "sync")


def build_program(debug=False):
    nc = bass.Bass("TRN2", target_bir_lowering=False)

    d_x = nc.dram_tensor("x_main", [NT, DM], F32, kind="ExternalInput")
    d_xp = nc.dram_tensor("x_prefix", [NT, DM], F32, kind="ExternalInput")
    d_wqT = nc.dram_tensor("wqT", [DM, H * HD], F32, kind="ExternalInput")
    d_wkT = nc.dram_tensor("wkT", [DM, KVH * HD], F32, kind="ExternalInput")
    d_wvT = nc.dram_tensor("wvT", [DM, KVH * HD], F32, kind="ExternalInput")
    d_woT = nc.dram_tensor("woT", [H * HD, DM], F32, kind="ExternalInput")
    d_rtT = nc.dram_tensor("rtT", [128, ICH * NE], F32, kind="ExternalInput")
    d_wg = nc.dram_tensor("wg", [NE, DM, IFF], F32, kind="ExternalInput")
    d_wu = nc.dram_tensor("wu", [NE, DM, IFF], F32, kind="ExternalInput")
    d_wd = nc.dram_tensor("wd", [NE, IFF, DM], F32, kind="ExternalInput")
    d_swg = nc.dram_tensor("swg", [DM, SI], F32, kind="ExternalInput")
    d_swu = nc.dram_tensor("swu", [DM, SI], F32, kind="ExternalInput")
    d_swd = nc.dram_tensor("swd", [SI, DM], F32, kind="ExternalInput")
    d_sgT = nc.dram_tensor("sgT", [128, ICH], F32, kind="ExternalInput")
    d_cos = nc.dram_tensor("cosT", [128, NT], F32, kind="ExternalInput")
    d_sin = nc.dram_tensor("sinT", [128, NT], F32, kind="ExternalInput")
    d_cosp = nc.dram_tensor("cosTp", [128, NT], F32, kind="ExternalInput")
    d_sinp = nc.dram_tensor("sinTp", [128, NT], F32, kind="ExternalInput")
    d_mask = nc.dram_tensor("maskJI", [128, 128], F32, kind="ExternalInput")
    d_id32 = nc.dram_tensor("id32", [128, 128], F32, kind="ExternalInput")
    d_idb = nc.dram_tensor("idb", [128, 128], F32, kind="ExternalInput")
    d_rmat = nc.dram_tensor("rmat", [128, 128], F32, kind="ExternalInput")
    d_ones = nc.dram_tensor("onesb", [128, 1], F32, kind="ExternalInput")
    d_pmask = nc.dram_tensor("pmask", [128, 1], F32, kind="ExternalInput")
    d_out = nc.dram_tensor("out", [NT, DM], F32, kind="ExternalOutput")

    dbg = {}
    if debug:
        dbg["xn"] = nc.dram_tensor("dbg_xn", [128, ICH * NT], BF16, kind="ExternalOutput")
        dbg["qf"] = nc.dram_tensor("dbg_qf", [128, H * NT], BF16, kind="ExternalOutput")
        dbg["kf"] = nc.dram_tensor("dbg_kf", [128, KVH * NT], BF16, kind="ExternalOutput")
        dbg["vt"] = nc.dram_tensor("dbg_vt", [128, TCH * KVH * 129], BF16, kind="ExternalOutput")
        dbg["ao"] = nc.dram_tensor("dbg_ao", [128, H * NT], BF16, kind="ExternalOutput")
        dbg["cmb"] = nc.dram_tensor("dbg_cmb", [128, TCH * NE], F32, kind="ExternalOutput")
        dbg["sig"] = nc.dram_tensor("dbg_sig", [128, TCH], F32, kind="ExternalOutput")
        dbg["oa1"] = nc.dram_tensor("dbg_oa1", [NT, DM], F32, kind="ExternalOutput")
        dbg["pkf"] = nc.dram_tensor("dbg_pkf", [128, KVH * NT], BF16, kind="ExternalOutput")
        dbg["pvt"] = nc.dram_tensor("dbg_pvt", [128, TCH * KVH * 129], BF16, kind="ExternalOutput")
        dbg["pst"] = nc.dram_tensor("dbg_pst", [128, KVH * 129], F32, kind="ExternalOutput")
        dbg["h0"] = nc.dram_tensor("dbg_h0", [128, IFCH * NT], BF16, kind="ExternalOutput")
        dbg["dsl0"] = nc.dram_tensor("dbg_dsl0", [128, IFCH * 512], BF16, kind="ExternalOutput")
        dbg["oau0"] = nc.dram_tensor("dbg_oau0", [NT, DM], F32, kind="ExternalOutput")

    OA = nc.alloc_sbuf_tensor("OA", [128, TCH * DM], F32)
    XN = nc.alloc_sbuf_tensor("XN", [128, ICH * NT], BF16)
    A1 = nc.alloc_sbuf_tensor("A1", [128, 16 * NT], BF16)
    KF = nc.alloc_sbuf_tensor("KF", [128, KVH * NT], BF16)
    VT = nc.alloc_sbuf_tensor("VT", [128, TCH * KVH * 129], BF16)
    KTOK = nc.alloc_sbuf_tensor("KTOK", [128, KVH * 128], BF16)
    STS = nc.alloc_sbuf_tensor("STS", [128, KVH * 129], BF16)
    STS32 = nc.alloc_sbuf_tensor("STS32", [128, KVH * 129], F32)
    U = nc.alloc_sbuf_tensor("U", [128, 5632], F32)
    UB = U.bitcast(BF16)

    WA = nc.alloc_sbuf_tensor("WA", [128, 12288], BF16)
    GS = nc.alloc_sbuf_tensor("GS", [128, 2 * 512], F32)
    MASK = nc.alloc_sbuf_tensor("MASK", [128, 128], F32)
    ID32 = nc.alloc_sbuf_tensor("ID32", [128, 128], F32)
    IDB = nc.alloc_sbuf_tensor("IDB", [128, 128], BF16)
    RMAT = nc.alloc_sbuf_tensor("RMAT", [128, 128], BF16)
    ONESB = nc.alloc_sbuf_tensor("ONESB", [128, 1], BF16)
    PMASK = nc.alloc_sbuf_tensor("PMASK", [128, 1], F32)
    RT32 = nc.alloc_sbuf_tensor("RT32", [128, ICH * NE], F32)
    SGT = nc.alloc_sbuf_tensor("SGT", [128, ICH], BF16)
    CMB = nc.alloc_sbuf_tensor("CMB", [128, TCH * NE], F32)
    SIG = nc.alloc_sbuf_tensor("SIG", [128, TCH], F32)
    AM = nc.alloc_sbuf_tensor("AM", [128, 2 * 128], BF16)
    STAT = nc.alloc_sbuf_tensor("STAT", [128, 16 * 8], F32)
    T2 = nc.alloc_sbuf_tensor("T2", [128, 32], F32)
    DEN = nc.alloc_sbuf_tensor("DEN", [128, 16], F32)
    LG = nc.alloc_sbuf_tensor("LG", [128, TCH * NE], F32)
    TOP = nc.alloc_sbuf_tensor("TOP", [128, 192], F32)
    TRASH = nc.alloc_sbuf_tensor("TRASH", [128, 2048], BF16)
    CSB = TRASH

    PS = [nc.alloc_psum_tensor("ps%d" % i, [128, 512], F32) for i in range(7)]
    PT = nc.alloc_psum_tensor("pt", [128, 1024], BF16)

    p = Prog()

    b_oa = [Buf("oa%d" % i) for i in range(TCH)]
    b_xn = Buf("xn")
    b_a1 = [Buf("a1_%d" % h) for h in range(16)]
    b_kf_h = [Buf("kf%d" % g) for g in range(KVH)]
    b_vt = Buf("vt")
    b_ktok = Buf("ktok")
    b_sts = Buf("sts")
    b_sts32 = [Buf("sts32_%d" % g) for g in range(KVH)]
    b_gs = [Buf("gs0"), Buf("gs1"), Buf("gs2")]
    b_ps = [Buf("ps%d" % i, psum=True) for i in range(7)]
    b_pt = Buf("pt", psum=True)
    b_const = Buf("const")
    b_rt = Buf("rt")
    b_cmb = Buf("cmb")
    b_am = [Buf("am0"), Buf("am1")]
    b_stat = [Buf("stat%d" % i) for i in range(16)]
    b_stat2 = [Buf("statq%d" % i) for i in range(16)]
    b_t2 = Buf("t2")
    b_den = [Buf("den%d" % i) for i in range(4)]
    b_denr = Buf("denr")
    b_lg = Buf("lg")
    b_csb = b_trash = Buf("trash")
    b_top = Buf("top")
    b_wa = [Buf("wa%d" % i) for i in range(24)]
    b_u = [Buf("u%d" % i) for i in range(11)]

    def wa_bufs(c0, c1):
        return b_wa[c0 // 512: (c1 + 511) // 512]

    def u_bufs(byte0, byte1):
        return b_u[byte0 // 2048: (byte1 + 2047) // 2048]

    def bcast(t, col, n):
        return bass.AP(t, col, [[t.shape[1], 128], [0, n]])

    def dma(eng, dst_ap, src_ap, reads=(), writes=(), deps=(), tag=""):
        return p.add(eng, lambda e, d=dst_ap, s=src_ap: e.dma_start(d, s),
                     reads=reads, writes=writes, deps=deps, kind="dma", tag=tag)

    dma("sync", MASK[:, :], d_mask.ap(), writes=[b_const], tag="mask")
    dma("sync", ID32[:, :], d_id32.ap(), writes=[b_const], tag="id32")
    dma("gpsimd", IDB[:, :], d_idb.ap(), writes=[b_const], tag="idb")
    dma("gpsimd", RMAT[:, :], d_rmat.ap(), writes=[b_const], tag="rmat")
    dma("gpsimd", ONESB[:, :], d_ones.ap(), writes=[b_const], tag="ones")
    dma("sync", PMASK[:, :], d_pmask.ap(), writes=[b_const], tag="pmask")
    dma("sync", RT32[:, :], d_rtT.ap(), writes=[b_rt], tag="rt")
    dma("gpsimd", SGT[:, :], d_sgT.ap(), writes=[b_rt], tag="sgt")
    for ct in range(TCH):
        dma("sync", OA[:, ct * DM:(ct + 1) * DM], d_x[ct * 128:(ct + 1) * 128, :],
            writes=[b_oa[ct]], tag="xmain%d" % ct)

    ones_ap = bass.AP(VT, 128, [[TCH * KVH * 129, 128], [129, TCH * KVH], [1, 1]])
    p.add("vector", lambda e, a=ones_ap: e.memset(a, 1.0), writes=[b_vt], tag="vtones")
    p.add("vector", lambda e: e.memset(STS32[:, :], 0.0), writes=b_sts32, tag="sts0")

    def ln_statsA(uid, idx, src_ap, src_bufs, sum_eng="vector"):
        st = idx * 8
        sb = b_stat[idx]
        sb2 = b_stat2[idx]
        if sum_eng == "vector":
            p.add("vector", lambda e, s=src_ap, st=st: e.reduce_sum(
                STAT[:, st:st + 1], s, axis=AX.X),
                reads=src_bufs, writes=[sb], tag="lnsum%s" % uid)
        else:
            p.add("scalar", lambda e, s=src_ap, st=st: e.activation(
                TRASH[:, :], s, AF.Copy, accum_out=STAT[:, st:st + 1]),
                reads=src_bufs, writes=[sb, b_trash], tag="lnsum%s" % uid)
        p.add("scalar", lambda e, s=src_ap, st=st: e.activation(
            GS.bitcast(BF16)[:, :], s, AF.Square, accum_out=STAT[:, st + 1:st + 2]),
            reads=src_bufs, writes=[sb2] + b_gs, tag="lnsq%s" % uid)

    def ln_statsB(uid, idx):
        st = idx * 8
        sb = b_stat[idx]
        sb2 = b_stat2[idx]

        def stats(e, st=st):
            e.tensor_scalar(STAT[:, st + 2:st + 3], STAT[:, st:st + 1], 1.0 / DM, None, ALU.mult)
            e.tensor_scalar(STAT[:, st + 4:st + 5], STAT[:, st + 1:st + 2], 1.0 / DM, None, ALU.mult)
            e.drain()
            e.tensor_tensor(STAT[:, st + 3:st + 4], STAT[:, st + 2:st + 3],
                            STAT[:, st + 2:st + 3], ALU.mult)
            e.drain()
            e.tensor_tensor(STAT[:, st + 4:st + 5], STAT[:, st + 4:st + 5],
                            STAT[:, st + 3:st + 4], ALU.subtract)
            e.drain()
            return e.tensor_scalar(STAT[:, st + 4:st + 5], STAT[:, st + 4:st + 5],
                                   LN_EPS, None, ALU.add)
        p.add("vector", stats, reads=[sb, sb2], writes=[sb], tag="lnstat%s" % uid)

        def rstd(e, st=st):
            e.activation(STAT[:, st + 7:st + 8], STAT[:, st + 4:st + 5], AF.Ln)
            e.drain()
            return e.activation(STAT[:, st + 6:st + 7], STAT[:, st + 7:st + 8],
                                AF.Exp, scale=-0.5)
        p.add("scalar", rstd, reads=[sb], writes=[sb], tag="lnrstd%s" % uid)

    def ln_norm(uid, idx, src_ap, src_bufs, xr_slot, dstT, dst_bufs, keep32=False,
                xs=None, xr_b=None):
        st = idx * 8
        sb = b_stat[idx]
        if xs is None:
            xs = U[:, xr_slot * 2048:(xr_slot + 1) * 2048]
            xr_b = u_bufs(xr_slot * 8192, (xr_slot + 1) * 8192)
        p.add("vector", lambda e, s=src_ap, x=xs, st=st: e.tensor_scalar(
            x, s, STAT[:, st + 2:st + 3], STAT[:, st + 6:st + 7], ALU.subtract, ALU.mult),
            reads=src_bufs + [sb], writes=xr_b, tag="lnxn%s" % uid)
        ct = int(uid.split("_")[-1])
        for q in range(4):
            bank = q % 2

            def tp(e, q=q, bank=bank, x=xs):
                ins = None
                for k in range(4):
                    ic = q * 4 + k
                    ins = e.transpose(PS[bank][:, k * 128:(k + 1) * 128],
                                      x[:, ic * 128:(ic + 1) * 128], ID32[:, :])
                return ins
            p.add("tensor", tp, reads=xr_b + [b_const], writes=[b_ps[bank]],
                  tag="lntp%s_%d" % (uid, q))
            dap = bass.AP(dstT, (q * 4) * NT + ct * 128, [[dstT.shape[1], 128], [NT, 4], [1, 128]])
            ev_eng = "vector" if q % 2 == 0 else "scalar"

            def ev(e, a=dap, bank=bank):
                if hasattr(e, "tensor_copy"):
                    return e.tensor_copy(a, PS[bank][:, 0:512])
                return e.copy(a, PS[bank][:, 0:512])
            p.add(ev_eng, ev, reads=[b_ps[bank]],
                  writes=[dst_bufs[q * 4 + k2] for k2 in range(4)],
                  tag="lnev%s_%d" % (uid, q))
            if keep32:
                oslot = 1 - xr_slot
                d32 = U[:, oslot * 2048 + q * 512: oslot * 2048 + (q + 1) * 512]
                k_eng = "scalar" if q % 2 == 0 else "vector"

                def k32(e, d=d32, bank=bank):
                    if hasattr(e, "tensor_copy"):
                        return e.tensor_copy(d, PS[bank][:, 0:512])
                    return e.copy(d, PS[bank][:, 0:512])
                p.add(k_eng, k32,
                      reads=[b_ps[bank]], writes=u_bufs(oslot * 8192 + q * 2048,
                                                        oslot * 8192 + (q + 1) * 2048),
                      tag="lnev32%s_%d" % (uid, q))

    def kq_proj(name, wT_d, dout_ch, srcT, src_bufs, dst, dst_bufs, pm, drip=None):
        pend = []

        GSB = GS.bitcast(BF16)

        def emit_rope(oc, tt, bank, rbank, dslice):
            cosap = CSB[:, tt * 512: tt * 512 + 512]
            sinap = CSB[:, NT + tt * 512: NT + tt * 512 + 512]
            p.add("tensor", lambda e, d=dslice, rbank=rbank: e.matmul(
                PS[rbank][:, :], RMAT[:, :], d, start=True, stop=True),
                reads=[dst_bufs[oc], b_const], writes=[b_ps[rbank]],
                tag="%srot%d_%d" % (name, oc, tt))
            p.add("scalar", lambda e, rbank=rbank: e.copy(GSB[:, 1024:1536], PS[rbank][:, :]),
                  reads=[b_ps[rbank]], writes=[b_gs[2]], tag="%srotev%d_%d" % (name, oc, tt))
            p.add("vector", lambda e, c2=cosap, d=dslice: e.tensor_tensor(
                GSB[:, 0:512], d, c2, ALU.mult),
                reads=[dst_bufs[oc], b_csb], writes=[b_gs[0]], tag="%st1%d_%d" % (name, oc, tt))
            p.add("vector", lambda e, s2=sinap: e.tensor_tensor(
                GSB[:, 512:1024], GSB[:, 1024:1536], s2, ALU.mult),
                reads=[b_gs[2], b_csb], writes=[b_gs[1]], tag="%st2%d_%d" % (name, oc, tt))
            p.add("vector", lambda e: e.tensor_tensor(GSB[:, 0:512], GSB[:, 0:512],
                                                      GSB[:, 512:1024], ALU.add),
                  reads=[b_gs[0], b_gs[1]], writes=[b_gs[0]], tag="%st3%d_%d" % (name, oc, tt))
            p.add("vector", lambda e: e.tensor_scalar(GSB[:, 512:1024], GSB[:, 0:512],
                                                      0.0, None, ALU.min),
                  reads=[b_gs[0]], writes=[b_gs[1]], tag="%smin%d_%d" % (name, oc, tt))
            p.add("scalar", lambda e: e.activation(GSB[:, 512:1024], GSB[:, 512:1024], AF.Exp),
                  reads=[b_gs[1]], writes=[b_gs[1]], tag="%sexp%d_%d" % (name, oc, tt))
            p.add("vector", lambda e: e.tensor_scalar(GSB[:, 0:512], GSB[:, 0:512],
                                                      0.0, None, ALU.max),
                  reads=[b_gs[0]], writes=[b_gs[0]], tag="%smax%d_%d" % (name, oc, tt))
            if pm:
                def fin(e, d=dslice):
                    e.tensor_tensor(GSB[:, 0:512], GSB[:, 0:512], GSB[:, 512:1024], ALU.add)
                    e.drain()
                    return e.tensor_scalar(d, GSB[:, 0:512], PMASK[:, 0:1], None, ALU.mult)
                p.add("vector", fin, reads=[b_gs[0], b_gs[1], b_const],
                      writes=[dst_bufs[oc], b_gs[0]], tag="%sfin%d_%d" % (name, oc, tt))
            else:
                p.add("vector", lambda e, d=dslice: e.tensor_tensor(
                    d, GSB[:, 0:512], GSB[:, 512:1024], ALU.add),
                    reads=[b_gs[0], b_gs[1]], writes=[dst_bufs[oc]],
                    tag="%sfin%d_%d" % (name, oc, tt))

        for oc in range(dout_ch):
            if drip:
                drip(oc)
            wslot = oc % 2
            wbase = wslot * 2048
            wview = WA[:, wbase:wbase + 2048].rearrange("p (c f) -> p c f", f=128)
            dma("gpsimd", wview,
                wT_d[:, oc * 128:(oc + 1) * 128].rearrange("(c p) f -> p c f", p=128),
                writes=wa_bufs(wbase, wbase + 2048), tag="%sw%d" % (name, oc))
            for tt in range(2):
                k_idx = oc * 2 + tt
                bank = k_idx % 4

                def mm(e, tt=tt, wbase=wbase, bank=bank, srcT=srcT):
                    ins = None
                    for ic in range(ICH):
                        ins = e.matmul(PS[bank][:, :],
                                       WA[:, wbase + ic * 128: wbase + (ic + 1) * 128],
                                       srcT[:, ic * NT + tt * 512: ic * NT + tt * 512 + 512],
                                       start=(ic == 0), stop=(ic == ICH - 1))
                    return ins
                p.add("tensor", mm, reads=wa_bufs(wbase, wbase + 2048) + src_bufs,
                      writes=[b_ps[bank]], tag="%smm%d_%d" % (name, oc, tt))
                dslice = dst[:, oc * NT + tt * 512: oc * NT + tt * 512 + 512]
                p.add("scalar", lambda e, d=dslice, bank=bank: e.copy(d, PS[bank][:, :]),
                      reads=[b_ps[bank]], writes=[dst_bufs[oc]], tag="%sev%d_%d" % (name, oc, tt))
                pend.append((oc, tt, bank, 4 + k_idx % 2, dslice))
                if len(pend) > 3:
                    emit_rope(*pend.pop(0))
        while pend:
            emit_rope(*pend.pop(0))

    def v_proj(name, srcT, src_bufs, load_w):
        if load_w:
            wview = WA[:, 4096:4096 + ICH * 512].rearrange("p (c f) -> p c f", f=512)
            dma("gpsimd", wview, d_wvT.ap().rearrange("(c p) f -> p c f", p=128),
                writes=wa_bufs(4096, 4096 + ICH * 512), tag="%swv" % name)
        for ct in range(TCH):
            bank = ct % 2

            def mm(e, ct=ct, bank=bank, srcT=srcT):
                ins = None
                for ic in range(ICH):
                    ins = e.matmul(PS[bank][:, :],
                                   srcT[:, ic * NT + ct * 128: ic * NT + ct * 128 + 128],
                                   WA[:, 4096 + ic * 512: 4096 + (ic + 1) * 512],
                                   start=(ic == 0), stop=(ic == ICH - 1))
                return ins
            p.add("tensor", mm, reads=wa_bufs(4096, 4096 + ICH * 512) + src_bufs,
                  writes=[b_ps[bank]], tag="%svmm%d" % (name, ct))
            dap = bass.AP(VT, ct * KVH * 129, [[VT.shape[1], 128], [129, KVH], [1, 128]])
            p.add("vector", lambda e, a=dap, bank=bank: e.tensor_copy(a, PS[bank][:, :]),
                  reads=[b_ps[bank]], writes=[b_vt], tag="%svev%d" % (name, ct))

    state_banks = [1, 4, 2, 3]

    def state_ktok(name, c):
        def tp(e, c=c):
            ins = None
            for g in range(KVH):
                ins = e.transpose(PT[:, g * 128:(g + 1) * 128],
                                  KF[:, g * NT + c * 128: g * NT + c * 128 + 128], IDB[:, :])
            return ins
        p.add("tensor", tp, reads=b_kf_h + [b_const], writes=[b_pt], tag="%sktp%d" % (name, c))
        p.add("scalar", lambda e: e.copy(KTOK[:, :], PT[:, 0:KVH * 128]),
              reads=[b_pt], writes=[b_ktok], tag="%sktev%d" % (name, c))

    def state_update(name, c, stcp=False):
        for g in range(KVH):
            bank = state_banks[g]
            p.add("tensor", lambda e, g=g, c=c, bank=bank: e.matmul(
                PS[bank][:, 0:129], KTOK[:, g * 128:(g + 1) * 128],
                VT[:, c * KVH * 129 + g * 129: c * KVH * 129 + (g + 1) * 129],
                start=True, stop=True),
                reads=[b_ktok, b_vt], writes=[b_ps[bank]], tag="%ssmm%d_%d" % (name, c, g))
            p.add("vector", lambda e, g=g, bank=bank: e.tensor_tensor(
                STS32[:, g * 129:(g + 1) * 129], STS32[:, g * 129:(g + 1) * 129],
                PS[bank][:, 0:129], ALU.add),
                reads=[b_ps[bank], b_sts32[g]], writes=[b_sts32[g]],
                tag="%ssacc%d_%d" % (name, c, g))
            if stcp:
                p.add("scalar", lambda e, g=g: e.copy(
                    STS[:, g * 129:(g + 1) * 129], STS32[:, g * 129:(g + 1) * 129]),
                    reads=[b_sts32[g]], writes=[b_sts], tag="%sstcp%d_%d" % (name, c, g))

    WA32 = WA.bitcast(F32)
    main_norms = []

    def emit_main_norm(ct):
        ln_norm("m_%d" % ct, 8 + ct, OA[:, ct * DM:(ct + 1) * DM], [b_oa[ct]], ct % 2,
                XN, [b_xn] * 16, keep32=True)
        oslot = 1 - ct % 2

        def rmm32(e, oslot=oslot):
            ins = None
            for ic in range(ICH):
                ins = e.matmul(PS[4][:, 0:NE],
                               U[:, oslot * 2048 + ic * 128: oslot * 2048 + (ic + 1) * 128],
                               RT32[:, ic * NE:(ic + 1) * NE], start=(ic == 0), stop=(ic == ICH - 1))
            return ins
        p.add("tensor", rmm32, reads=u_bufs(oslot * 8192, (oslot + 1) * 8192) + [b_rt],
              writes=[b_ps[4]], tag="rmm%d" % ct)

        p.add("vector", lambda e, ct=ct: e.tensor_copy(LG[:, ct * NE:(ct + 1) * NE],
                                                       PS[4][:, 0:NE]),
              reads=[b_ps[4]], writes=[b_lg], tag="lgstash%d" % ct)

    def xpre_dma(pc):
        xsl = pc % 3
        dma("scalar", WA32[:, xsl * 2048:(xsl + 1) * 2048], d_xp[pc * 128:(pc + 1) * 128, :],
            writes=wa_bufs(xsl * 4096, (xsl + 1) * 4096), tag="xpre%d" % pc)
    xpre_dma(0)
    xpre_dma(1)
    for pc in range(TCH + 2):
        if pc >= 2:
            qc = pc - 2
            xsl = qc % 3
            xs_ap = WA32[:, xsl * 2048:(xsl + 1) * 2048]
            ln_norm("p_%d" % qc, qc, xs_ap, wa_bufs(xsl * 4096, (xsl + 1) * 4096),
                    0, A1, b_a1, xs=xs_ap, xr_b=wa_bufs(xsl * 4096, (xsl + 1) * 4096))
            emit_main_norm(qc)
        if 1 <= pc and pc + 1 < TCH:
            xpre_dma(pc + 1)
        if pc < TCH:
            xsl = pc % 3
            ln_statsA("p_%d" % pc, pc, WA32[:, xsl * 2048:(xsl + 1) * 2048],
                      wa_bufs(xsl * 4096, (xsl + 1) * 4096), sum_eng="scalar")
            ln_statsA("m_%d" % pc, 8 + pc, OA[:, pc * DM:(pc + 1) * DM], [b_oa[pc]])
        if 1 <= pc <= TCH:
            ln_statsB("p_%d" % (pc - 1), pc - 1)
            ln_statsB("m_%d" % (pc - 1), 8 + pc - 1)

    def g3(t, col):
        return bass.AP(t, col, [[t.shape[1], 128], [NE, TCH], [1, NE]])

    def gbc(t, col):
        return bass.AP(t, col, [[t.shape[1], 128], [1, TCH], [0, NE]])

    def top2all(e):
        EQ1 = TOP[:, 0:64]
        L2 = TOP[:, 64:128]
        T1 = TOP[:, 128:192]
        M1, M2, D, E, W1, W2 = 0, 8, 16, 24, 0, 8
        e.reduce_max(T2[:, 0:8], g3(LG, 0), axis=AX.X)
        e.drain()
        e.tensor_tensor(EQ1, LG[:, :], gbc(T2, 0), ALU.is_equal)
        e.drain()
        e.scalar_tensor_tensor(L2, EQ1, -1e30, LG[:, :], ALU.mult, ALU.add)
        e.drain()
        e.reduce_max(T2[:, 8:16], g3(TOP, 64), axis=AX.X)
        e.drain()
        e.tensor_tensor(L2, L2, gbc(T2, 8), ALU.is_equal)
        e.drain()
        return e.tensor_tensor(T2[:, 16:24], T2[:, 8:16], T2[:, 0:8], ALU.subtract)
    p.add("vector", top2all, reads=[b_lg], writes=[b_top, b_t2], tag="top2all_a")
    p.add("scalar", lambda e: e.activation(T2[:, 24:32], T2[:, 16:24], AF.Exp),
          reads=[b_t2], writes=[b_t2], tag="top2all_e")

    def top2allc(e):
        EQ1 = TOP[:, 0:64]
        EQ2 = TOP[:, 64:128]
        T1 = TOP[:, 128:192]
        e.tensor_scalar(T2[:, 16:24], T2[:, 24:32], 1.0, None, ALU.add)
        e.drain()
        e.reciprocal(T2[:, 16:24], T2[:, 16:24])
        e.drain()
        e.tensor_tensor(T2[:, 24:32], T2[:, 24:32], T2[:, 16:24], ALU.mult)
        e.drain()
        e.tensor_tensor(T1, EQ1, gbc(T2, 16), ALU.mult)
        e.tensor_tensor(EQ2, EQ2, gbc(T2, 24), ALU.mult)
        e.drain()
        return e.tensor_tensor(CMB[:, :], T1, EQ2, ALU.add)
    p.add("vector", top2allc, reads=[b_t2, b_top], writes=[b_cmb, b_top], tag="top2all_c")

    def drip_main_stats(oc):
        for ct in (2 * oc, 2 * oc + 1):
            if ct < TCH:
                ln_stats("m_%d" % ct, 8 + ct, OA[:, ct * DM:(ct + 1) * DM], [b_oa[ct]])
    dma("gpsimd", CSB[:, 0:NT], d_cosp.ap(), writes=[b_csb], tag="cosp")
    dma("gpsimd", CSB[:, NT:2 * NT], d_sinp.ap(), writes=[b_csb], tag="sinp")
    all_a1 = list(b_a1)
    kq_proj("pk", d_wkT.ap(), KVH, A1, all_a1, KF, b_kf_h, pm=True)
    v_proj("pv", A1, all_a1, load_w=True)
    for c in range(TCH):
        state_ktok("p", c)
        state_update("p", c)
    if debug:
        dma("sync", dbg["pkf"].ap(), KF[:, :], reads=b_kf_h, tag="dbgpkf")
        dma("sync", dbg["pvt"].ap(), VT[:, :], reads=[b_vt], tag="dbgpvt")
        dma("sync", dbg["pst"].ap(), STS32[:, :], reads=list(b_sts32), tag="dbgpst")

    if debug:
        for blk in range(ICH):
            dma("sync", dbg["xn"].ap()[:, blk * NT:(blk + 1) * NT], XN[:, blk * NT:(blk + 1) * NT],
                reads=[b_xn], tag="dbgxn%d" % blk)

    dma("gpsimd", CSB[:, 0:NT], d_cos.ap(), writes=[b_csb], tag="cosm")
    dma("gpsimd", CSB[:, NT:2 * NT], d_sin.ap(), writes=[b_csb], tag="sinm")

    kq_proj("q", d_wqT.ap(), H, XN, [b_xn], A1, b_a1, pm=False)
    kq_proj("k", d_wkT.ap(), KVH, XN, [b_xn], KF, b_kf_h, pm=False)
    v_proj("v", XN, [b_xn], load_w=False)

    if debug:
        for g in range(KVH):
            dma("sync", dbg["kf"].ap()[:, g * NT:(g + 1) * NT], KF[:, g * NT:(g + 1) * NT],
                reads=[b_kf_h[g]], tag="dbgkf%d" % g)
        for h in range(H):
            dma("sync", dbg["qf"].ap()[:, h * NT:(h + 1) * NT], A1[:, h * NT:(h + 1) * NT],
                reads=[b_a1[h]], tag="dbgqf%d" % h)
        dma("sync", dbg["vt"].ap()[:, :], VT[:, :], reads=[b_vt], tag="dbgvt")

    for c in range(TCH):
        if c == 0:
            p.add("scalar", lambda e: e.copy(STS[:, :], STS32[:, :]),
                  reads=list(b_sts32), writes=[b_sts], tag="stcp%d" % c)
        numrot5 = [1, 4, 2, 3, 5]

        def emit_abar(h):
            g = h // KVH
            abank = 0 if h % 2 == 0 else 6
            amap = AM[:, (h % 2) * 128:(h % 2) * 128 + 128]
            p.add("tensor", lambda e, g=g, h=h, c=c, ab=abank: e.matmul(
                PS[ab][:, 0:128], KF[:, g * NT + c * 128: g * NT + c * 128 + 128],
                A1[:, h * NT + c * 128: h * NT + c * 128 + 128], start=True, stop=True),
                reads=[b_kf_h[g], b_a1[h]], writes=[b_ps[abank]], tag="abar%d_%d" % (c, h))
            p.add("vector", lambda e, a=amap, ab=abank: e.tensor_tensor(
                a, PS[ab][:, 0:128], MASK[:, :], ALU.mult),
                reads=[b_ps[abank], b_const], writes=[b_am[h % 2]], tag="amask%d_%d" % (c, h))

        emit_abar(0)
        for hb in range(4):
            for j in range(4):
                h = hb * 4 + j
                g = h // KVH
                amb = b_am[h % 2]
                amap = AM[:, (h % 2) * 128:(h % 2) * 128 + 128]
                nbank = numrot5[h % 5]
                if h + 1 < H:
                    emit_abar(h + 1)

                def nmm(e, g=g, h=h, c=c, a=amap, nbank=nbank):
                    e.matmul(PS[nbank][:, 0:129], a,
                             VT[:, c * KVH * 129 + g * 129: c * KVH * 129 + (g + 1) * 129],
                             start=True, stop=False)
                    return e.matmul(PS[nbank][:, 0:129],
                                    A1[:, h * NT + c * 128: h * NT + c * 128 + 128],
                                    STS[:, g * 129:(g + 1) * 129], start=False, stop=True)
                p.add("tensor", nmm, reads=[amb, b_vt, b_a1[h], b_sts], writes=[b_ps[nbank]],
                      tag="nmm%d_%d" % (c, h))
                p.add("vector", lambda e, j=j, nbank=nbank: e.tensor_scalar(
                    DEN[:, j:j + 1], PS[nbank][:, 128:129], ATT_EPS, None, ALU.add),
                    reads=[b_ps[nbank]], writes=[b_den[j]], tag="dstash%d_%d" % (c, h))
            p.add("vector", lambda e: e.reciprocal(DEN[:, 4:8], DEN[:, 0:4]),
                  reads=list(b_den), writes=[b_denr], tag="drecip%d_%d" % (c, hb))
            for j in range(4):
                h = hb * 4 + j
                nbank = numrot5[h % 5]
                p.add("scalar", lambda e, h=h, c=c, j=j, nbank=nbank: e.activation(
                    A1[:, h * NT + c * 128: h * NT + c * 128 + 128], PS[nbank][:, 0:128],
                    AF.Copy, scale=DEN[:, 4 + j:5 + j]),
                    reads=[b_ps[nbank], b_denr], writes=[b_a1[h]], tag="oev%d_%d" % (c, h))
        state_ktok("m", c)
        state_update("m", c, stcp=(c < TCH - 1))
        for hh in range(2):
            def tp(e, c=c, hh=hh):
                ins = None
                for k in range(8):
                    h = hh * 8 + k
                    ins = e.transpose(PT[:, k * 128:(k + 1) * 128],
                                      A1[:, h * NT + c * 128: h * NT + c * 128 + 128], IDB[:, :])
                return ins
            p.add("tensor", tp, reads=[b_a1[hh * 8 + k2] for k2 in range(8)] + [b_const],
                  writes=[b_pt], tag="flip%d_%d" % (c, hh))
            dap = bass.AP(A1, (hh * 8) * NT + c * 128, [[A1.shape[1], 128], [NT, 8], [1, 128]])
            if hh == 0:
                p.add("vector", lambda e, a=dap: e.tensor_copy(a, PT[:, :]),
                      reads=[b_pt], writes=[b_a1[hh * 8 + k2] for k2 in range(8)],
                      tag="flipev%d_%d" % (c, hh))
            else:
                p.add("scalar", lambda e, a=dap: e.copy(a, PT[:, :]),
                      reads=[b_pt], writes=[b_a1[hh * 8 + k2] for k2 in range(8)],
                      tag="flipev%d_%d" % (c, hh))

    if debug:
        for h in range(H):
            dma("sync", dbg["ao"].ap()[:, h * NT:(h + 1) * NT], A1[:, h * NT:(h + 1) * NT],
                reads=[b_a1[h]], tag="dbgao%d" % h)

    for fq in range(4):
        if fq % 2 == 0:
            sl_t, sl_bufs = UB, u_bufs(0, 16384)
        else:
            sl_t, sl_bufs = WA, wa_bufs(0, 8192)
        for qq in range(4):
            dst = sl_t[:, qq * 2048:(qq + 1) * 2048].rearrange("p (c f) -> p c f", f=512)
            src = d_woT[qq * 512:(qq + 1) * 512, fq * 512:(fq + 1) * 512].rearrange(
                "(c p) f -> p c f", p=128)
            dma("gpsimd", dst, src, writes=sl_bufs, tag="wo%d_%d" % (fq, qq))
        for ctg in range(4):
            cts = [ctg * 2, ctg * 2 + 1]
            pair = [[2, 3], [5, 6]][ctg % 2]
            for cc in range(16):
                def mm(e, cc=cc, cts=cts, sl_t=sl_t, pair=pair):
                    ins = None
                    for k, ct in enumerate(cts):
                        ins = e.matmul(PS[pair[k]][:, :],
                                       A1[:, cc * NT + ct * 128: cc * NT + ct * 128 + 128],
                                       sl_t[:, cc * 512:(cc + 1) * 512],
                                       start=(cc == 0), stop=(cc == 15))
                    return ins
                p.add("tensor", mm, reads=sl_bufs + [b_a1[cc]],
                      writes=[b_ps[k] for k in pair], tag="womm%d_%d_%d" % (fq, ctg, cc))
            for k, ct in enumerate(cts):
                p.add("vector", lambda e, k=k, ct=ct, fq=fq, pair=pair: e.tensor_tensor(
                    OA[:, ct * DM + fq * 512: ct * DM + (fq + 1) * 512],
                    OA[:, ct * DM + fq * 512: ct * DM + (fq + 1) * 512],
                    PS[pair[k]][:, :], ALU.add),
                    reads=[b_ps[pair[k]], b_oa[ct]], writes=[b_oa[ct]],
                    tag="woev%d_%d_%d" % (fq, ctg, ct))

    if debug:
        for ct in range(TCH):
            dma("sync", dbg["oa1"].ap()[ct * 128:(ct + 1) * 128, :], OA[:, ct * DM:(ct + 1) * DM],
                reads=[b_oa[ct]], tag="dbgoa1_%d" % ct)

    for ct in range(TCH):
        def gmm(e, ct=ct):
            ins = None
            for ic in range(ICH):
                ins = e.matmul(PS[4][:, 0:1],
                               XN[:, ic * NT + ct * 128: ic * NT + ct * 128 + 128],
                               SGT[:, ic:ic + 1], start=(ic == 0), stop=(ic == ICH - 1))
            return ins
        p.add("tensor", gmm, reads=[b_xn, b_rt], writes=[b_ps[4]], tag="gmm%d" % ct)
        p.add("scalar", lambda e, ct=ct: e.activation(SIG[:, ct:ct + 1], PS[4][:, 0:1], AF.Sigmoid),
              reads=[b_ps[4]], writes=[b_cmb], tag="sigev%d" % ct)

    if debug:
        dma("sync", dbg["cmb"].ap(), CMB[:, :], reads=[b_cmb], tag="dbgcmb")
        dma("sync", dbg["sig"].ap(), SIG[:, :], reads=[b_cmb], tag="dbgsig")

    def unit_srcs(u):
        if u < NE:
            return d_wg.ap()[u], d_wu.ap()[u], d_wd.ap()[u]
        blk = u - NE
        return (d_swg.ap()[:, blk * IFF:(blk + 1) * IFF],
                d_swu.ap()[:, blk * IFF:(blk + 1) * IFF],
                d_swd.ap()[blk * IFF:(blk + 1) * IFF, :])

    gu_dma_ops = {}
    gu_last_mm = {}

    def emit_gu_dmas(u, lo=0, hi=IFCH):
        wg_src, wu_src, _ = unit_srcs(u)
        if u not in gu_dma_ops:
            gu_dma_ops[u] = {}
        for i in range(lo, hi):
            gbase = (i % 2) * 2048
            ubase = 4096 + (i % 2) * 2048
            wr_g = wa_bufs(gbase, gbase + 2048) if i < 2 else []
            wr_u = wa_bufs(ubase, ubase + 2048) if i < 2 else []
            dep_g = [gu_last_mm[("g", i % 2)]] if (i < 2 and ("g", i % 2) in gu_last_mm) else []
            dep_u = [gu_last_mm[("u", i % 2)]] if (i < 2 and ("u", i % 2) in gu_last_mm) else []
            og = dma("gpsimd", WA[:, gbase:gbase + 2048].rearrange("p (c f) -> p c f", f=128),
                     wg_src[:, i * 128:(i + 1) * 128].rearrange("(c p) f -> p c f", p=128),
                     writes=wr_g, deps=dep_g, tag="u%dgw%d" % (u, i))
            ou = dma("gpsimd", WA[:, ubase:ubase + 2048].rearrange("p (c f) -> p c f", f=128),
                     wu_src[:, i * 128:(i + 1) * 128].rearrange("(c p) f -> p c f", p=128),
                     writes=wr_u, deps=dep_u, tag="u%duw%d" % (u, i))
            gu_dma_ops[u][("g", i)] = og
            gu_dma_ops[u][("u", i)] = ou

    down_mm_last = {0: None, 1: None}
    dw_ops = {}
    dw_pending = {}

    def emit_down_dma(u, ft):
        _, _, wd_src = unit_srcs(u)
        dsl = ft % 2
        dview = UB[:, dsl * 5632:(dsl + 1) * 5632].rearrange("p (c f) -> p c f", f=512)
        wr = u_bufs(dsl * 11264, (dsl + 1) * 11264) if (u == 0 and ft < 2) else []
        op = dma("gpsimd", dview, wd_src[:, ft * 512:(ft + 1) * 512].rearrange(
            "(c p) f -> p c f", p=128),
            writes=wr, tag="u%ddw%d" % (u, ft))
        dw_ops[(u, ft)] = op
        dw_pending.setdefault((u, ft) if ft >= 2 else ("n", u, dsl), op)
        return op

    out_ops = []
    emit_gu_dmas(0)
    emit_down_dma(0, 0)
    emit_down_dma(0, 1)
    for u in range(NUNIT):
        pool = [0, 1, 4]
        for i in range(IFCH):
            gbase = (i % 2) * 2048
            ubase = 4096 + (i % 2) * 2048
            for tt in range(2):
                k = i * 2 + tt
                bg = pool[(2 * k) % 3]
                bu = pool[(2 * k + 1) % 3]

                def gm(e, tt=tt, gbase=gbase, bg=bg):
                    ins = None
                    for ic in range(ICH):
                        ins = e.matmul(PS[bg][:, :],
                                       WA[:, gbase + ic * 128: gbase + (ic + 1) * 128],
                                       XN[:, ic * NT + tt * 512: ic * NT + tt * 512 + 512],
                                       start=(ic == 0), stop=(ic == ICH - 1))
                    return ins
                op_gm = p.add("tensor", gm, reads=[b_xn], writes=[b_ps[bg]],
                              deps=[gu_dma_ops[u][("g", i)]], tag="u%dgm%d_%d" % (u, i, tt))
                if ("g", i + 2) in gu_dma_ops[u]:
                    gu_dma_ops[u][("g", i + 2)].deps.append(op_gm)
                gu_last_mm[("g", i % 2)] = op_gm

                def um(e, tt=tt, ubase=ubase, bu=bu):
                    ins = None
                    for ic in range(ICH):
                        ins = e.matmul(PS[bu][:, :],
                                       WA[:, ubase + ic * 128: ubase + (ic + 1) * 128],
                                       XN[:, ic * NT + tt * 512: ic * NT + tt * 512 + 512],
                                       start=(ic == 0), stop=(ic == ICH - 1))
                    return ins
                op_um = p.add("tensor", um, reads=[b_xn], writes=[b_ps[bu]],
                              deps=[gu_dma_ops[u][("u", i)]], tag="u%dum%d_%d" % (u, i, tt))
                if ("u", i + 2) in gu_dma_ops[u]:
                    gu_dma_ops[u][("u", i + 2)].deps.append(op_um)
                gu_last_mm[("u", i % 2)] = op_um
                gsl = k % 2
                p.add("scalar", lambda e, gsl=gsl, bg=bg: e.activation(
                    GS[:, gsl * 512:(gsl + 1) * 512], PS[bg][:, :], AF.Silu),
                    reads=[b_ps[bg]], writes=[b_gs[gsl]], tag="u%dsil%d_%d" % (u, i, tt))
                hslice = A1[:, i * NT + tt * 512: i * NT + tt * 512 + 512]
                p.add("vector", lambda e, h2=hslice, gsl=gsl, bu=bu: e.tensor_tensor(
                    h2, GS[:, gsl * 512:(gsl + 1) * 512], PS[bu][:, :], ALU.mult),
                    reads=[b_gs[gsl], b_ps[bu]], writes=[b_a1[i]], tag="u%dh%d_%d" % (u, i, tt))
        if u + 1 < NUNIT:
            emit_gu_dmas(u + 1, 0, 2)
        dw23 = [emit_down_dma(u, 2), emit_down_dma(u, 3)]
        if u + 1 < NUNIT:
            dw01n = [emit_down_dma(u + 1, 0), emit_down_dma(u + 1, 1)]
        else:
            dw01n = []
        if u + 1 < NUNIT:
            emit_gu_dmas(u + 1, 2, IFCH)
        for ft in range(4):
            dsl = ft % 2
            dbase = dsl * 5632
            for ctg in range(4):
                cts = [ctg * 2, ctg * 2 + 1]
                pair = [[2, 3], [5, 6]][ctg % 2]

                def dmm(e, cts=cts, dbase=dbase, pair=pair):
                    ins = None
                    for i in range(IFCH):
                        for kk, ct in enumerate(cts):
                            ins = e.matmul(PS[pair[kk]][:, :],
                                           A1[:, i * NT + ct * 128: i * NT + ct * 128 + 128],
                                           UB[:, dbase + i * 512: dbase + (i + 1) * 512],
                                           start=(i == 0), stop=(i == IFCH - 1))
                    return ins
                op_dmm = p.add("tensor", dmm, reads=[b_a1[i2] for i2 in range(IFCH)],
                               writes=[b_ps[k] for k in pair],
                               deps=[dw_ops[(u, ft)]],
                               tag="u%ddm%d_%d" % (u, ft, ctg))
                if ft < 2:
                    dw23[ft].deps.append(op_dmm)
                elif dw01n:
                    dw01n[ft - 2].deps.append(op_dmm)
                down_mm_last[ft % 2] = op_dmm
                for kk, ct in enumerate(cts):
                    oaslice = OA[:, ct * DM + ft * 512: ct * DM + (ft + 1) * 512]
                    sap = CMB[:, ct * NE + u: ct * NE + u + 1] if u < NE else SIG[:, ct:ct + 1]
                    p.add("vector", lambda e, o=oaslice, kk=kk, sap=sap, pair=pair:
                          e.scalar_tensor_tensor(o, PS[pair[kk]][:, :], sap, o,
                                                 ALU.mult, ALU.add),
                          reads=[b_ps[pair[kk]], b_cmb, b_oa[ct]], writes=[b_oa[ct]],
                          tag="u%ddev%d_%d" % (u, ft, ct))
        if debug and u == 0:
            for i2 in range(IFCH):
                dma("sync", dbg["h0"].ap()[:, i2 * NT:(i2 + 1) * NT], A1[:, i2 * NT:(i2 + 1) * NT],
                    reads=[b_a1[i2]], tag="dbgh0_%d" % i2)
            _dmp = dma("sync", dbg["dsl0"].ap(), UB[:, 0:IFCH * 512],
                       deps=[dw_ops[(0, 2)], down_mm_last[0]], tag="dbgdsl0")
            if (1, 0) in dw_ops:
                dw_ops[(1, 0)].deps.append(_dmp)
            for ct in range(TCH):
                dma("sync", dbg["oau0"].ap()[ct * 128:(ct + 1) * 128, :],
                    OA[:, ct * DM:(ct + 1) * DM], reads=[b_oa[ct]], tag="dbgoau0_%d" % ct)
        if u == NUNIT - 1:
            for ct in range(TCH):
                out_ops.append(dma("sync", d_out[ct * 128:(ct + 1) * 128, :],
                                   OA[:, ct * DM:(ct + 1) * DM],
                                   reads=[b_oa[ct]], tag="outst%d" % ct))

    p.add("scalar", lambda e: e.memzero(STAT[:, 0:1]), deps=out_ops, tag="finwait")

    import contextlib
    with contextlib.ExitStack() as st:
        sems = {}
        for e in ENGS:
            sems["eng_" + e] = st.enter_context(nc.semaphore("s_" + e))
        for q in ("gpsimd", "sync", "scalar"):
            sems["dma_" + q] = [st.enter_context(nc.semaphore("sd_%s_%d" % (q, i)))
                                for i in range(NSEM[q])]
        block = st.enter_context(nc.Block())
        p.finalize_and_emit(block, sems)
    return nc


_cached = {}


def _get_nc(debug=False):
    if debug not in _cached:
        _cached[debug] = build_program(debug)
    return _cached[debug]


def _rope_tables():
    inv_freq = 1.0 / ROPE_BASE ** (np.arange(0, HD, 2, dtype=np.float32) / HD)
    t = np.arange(S, dtype=np.float32)
    freqs = np.outer(t, inv_freq).astype(np.float32)
    emb = np.concatenate([freqs, freqs], -1)
    return np.cos(emb).astype(np.float32), np.sin(emb).astype(np.float32)


def make_in_maps(inputs):
    x = np.asarray(inputs["L0_attn_x"], dtype=np.float32)
    wq = np.asarray(inputs["L0_attn_Wq"], dtype=np.float32)
    wk = np.asarray(inputs["L0_attn_Wk"], dtype=np.float32)
    wv = np.asarray(inputs["L0_attn_Wv"], dtype=np.float32)
    wo = np.asarray(inputs["L0_attn_Wo"], dtype=np.float32)
    rw = np.asarray(inputs["L0_ffn_router_w"], dtype=np.float32)
    wg = np.ascontiguousarray(np.asarray(inputs["L0_ffn_w_gate"], dtype=np.float32))
    wu = np.ascontiguousarray(np.asarray(inputs["L0_ffn_w_up"], dtype=np.float32))
    wd = np.ascontiguousarray(np.asarray(inputs["L0_ffn_w_down"], dtype=np.float32))
    swg = np.ascontiguousarray(np.asarray(inputs["L0_ffn_sw_gate"], dtype=np.float32))
    swu = np.ascontiguousarray(np.asarray(inputs["L0_ffn_sw_up"], dtype=np.float32))
    swd = np.ascontiguousarray(np.asarray(inputs["L0_ffn_sw_down"], dtype=np.float32))
    sg = np.asarray(inputs["L0_ffn_shared_gate_w"], dtype=np.float32)

    wqT = np.ascontiguousarray(wq.T)
    wkT = np.ascontiguousarray(wk.T)
    wvT = np.ascontiguousarray(wv.T)
    woT = np.ascontiguousarray(wo.T)
    rtT = np.ascontiguousarray(rw.reshape(NE, ICH, 128).transpose(2, 1, 0).reshape(128, ICH * NE))
    sgT = np.ascontiguousarray(sg.reshape(ICH, 128).T)

    cos, sin = _rope_tables()
    cosT = np.ascontiguousarray(cos.T)
    sinT = np.ascontiguousarray(sin.T)

    j = np.arange(128)
    maskJI = (j[:, None] <= j[None, :]).astype(np.float32)
    id32 = np.eye(128, dtype=np.float32)
    idb = np.eye(128, dtype=np.float32)
    R = np.zeros((128, 128), dtype=np.float32)
    for dp in range(64):
        R[dp + 64, dp] = -1.0
    for dp in range(64, 128):
        R[dp - 64, dp] = 1.0
    rmat = R
    onesb = np.ones((128, 1), dtype=np.float32)

    in_maps = []
    for core in range(8):
        b = core // 2
        hh = core % 2
        in_maps.append({
            "x_main": np.ascontiguousarray(x[b, hh * NT:(hh + 1) * NT, :]),
            "x_prefix": np.ascontiguousarray(x[b, 0:NT, :]),
            "wqT": wqT, "wkT": wkT, "wvT": wvT, "woT": woT, "rtT": rtT,
            "wg": wg, "wu": wu, "wd": wd, "swg": swg, "swu": swu, "swd": swd,
            "sgT": sgT,
            "cosT": np.ascontiguousarray(cosT[:, hh * NT:(hh + 1) * NT]),
            "sinT": np.ascontiguousarray(sinT[:, hh * NT:(hh + 1) * NT]),
            "cosTp": np.ascontiguousarray(cosT[:, 0:NT]),
            "sinTp": np.ascontiguousarray(sinT[:, 0:NT]),
            "maskJI": maskJI, "id32": id32, "idb": idb, "rmat": rmat,
            "onesb": onesb,
            "pmask": np.full((128, 1), float(hh), dtype=np.float32),
        })
    return in_maps


def kernel(**inputs):
    nc = _get_nc(False)
    in_maps = make_in_maps(inputs)
    res = run_bass_kernel_spmd(nc, in_maps, core_ids=list(range(8)))
    out = np.empty((B, S, DM), dtype=np.float32)
    for core in range(8):
        b = core // 2
        hh = core % 2
        out[b, hh * NT:(hh + 1) * NT, :] = res.results[core]["out"]
    return out


# version 22 (final)
# speedup vs baseline: 1.1017x; 1.0094x over previous
import numpy as np
import ml_dtypes
import concourse.bass as bass
import concourse.mybir as mybir
from concourse.bass_utils import run_bass_kernel_spmd

F32 = mybir.dt.float32
BF16 = mybir.dt.bfloat16
AF = mybir.ActivationFunctionType
ALU = mybir.AluOpType
AX = mybir.AxisListType

B, S, DM = 4, 2048, 2048
H, KVH, HD = 16, 4, 128
NT = 1024
TCH = NT // 128
ICH = DM // 128
NE, IFF, SI = 8, 1408, 5632
IFCH = IFF // 128
NUNIT = 12
LN_EPS = 1e-5
ATT_EPS = 1e-6
ROPE_BASE = 10000.0

ENGS = ("tensor", "vector", "scalar", "gpsimd", "sync")
NSEM = {"gpsimd": 10, "sync": 6, "scalar": 4}


class Op:
    __slots__ = ("eng", "emit", "deps", "kind", "seq", "observed", "qidx", "tag", "eidx")

    def __init__(self, eng, emit, deps, kind, tag=""):
        self.eng = eng
        self.emit = emit
        self.deps = deps
        self.kind = kind
        self.seq = None
        self.observed = False
        self.qidx = None
        self.tag = tag


class Buf:
    __slots__ = ("writer", "readers", "name", "psum")

    def __init__(self, name="", psum=False):
        self.writer = None
        self.readers = []
        self.name = name
        self.psum = psum


class Prog:
    def __init__(self):
        self.ops = []
        self.dma_counts = {"gpsimd": 0, "sync": 0, "scalar": 0}

    def add(self, eng, emit, reads=(), writes=(), deps=(), kind="compute", tag=""):
        dd = list(deps)
        for b in reads:
            if b.writer is not None:
                dd.append(b.writer)
            if b.psum:
                dd.extend(r for r in b.readers if r.eng != eng)
        for b in writes:
            if b.writer is not None:
                dd.append(b.writer)
            dd.extend(b.readers)
        op = Op(eng, emit, dd, kind, tag)
        if kind == "dma":
            i = self.dma_counts[eng]
            self.dma_counts[eng] += 1
            op.qidx = (i % NSEM[eng], i // NSEM[eng])
        self.ops.append(op)
        for b in reads:
            b.readers.append(op)
        for b in writes:
            b.writer = op
            b.readers = []
        return op

    def finalize_and_emit(self, block, sems):
        for op in self.ops:
            for d in op.deps:
                if d.kind != "dma" and d.eng != op.eng:
                    d.observed = True
        cnt = {e: 0 for e in ENGS}
        for op in self.ops:
            if op.kind == "compute" and op.observed:
                cnt[op.eng] += 1
                op.seq = cnt[op.eng]
        eidx_cnt = {e: 0 for e in ENGS}
        for op in self.ops:
            op.eidx = eidx_cnt[op.eng]
            eidx_cnt[op.eng] += 1
        per_eng = {e: [] for e in ENGS}
        for op in self.ops:
            waits = {}
            if op.kind == "dma" and op.qidx[1] > 0:
                waits[("dma", op.eng, op.qidx[0])] = 16 * op.qidx[1]
            for d in op.deps:
                if d.kind == "dma":
                    key, val = ("dma", d.eng, d.qidx[0]), 16 * (d.qidx[1] + 1)
                elif d.eng != op.eng:
                    key, val = ("eng", d.eng), d.seq
                else:
                    continue
                if waits.get(key, -1) < val:
                    waits[key] = val
            per_eng[op.eng].append((op, waits))

        def emit_engine(eng_obj, eng_name):
            hw = {}
            last_drain = -1
            for op, waits in per_eng[eng_name]:
                need_drain = eng_name != "tensor" and any(
                    d.eng == eng_name and d.kind == "compute"
                    and d.eidx > last_drain for d in op.deps)
                if need_drain:
                    eng_obj.drain()
                    last_drain = op.eidx - 1
                for key in sorted(waits, key=str):
                    val = waits[key]
                    if hw.get(key, 0) >= val:
                        continue
                    hw[key] = val
                    if key[0] == "dma":
                        sem = sems["dma_" + key[1]][key[2]]
                    else:
                        sem = sems["eng_" + key[1]]
                    eng_obj.wait_ge(sem, val)
                ins = op.emit(eng_obj)
                if op.kind == "dma":
                    ins.then_inc(sems["dma_" + op.eng][op.qidx[0]], 16)
                elif op.observed:
                    ins.then_inc(sems["eng_" + op.eng], 1)

        @block.tensor
        def _(t):
            emit_engine(t, "tensor")

        @block.vector
        def _(v):
            emit_engine(v, "vector")

        @block.scalar
        def _(s):
            emit_engine(s, "scalar")

        @block.gpsimd
        def _(g):
            emit_engine(g, "gpsimd")

        @block.sync
        def _(sy):
            emit_engine(sy, "sync")


def build_program(debug=False):
    nc = bass.Bass("TRN2", target_bir_lowering=False)

    d_x = nc.dram_tensor("x_main", [NT, DM], F32, kind="ExternalInput")
    d_xp = nc.dram_tensor("x_prefix", [NT, DM], F32, kind="ExternalInput")
    d_wqT = nc.dram_tensor("wqT", [DM, H * HD], F32, kind="ExternalInput")
    d_wkT = nc.dram_tensor("wkT", [DM, KVH * HD], F32, kind="ExternalInput")
    d_wvT = nc.dram_tensor("wvT", [DM, KVH * HD], F32, kind="ExternalInput")
    d_woT = nc.dram_tensor("woT", [H * HD, DM], F32, kind="ExternalInput")
    d_rtT = nc.dram_tensor("rtT", [128, ICH * NE], F32, kind="ExternalInput")
    d_wg = nc.dram_tensor("wg", [NE, DM, IFF], F32, kind="ExternalInput")
    d_wu = nc.dram_tensor("wu", [NE, DM, IFF], F32, kind="ExternalInput")
    d_wd = nc.dram_tensor("wd", [NE, IFF, DM], F32, kind="ExternalInput")
    d_swg = nc.dram_tensor("swg", [DM, SI], F32, kind="ExternalInput")
    d_swu = nc.dram_tensor("swu", [DM, SI], F32, kind="ExternalInput")
    d_swd = nc.dram_tensor("swd", [SI, DM], F32, kind="ExternalInput")
    d_sgT = nc.dram_tensor("sgT", [128, ICH], F32, kind="ExternalInput")
    d_cos = nc.dram_tensor("cosT", [128, NT], F32, kind="ExternalInput")
    d_sin = nc.dram_tensor("sinT", [128, NT], F32, kind="ExternalInput")
    d_cosp = nc.dram_tensor("cosTp", [128, NT], F32, kind="ExternalInput")
    d_sinp = nc.dram_tensor("sinTp", [128, NT], F32, kind="ExternalInput")
    d_mask = nc.dram_tensor("maskJI", [128, 128], F32, kind="ExternalInput")
    d_id32 = nc.dram_tensor("id32", [128, 128], F32, kind="ExternalInput")
    d_idb = nc.dram_tensor("idb", [128, 128], F32, kind="ExternalInput")
    d_rmat = nc.dram_tensor("rmat", [128, 128], F32, kind="ExternalInput")
    d_ones = nc.dram_tensor("onesb", [128, 1], F32, kind="ExternalInput")
    d_pmask = nc.dram_tensor("pmask", [128, 1], F32, kind="ExternalInput")
    d_out = nc.dram_tensor("out", [NT, DM], F32, kind="ExternalOutput")

    dbg = {}
    if debug:
        dbg["xn"] = nc.dram_tensor("dbg_xn", [128, ICH * NT], BF16, kind="ExternalOutput")
        dbg["qf"] = nc.dram_tensor("dbg_qf", [128, H * NT], BF16, kind="ExternalOutput")
        dbg["kf"] = nc.dram_tensor("dbg_kf", [128, KVH * NT], BF16, kind="ExternalOutput")
        dbg["vt"] = nc.dram_tensor("dbg_vt", [128, TCH * KVH * 129], BF16, kind="ExternalOutput")
        dbg["ao"] = nc.dram_tensor("dbg_ao", [128, H * NT], BF16, kind="ExternalOutput")
        dbg["cmb"] = nc.dram_tensor("dbg_cmb", [128, TCH * NE], F32, kind="ExternalOutput")
        dbg["sig"] = nc.dram_tensor("dbg_sig", [128, TCH], F32, kind="ExternalOutput")
        dbg["oa1"] = nc.dram_tensor("dbg_oa1", [NT, DM], F32, kind="ExternalOutput")
        dbg["pkf"] = nc.dram_tensor("dbg_pkf", [128, KVH * NT], BF16, kind="ExternalOutput")
        dbg["pvt"] = nc.dram_tensor("dbg_pvt", [128, TCH * KVH * 129], BF16, kind="ExternalOutput")
        dbg["pst"] = nc.dram_tensor("dbg_pst", [128, KVH * 129], F32, kind="ExternalOutput")
        dbg["h0"] = nc.dram_tensor("dbg_h0", [128, IFCH * NT], BF16, kind="ExternalOutput")
        dbg["dsl0"] = nc.dram_tensor("dbg_dsl0", [128, IFCH * 512], BF16, kind="ExternalOutput")
        dbg["oau0"] = nc.dram_tensor("dbg_oau0", [NT, DM], F32, kind="ExternalOutput")

    OA = nc.alloc_sbuf_tensor("OA", [128, TCH * DM], F32)
    XN = nc.alloc_sbuf_tensor("XN", [128, ICH * NT], BF16)
    A1 = nc.alloc_sbuf_tensor("A1", [128, 16 * NT], BF16)
    KF = nc.alloc_sbuf_tensor("KF", [128, KVH * NT], BF16)
    VT = nc.alloc_sbuf_tensor("VT", [128, TCH * KVH * 129], BF16)
    KTOK = nc.alloc_sbuf_tensor("KTOK", [128, KVH * 128], BF16)
    STS = nc.alloc_sbuf_tensor("STS", [128, KVH * 129], BF16)
    STS32 = nc.alloc_sbuf_tensor("STS32", [128, KVH * 129], F32)
    U = nc.alloc_sbuf_tensor("U", [128, 5632], F32)
    UB = U.bitcast(BF16)

    WA = nc.alloc_sbuf_tensor("WA", [128, 12288], BF16)
    GS = nc.alloc_sbuf_tensor("GS", [128, 2 * 512], F32)
    MASK = nc.alloc_sbuf_tensor("MASK", [128, 128], F32)
    ID32 = nc.alloc_sbuf_tensor("ID32", [128, 128], F32)
    IDB = nc.alloc_sbuf_tensor("IDB", [128, 128], BF16)
    RMAT = nc.alloc_sbuf_tensor("RMAT", [128, 128], BF16)
    ONESB = nc.alloc_sbuf_tensor("ONESB", [128, 1], BF16)
    PMASK = nc.alloc_sbuf_tensor("PMASK", [128, 1], F32)
    RT32 = nc.alloc_sbuf_tensor("RT32", [128, ICH * NE], F32)
    SGT = nc.alloc_sbuf_tensor("SGT", [128, ICH], BF16)
    CMB = nc.alloc_sbuf_tensor("CMB", [128, TCH * NE], F32)
    SIG = nc.alloc_sbuf_tensor("SIG", [128, TCH], F32)
    AM = nc.alloc_sbuf_tensor("AM", [128, 2 * 128], BF16)
    STAT = nc.alloc_sbuf_tensor("STAT", [128, 16 * 8], F32)
    T2 = nc.alloc_sbuf_tensor("T2", [128, 32], F32)
    DEN = nc.alloc_sbuf_tensor("DEN", [128, 16], F32)
    LG = nc.alloc_sbuf_tensor("LG", [128, TCH * NE], F32)
    TOP = nc.alloc_sbuf_tensor("TOP", [128, 192], F32)
    TRASH = nc.alloc_sbuf_tensor("TRASH", [128, 2048], BF16)
    CSB = TRASH

    PS = [nc.alloc_psum_tensor("ps%d" % i, [128, 512], F32) for i in range(7)]
    PT = nc.alloc_psum_tensor("pt", [128, 1024], BF16)

    p = Prog()

    b_oa = [Buf("oa%d" % i) for i in range(TCH)]
    b_xn = Buf("xn")
    b_a1 = [Buf("a1_%d" % h) for h in range(16)]
    b_kf_h = [Buf("kf%d" % g) for g in range(KVH)]
    b_vt = Buf("vt")
    b_ktok = Buf("ktok")
    b_sts = Buf("sts")
    b_sts32 = [Buf("sts32_%d" % g) for g in range(KVH)]
    b_gs = [Buf("gs0"), Buf("gs1"), Buf("gs2")]
    b_ps = [Buf("ps%d" % i, psum=True) for i in range(7)]
    b_pt = Buf("pt", psum=True)
    b_const = Buf("const")
    b_rt = Buf("rt")
    b_cmb = Buf("cmb")
    b_am = [Buf("am0"), Buf("am1")]
    b_stat = [Buf("stat%d" % i) for i in range(16)]
    b_stat2 = [Buf("statq%d" % i) for i in range(16)]
    b_t2 = Buf("t2")
    b_den = [Buf("den%d" % i) for i in range(4)]
    b_denr = Buf("denr")
    b_lg = Buf("lg")
    b_csb = b_trash = Buf("trash")
    b_top = Buf("top")
    b_wa = [Buf("wa%d" % i) for i in range(24)]
    b_u = [Buf("u%d" % i) for i in range(11)]

    def wa_bufs(c0, c1):
        return b_wa[c0 // 512: (c1 + 511) // 512]

    def u_bufs(byte0, byte1):
        return b_u[byte0 // 2048: (byte1 + 2047) // 2048]

    def bcast(t, col, n):
        return bass.AP(t, col, [[t.shape[1], 128], [0, n]])

    def dma(eng, dst_ap, src_ap, reads=(), writes=(), deps=(), tag=""):
        return p.add(eng, lambda e, d=dst_ap, s=src_ap: e.dma_start(d, s),
                     reads=reads, writes=writes, deps=deps, kind="dma", tag=tag)

    dma("sync", MASK[:, :], d_mask.ap(), writes=[b_const], tag="mask")
    dma("sync", ID32[:, :], d_id32.ap(), writes=[b_const], tag="id32")
    dma("gpsimd", IDB[:, :], d_idb.ap(), writes=[b_const], tag="idb")
    dma("gpsimd", RMAT[:, :], d_rmat.ap(), writes=[b_const], tag="rmat")
    dma("gpsimd", ONESB[:, :], d_ones.ap(), writes=[b_const], tag="ones")
    dma("sync", PMASK[:, :], d_pmask.ap(), writes=[b_const], tag="pmask")
    dma("sync", RT32[:, :], d_rtT.ap(), writes=[b_rt], tag="rt")
    dma("gpsimd", SGT[:, :], d_sgT.ap(), writes=[b_rt], tag="sgt")
    for ct in range(TCH):
        dma("sync", OA[:, ct * DM:(ct + 1) * DM], d_x[ct * 128:(ct + 1) * 128, :],
            writes=[b_oa[ct]], tag="xmain%d" % ct)

    ones_ap = bass.AP(VT, 128, [[TCH * KVH * 129, 128], [129, TCH * KVH], [1, 1]])
    p.add("vector", lambda e, a=ones_ap: e.memset(a, 1.0), writes=[b_vt], tag="vtones")
    p.add("vector", lambda e: e.memset(STS32[:, :], 0.0), writes=b_sts32, tag="sts0")

    def ln_statsA(uid, idx, src_ap, src_bufs, sum_eng="vector"):
        st = idx * 8
        sb = b_stat[idx]
        sb2 = b_stat2[idx]
        if sum_eng == "vector":
            p.add("vector", lambda e, s=src_ap, st=st: e.reduce_sum(
                STAT[:, st:st + 1], s, axis=AX.X),
                reads=src_bufs, writes=[sb], tag="lnsum%s" % uid)
        else:
            p.add("scalar", lambda e, s=src_ap, st=st: e.activation(
                TRASH[:, :], s, AF.Copy, accum_out=STAT[:, st:st + 1]),
                reads=src_bufs, writes=[sb, b_trash], tag="lnsum%s" % uid)
        p.add("scalar", lambda e, s=src_ap, st=st: e.activation(
            GS.bitcast(BF16)[:, :], s, AF.Square, accum_out=STAT[:, st + 1:st + 2]),
            reads=src_bufs, writes=[sb2] + b_gs, tag="lnsq%s" % uid)

    def ln_statsB(uid, idx):
        st = idx * 8
        sb = b_stat[idx]
        sb2 = b_stat2[idx]

        def stats(e, st=st):
            e.tensor_scalar(STAT[:, st + 2:st + 3], STAT[:, st:st + 1], 1.0 / DM, None, ALU.mult)
            e.tensor_scalar(STAT[:, st + 4:st + 5], STAT[:, st + 1:st + 2], 1.0 / DM, None, ALU.mult)
            e.drain()
            e.tensor_tensor(STAT[:, st + 3:st + 4], STAT[:, st + 2:st + 3],
                            STAT[:, st + 2:st + 3], ALU.mult)
            e.drain()
            e.tensor_tensor(STAT[:, st + 4:st + 5], STAT[:, st + 4:st + 5],
                            STAT[:, st + 3:st + 4], ALU.subtract)
            e.drain()
            return e.tensor_scalar(STAT[:, st + 4:st + 5], STAT[:, st + 4:st + 5],
                                   LN_EPS, None, ALU.add)
        p.add("vector", stats, reads=[sb, sb2], writes=[sb], tag="lnstat%s" % uid)

        def rstd(e, st=st):
            e.activation(STAT[:, st + 7:st + 8], STAT[:, st + 4:st + 5], AF.Ln)
            e.drain()
            return e.activation(STAT[:, st + 6:st + 7], STAT[:, st + 7:st + 8],
                                AF.Exp, scale=-0.5)
        p.add("scalar", rstd, reads=[sb], writes=[sb], tag="lnrstd%s" % uid)

    def ln_norm(uid, idx, src_ap, src_bufs, xr_slot, dstT, dst_bufs, keep32=False,
                xs=None, xr_b=None):
        st = idx * 8
        sb = b_stat[idx]
        if xs is None:
            xs = U[:, xr_slot * 2048:(xr_slot + 1) * 2048]
            xr_b = u_bufs(xr_slot * 8192, (xr_slot + 1) * 8192)
        p.add("vector", lambda e, s=src_ap, x=xs, st=st: e.tensor_scalar(
            x, s, STAT[:, st + 2:st + 3], STAT[:, st + 6:st + 7], ALU.subtract, ALU.mult),
            reads=src_bufs + [sb], writes=xr_b, tag="lnxn%s" % uid)
        ct = int(uid.split("_")[-1])
        for q in range(4):
            bank = q % 2

            def tp(e, q=q, bank=bank, x=xs):
                ins = None
                for k in range(4):
                    ic = q * 4 + k
                    ins = e.transpose(PS[bank][:, k * 128:(k + 1) * 128],
                                      x[:, ic * 128:(ic + 1) * 128], ID32[:, :])
                return ins
            p.add("tensor", tp, reads=xr_b + [b_const], writes=[b_ps[bank]],
                  tag="lntp%s_%d" % (uid, q))
            dap = bass.AP(dstT, (q * 4) * NT + ct * 128, [[dstT.shape[1], 128], [NT, 4], [1, 128]])
            ev_eng = "vector" if q % 2 == 0 else "scalar"

            def ev(e, a=dap, bank=bank):
                if hasattr(e, "tensor_copy"):
                    return e.tensor_copy(a, PS[bank][:, 0:512])
                return e.copy(a, PS[bank][:, 0:512])
            p.add(ev_eng, ev, reads=[b_ps[bank]],
                  writes=[dst_bufs[q * 4 + k2] for k2 in range(4)],
                  tag="lnev%s_%d" % (uid, q))
            if keep32:
                oslot = 1 - xr_slot
                d32 = U[:, oslot * 2048 + q * 512: oslot * 2048 + (q + 1) * 512]
                k_eng = "scalar" if q % 2 == 0 else "vector"

                def k32(e, d=d32, bank=bank):
                    if hasattr(e, "tensor_copy"):
                        return e.tensor_copy(d, PS[bank][:, 0:512])
                    return e.copy(d, PS[bank][:, 0:512])
                p.add(k_eng, k32,
                      reads=[b_ps[bank]], writes=u_bufs(oslot * 8192 + q * 2048,
                                                        oslot * 8192 + (q + 1) * 2048),
                      tag="lnev32%s_%d" % (uid, q))

    def kq_proj(name, wT_d, dout_ch, srcT, src_bufs, dst, dst_bufs, pm, drip=None):
        pend = []
        loc = [[Buf("%sl%d_%d" % (name, o2, t2)) for t2 in range(2)]
               for o2 in range(dout_ch)]

        GSB = GS.bitcast(BF16)

        def emit_rope(oc, tt, bank, rbank, dslice):
            cosap = CSB[:, tt * 512: tt * 512 + 512]
            sinap = CSB[:, NT + tt * 512: NT + tt * 512 + 512]
            p.add("tensor", lambda e, d=dslice, rbank=rbank: e.matmul(
                PS[rbank][:, :], RMAT[:, :], d, start=True, stop=True),
                reads=[loc[oc][tt], b_const], writes=[b_ps[rbank]],
                tag="%srot%d_%d" % (name, oc, tt))
            p.add("scalar", lambda e, rbank=rbank: e.copy(GSB[:, 1024:1536], PS[rbank][:, :]),
                  reads=[b_ps[rbank]], writes=[b_gs[2]], tag="%srotev%d_%d" % (name, oc, tt))
            p.add("vector", lambda e, c2=cosap, d=dslice: e.tensor_tensor(
                GSB[:, 0:512], d, c2, ALU.mult),
                reads=[loc[oc][tt], b_csb], writes=[b_gs[0]], tag="%st1%d_%d" % (name, oc, tt))
            p.add("vector", lambda e, s2=sinap: e.tensor_tensor(
                GSB[:, 512:1024], GSB[:, 1024:1536], s2, ALU.mult),
                reads=[b_gs[2], b_csb], writes=[b_gs[1]], tag="%st2%d_%d" % (name, oc, tt))
            p.add("vector", lambda e: e.tensor_tensor(GSB[:, 0:512], GSB[:, 0:512],
                                                      GSB[:, 512:1024], ALU.add),
                  reads=[b_gs[0], b_gs[1]], writes=[b_gs[0]], tag="%st3%d_%d" % (name, oc, tt))
            p.add("vector", lambda e: e.tensor_scalar(GSB[:, 512:1024], GSB[:, 0:512],
                                                      0.0, None, ALU.min),
                  reads=[b_gs[0]], writes=[b_gs[1]], tag="%smin%d_%d" % (name, oc, tt))
            p.add("scalar", lambda e: e.activation(GSB[:, 512:1024], GSB[:, 512:1024], AF.Exp),
                  reads=[b_gs[1]], writes=[b_gs[1]], tag="%sexp%d_%d" % (name, oc, tt))
            p.add("vector", lambda e: e.tensor_scalar(GSB[:, 0:512], GSB[:, 0:512],
                                                      0.0, None, ALU.max),
                  reads=[b_gs[0]], writes=[b_gs[0]], tag="%smax%d_%d" % (name, oc, tt))
            if pm:
                def fin(e, d=dslice):
                    e.tensor_tensor(GSB[:, 0:512], GSB[:, 0:512], GSB[:, 512:1024], ALU.add)
                    e.drain()
                    return e.tensor_scalar(d, GSB[:, 0:512], PMASK[:, 0:1], None, ALU.mult)
                p.add("vector", fin, reads=[b_gs[0], b_gs[1], b_const],
                      writes=[dst_bufs[oc], loc[oc][tt], b_gs[0]], tag="%sfin%d_%d" % (name, oc, tt))
            else:
                p.add("vector", lambda e, d=dslice: e.tensor_tensor(
                    d, GSB[:, 0:512], GSB[:, 512:1024], ALU.add),
                    reads=[b_gs[0], b_gs[1]], writes=[dst_bufs[oc], loc[oc][tt]],
                    tag="%sfin%d_%d" % (name, oc, tt))

        for oc in range(dout_ch):
            if drip:
                drip(oc)
            wslot = oc % 2
            wbase = wslot * 2048
            wview = WA[:, wbase:wbase + 2048].rearrange("p (c f) -> p c f", f=128)
            dma("gpsimd", wview,
                wT_d[:, oc * 128:(oc + 1) * 128].rearrange("(c p) f -> p c f", p=128),
                writes=wa_bufs(wbase, wbase + 2048), tag="%sw%d" % (name, oc))
            for tt in range(2):
                k_idx = oc * 2 + tt
                bank = k_idx % 4

                def mm(e, tt=tt, wbase=wbase, bank=bank, srcT=srcT):
                    ins = None
                    for ic in range(ICH):
                        ins = e.matmul(PS[bank][:, :],
                                       WA[:, wbase + ic * 128: wbase + (ic + 1) * 128],
                                       srcT[:, ic * NT + tt * 512: ic * NT + tt * 512 + 512],
                                       start=(ic == 0), stop=(ic == ICH - 1))
                    return ins
                p.add("tensor", mm, reads=wa_bufs(wbase, wbase + 2048) + src_bufs,
                      writes=[b_ps[bank]], tag="%smm%d_%d" % (name, oc, tt))
                dslice = dst[:, oc * NT + tt * 512: oc * NT + tt * 512 + 512]
                p.add("scalar", lambda e, d=dslice, bank=bank: e.copy(d, PS[bank][:, :]),
                      reads=[b_ps[bank]], writes=[loc[oc][tt]], tag="%sev%d_%d" % (name, oc, tt))
                pend.append((oc, tt, bank, 4 + k_idx % 2, dslice))
                if len(pend) > 3:
                    emit_rope(*pend.pop(0))
        while pend:
            emit_rope(*pend.pop(0))

    def v_proj(name, srcT, src_bufs, load_w):
        if load_w:
            wview = WA[:, 4096:4096 + ICH * 512].rearrange("p (c f) -> p c f", f=512)
            dma("gpsimd", wview, d_wvT.ap().rearrange("(c p) f -> p c f", p=128),
                writes=wa_bufs(4096, 4096 + ICH * 512), tag="%swv" % name)
        for ct in range(TCH):
            bank = ct % 2

            def mm(e, ct=ct, bank=bank, srcT=srcT):
                ins = None
                for ic in range(ICH):
                    ins = e.matmul(PS[bank][:, :],
                                   srcT[:, ic * NT + ct * 128: ic * NT + ct * 128 + 128],
                                   WA[:, 4096 + ic * 512: 4096 + (ic + 1) * 512],
                                   start=(ic == 0), stop=(ic == ICH - 1))
                return ins
            p.add("tensor", mm, reads=wa_bufs(4096, 4096 + ICH * 512) + src_bufs,
                  writes=[b_ps[bank]], tag="%svmm%d" % (name, ct))
            dap = bass.AP(VT, ct * KVH * 129, [[VT.shape[1], 128], [129, KVH], [1, 128]])
            p.add("vector", lambda e, a=dap, bank=bank: e.tensor_copy(a, PS[bank][:, :]),
                  reads=[b_ps[bank]], writes=[b_vt], tag="%svev%d" % (name, ct))

    state_banks = [1, 4, 2, 3]

    def state_ktok(name, c):
        def tp(e, c=c):
            ins = None
            for g in range(KVH):
                ins = e.transpose(PT[:, g * 128:(g + 1) * 128],
                                  KF[:, g * NT + c * 128: g * NT + c * 128 + 128], IDB[:, :])
            return ins
        p.add("tensor", tp, reads=b_kf_h + [b_const], writes=[b_pt], tag="%sktp%d" % (name, c))
        p.add("scalar", lambda e: e.copy(KTOK[:, :], PT[:, 0:KVH * 128]),
              reads=[b_pt], writes=[b_ktok], tag="%sktev%d" % (name, c))

    def state_update(name, c, stcp=False):
        for g in range(KVH):
            bank = state_banks[g]
            p.add("tensor", lambda e, g=g, c=c, bank=bank: e.matmul(
                PS[bank][:, 0:129], KTOK[:, g * 128:(g + 1) * 128],
                VT[:, c * KVH * 129 + g * 129: c * KVH * 129 + (g + 1) * 129],
                start=True, stop=True),
                reads=[b_ktok, b_vt], writes=[b_ps[bank]], tag="%ssmm%d_%d" % (name, c, g))
            p.add("vector", lambda e, g=g, bank=bank: e.tensor_tensor(
                STS32[:, g * 129:(g + 1) * 129], STS32[:, g * 129:(g + 1) * 129],
                PS[bank][:, 0:129], ALU.add),
                reads=[b_ps[bank], b_sts32[g]], writes=[b_sts32[g]],
                tag="%ssacc%d_%d" % (name, c, g))
            if stcp:
                p.add("scalar", lambda e, g=g: e.copy(
                    STS[:, g * 129:(g + 1) * 129], STS32[:, g * 129:(g + 1) * 129]),
                    reads=[b_sts32[g]], writes=[b_sts], tag="%sstcp%d_%d" % (name, c, g))

    WA32 = WA.bitcast(F32)
    main_norms = []

    def emit_main_norm(ct):
        ln_norm("m_%d" % ct, 8 + ct, OA[:, ct * DM:(ct + 1) * DM], [b_oa[ct]], ct % 2,
                XN, [b_xn] * 16, keep32=True)
        oslot = 1 - ct % 2

        def rmm32(e, oslot=oslot):
            ins = None
            for ic in range(ICH):
                ins = e.matmul(PS[4][:, 0:NE],
                               U[:, oslot * 2048 + ic * 128: oslot * 2048 + (ic + 1) * 128],
                               RT32[:, ic * NE:(ic + 1) * NE], start=(ic == 0), stop=(ic == ICH - 1))
            return ins
        p.add("tensor", rmm32, reads=u_bufs(oslot * 8192, (oslot + 1) * 8192) + [b_rt],
              writes=[b_ps[4]], tag="rmm%d" % ct)

        p.add("vector", lambda e, ct=ct: e.tensor_copy(LG[:, ct * NE:(ct + 1) * NE],
                                                       PS[4][:, 0:NE]),
              reads=[b_ps[4]], writes=[b_lg], tag="lgstash%d" % ct)

    def xpre_dma(pc):
        xsl = pc % 3
        dma("scalar", WA32[:, xsl * 2048:(xsl + 1) * 2048], d_xp[pc * 128:(pc + 1) * 128, :],
            writes=wa_bufs(xsl * 4096, (xsl + 1) * 4096), tag="xpre%d" % pc)
    xpre_dma(0)
    xpre_dma(1)
    for pc in range(TCH + 2):
        if pc >= 2:
            qc = pc - 2
            xsl = qc % 3
            xs_ap = WA32[:, xsl * 2048:(xsl + 1) * 2048]
            ln_norm("p_%d" % qc, qc, xs_ap, wa_bufs(xsl * 4096, (xsl + 1) * 4096),
                    0, A1, b_a1, xs=xs_ap, xr_b=wa_bufs(xsl * 4096, (xsl + 1) * 4096))
            emit_main_norm(qc)
        if 1 <= pc and pc + 1 < TCH:
            xpre_dma(pc + 1)
        if pc < TCH:
            xsl = pc % 3
            ln_statsA("p_%d" % pc, pc, WA32[:, xsl * 2048:(xsl + 1) * 2048],
                      wa_bufs(xsl * 4096, (xsl + 1) * 4096), sum_eng="scalar")
            ln_statsA("m_%d" % pc, 8 + pc, OA[:, pc * DM:(pc + 1) * DM], [b_oa[pc]])
        if 1 <= pc <= TCH:
            ln_statsB("p_%d" % (pc - 1), pc - 1)
            ln_statsB("m_%d" % (pc - 1), 8 + pc - 1)

    def g3(t, col):
        return bass.AP(t, col, [[t.shape[1], 128], [NE, TCH], [1, NE]])

    def gbc(t, col):
        return bass.AP(t, col, [[t.shape[1], 128], [1, TCH], [0, NE]])

    def top2all(e):
        EQ1 = TOP[:, 0:64]
        L2 = TOP[:, 64:128]
        T1 = TOP[:, 128:192]
        M1, M2, D, E, W1, W2 = 0, 8, 16, 24, 0, 8
        e.reduce_max(T2[:, 0:8], g3(LG, 0), axis=AX.X)
        e.drain()
        e.tensor_tensor(EQ1, LG[:, :], gbc(T2, 0), ALU.is_equal)
        e.drain()
        e.scalar_tensor_tensor(L2, EQ1, -1e30, LG[:, :], ALU.mult, ALU.add)
        e.drain()
        e.reduce_max(T2[:, 8:16], g3(TOP, 64), axis=AX.X)
        e.drain()
        e.tensor_tensor(L2, L2, gbc(T2, 8), ALU.is_equal)
        e.drain()
        return e.tensor_tensor(T2[:, 16:24], T2[:, 8:16], T2[:, 0:8], ALU.subtract)
    p.add("vector", top2all, reads=[b_lg], writes=[b_top, b_t2], tag="top2all_a")
    p.add("scalar", lambda e: e.activation(T2[:, 24:32], T2[:, 16:24], AF.Exp),
          reads=[b_t2], writes=[b_t2], tag="top2all_e")

    def top2allc(e):
        EQ1 = TOP[:, 0:64]
        EQ2 = TOP[:, 64:128]
        T1 = TOP[:, 128:192]
        e.tensor_scalar(T2[:, 16:24], T2[:, 24:32], 1.0, None, ALU.add)
        e.drain()
        e.reciprocal(T2[:, 16:24], T2[:, 16:24])
        e.drain()
        e.tensor_tensor(T2[:, 24:32], T2[:, 24:32], T2[:, 16:24], ALU.mult)
        e.drain()
        e.tensor_tensor(T1, EQ1, gbc(T2, 16), ALU.mult)
        e.tensor_tensor(EQ2, EQ2, gbc(T2, 24), ALU.mult)
        e.drain()
        return e.tensor_tensor(CMB[:, :], T1, EQ2, ALU.add)
    p.add("vector", top2allc, reads=[b_t2, b_top], writes=[b_cmb, b_top], tag="top2all_c")

    def drip_main_stats(oc):
        for ct in (2 * oc, 2 * oc + 1):
            if ct < TCH:
                ln_stats("m_%d" % ct, 8 + ct, OA[:, ct * DM:(ct + 1) * DM], [b_oa[ct]])
    dma("gpsimd", CSB[:, 0:NT], d_cosp.ap(), writes=[b_csb], tag="cosp")
    dma("gpsimd", CSB[:, NT:2 * NT], d_sinp.ap(), writes=[b_csb], tag="sinp")
    all_a1 = list(b_a1)
    kq_proj("pk", d_wkT.ap(), KVH, A1, all_a1, KF, b_kf_h, pm=True)
    v_proj("pv", A1, all_a1, load_w=True)
    for c in range(TCH):
        state_ktok("p", c)
        state_update("p", c)
    if debug:
        dma("sync", dbg["pkf"].ap(), KF[:, :], reads=b_kf_h, tag="dbgpkf")
        dma("sync", dbg["pvt"].ap(), VT[:, :], reads=[b_vt], tag="dbgpvt")
        dma("sync", dbg["pst"].ap(), STS32[:, :], reads=list(b_sts32), tag="dbgpst")

    if debug:
        for blk in range(ICH):
            dma("sync", dbg["xn"].ap()[:, blk * NT:(blk + 1) * NT], XN[:, blk * NT:(blk + 1) * NT],
                reads=[b_xn], tag="dbgxn%d" % blk)

    dma("gpsimd", CSB[:, 0:NT], d_cos.ap(), writes=[b_csb], tag="cosm")
    dma("gpsimd", CSB[:, NT:2 * NT], d_sin.ap(), writes=[b_csb], tag="sinm")

    kq_proj("q", d_wqT.ap(), H, XN, [b_xn], A1, b_a1, pm=False)
    kq_proj("k", d_wkT.ap(), KVH, XN, [b_xn], KF, b_kf_h, pm=False)
    v_proj("v", XN, [b_xn], load_w=False)

    if debug:
        for g in range(KVH):
            dma("sync", dbg["kf"].ap()[:, g * NT:(g + 1) * NT], KF[:, g * NT:(g + 1) * NT],
                reads=[b_kf_h[g]], tag="dbgkf%d" % g)
        for h in range(H):
            dma("sync", dbg["qf"].ap()[:, h * NT:(h + 1) * NT], A1[:, h * NT:(h + 1) * NT],
                reads=[b_a1[h]], tag="dbgqf%d" % h)
        dma("sync", dbg["vt"].ap()[:, :], VT[:, :], reads=[b_vt], tag="dbgvt")

    for c in range(TCH):
        if c == 0:
            p.add("scalar", lambda e: e.copy(STS[:, :], STS32[:, :]),
                  reads=list(b_sts32), writes=[b_sts], tag="stcp%d" % c)
        numrot5 = [1, 4, 2, 3, 5]

        def emit_abar(h):
            g = h // KVH
            abank = 0 if h % 2 == 0 else 6
            amap = AM[:, (h % 2) * 128:(h % 2) * 128 + 128]
            p.add("tensor", lambda e, g=g, h=h, c=c, ab=abank: e.matmul(
                PS[ab][:, 0:128], KF[:, g * NT + c * 128: g * NT + c * 128 + 128],
                A1[:, h * NT + c * 128: h * NT + c * 128 + 128], start=True, stop=True),
                reads=[b_kf_h[g], b_a1[h]], writes=[b_ps[abank]], tag="abar%d_%d" % (c, h))
            p.add("vector", lambda e, a=amap, ab=abank: e.tensor_tensor(
                a, PS[ab][:, 0:128], MASK[:, :], ALU.mult),
                reads=[b_ps[abank], b_const], writes=[b_am[h % 2]], tag="amask%d_%d" % (c, h))

        emit_abar(0)
        for hb in range(4):
            for j in range(4):
                h = hb * 4 + j
                g = h // KVH
                amb = b_am[h % 2]
                amap = AM[:, (h % 2) * 128:(h % 2) * 128 + 128]
                nbank = numrot5[h % 5]
                if h + 1 < H:
                    emit_abar(h + 1)

                def nmm(e, g=g, h=h, c=c, a=amap, nbank=nbank):
                    e.matmul(PS[nbank][:, 0:129], a,
                             VT[:, c * KVH * 129 + g * 129: c * KVH * 129 + (g + 1) * 129],
                             start=True, stop=False)
                    return e.matmul(PS[nbank][:, 0:129],
                                    A1[:, h * NT + c * 128: h * NT + c * 128 + 128],
                                    STS[:, g * 129:(g + 1) * 129], start=False, stop=True)
                p.add("tensor", nmm, reads=[amb, b_vt, b_a1[h], b_sts], writes=[b_ps[nbank]],
                      tag="nmm%d_%d" % (c, h))
                p.add("vector", lambda e, j=j, nbank=nbank: e.tensor_scalar(
                    DEN[:, j:j + 1], PS[nbank][:, 128:129], ATT_EPS, None, ALU.add),
                    reads=[b_ps[nbank]], writes=[b_den[j]], tag="dstash%d_%d" % (c, h))
            p.add("vector", lambda e: e.reciprocal(DEN[:, 4:8], DEN[:, 0:4]),
                  reads=list(b_den), writes=[b_denr], tag="drecip%d_%d" % (c, hb))
            for j in range(4):
                h = hb * 4 + j
                nbank = numrot5[h % 5]
                p.add("scalar", lambda e, h=h, c=c, j=j, nbank=nbank: e.activation(
                    A1[:, h * NT + c * 128: h * NT + c * 128 + 128], PS[nbank][:, 0:128],
                    AF.Copy, scale=DEN[:, 4 + j:5 + j]),
                    reads=[b_ps[nbank], b_denr], writes=[b_a1[h]], tag="oev%d_%d" % (c, h))
        state_ktok("m", c)
        state_update("m", c, stcp=(c < TCH - 1))
        for hh in range(2):
            def tp(e, c=c, hh=hh):
                ins = None
                for k in range(8):
                    h = hh * 8 + k
                    ins = e.transpose(PT[:, k * 128:(k + 1) * 128],
                                      A1[:, h * NT + c * 128: h * NT + c * 128 + 128], IDB[:, :])
                return ins
            p.add("tensor", tp, reads=[b_a1[hh * 8 + k2] for k2 in range(8)] + [b_const],
                  writes=[b_pt], tag="flip%d_%d" % (c, hh))
            dap = bass.AP(A1, (hh * 8) * NT + c * 128, [[A1.shape[1], 128], [NT, 8], [1, 128]])
            if hh == 0:
                p.add("vector", lambda e, a=dap: e.tensor_copy(a, PT[:, :]),
                      reads=[b_pt], writes=[b_a1[hh * 8 + k2] for k2 in range(8)],
                      tag="flipev%d_%d" % (c, hh))
            else:
                p.add("scalar", lambda e, a=dap: e.copy(a, PT[:, :]),
                      reads=[b_pt], writes=[b_a1[hh * 8 + k2] for k2 in range(8)],
                      tag="flipev%d_%d" % (c, hh))

    if debug:
        for h in range(H):
            dma("sync", dbg["ao"].ap()[:, h * NT:(h + 1) * NT], A1[:, h * NT:(h + 1) * NT],
                reads=[b_a1[h]], tag="dbgao%d" % h)

    for fq in range(4):
        if fq % 2 == 0:
            sl_t, sl_bufs = UB, u_bufs(0, 16384)
        else:
            sl_t, sl_bufs = WA, wa_bufs(0, 8192)
        for qq in range(4):
            dst = sl_t[:, qq * 2048:(qq + 1) * 2048].rearrange("p (c f) -> p c f", f=512)
            src = d_woT[qq * 512:(qq + 1) * 512, fq * 512:(fq + 1) * 512].rearrange(
                "(c p) f -> p c f", p=128)
            dma("gpsimd", dst, src, writes=sl_bufs, tag="wo%d_%d" % (fq, qq))
        for ctg in range(4):
            cts = [ctg * 2, ctg * 2 + 1]
            pair = [[2, 3], [5, 6]][ctg % 2]
            for cc in range(16):
                def mm(e, cc=cc, cts=cts, sl_t=sl_t, pair=pair):
                    ins = None
                    for k, ct in enumerate(cts):
                        ins = e.matmul(PS[pair[k]][:, :],
                                       A1[:, cc * NT + ct * 128: cc * NT + ct * 128 + 128],
                                       sl_t[:, cc * 512:(cc + 1) * 512],
                                       start=(cc == 0), stop=(cc == 15))
                    return ins
                p.add("tensor", mm, reads=sl_bufs + [b_a1[cc]],
                      writes=[b_ps[k] for k in pair], tag="womm%d_%d_%d" % (fq, ctg, cc))
            for k, ct in enumerate(cts):
                p.add("vector", lambda e, k=k, ct=ct, fq=fq, pair=pair: e.tensor_tensor(
                    OA[:, ct * DM + fq * 512: ct * DM + (fq + 1) * 512],
                    OA[:, ct * DM + fq * 512: ct * DM + (fq + 1) * 512],
                    PS[pair[k]][:, :], ALU.add),
                    reads=[b_ps[pair[k]], b_oa[ct]], writes=[b_oa[ct]],
                    tag="woev%d_%d_%d" % (fq, ctg, ct))

    if debug:
        for ct in range(TCH):
            dma("sync", dbg["oa1"].ap()[ct * 128:(ct + 1) * 128, :], OA[:, ct * DM:(ct + 1) * DM],
                reads=[b_oa[ct]], tag="dbgoa1_%d" % ct)

    for ct in range(TCH):
        def gmm(e, ct=ct):
            ins = None
            for ic in range(ICH):
                ins = e.matmul(PS[4][:, 0:1],
                               XN[:, ic * NT + ct * 128: ic * NT + ct * 128 + 128],
                               SGT[:, ic:ic + 1], start=(ic == 0), stop=(ic == ICH - 1))
            return ins
        p.add("tensor", gmm, reads=[b_xn, b_rt], writes=[b_ps[4]], tag="gmm%d" % ct)
        p.add("scalar", lambda e, ct=ct: e.activation(SIG[:, ct:ct + 1], PS[4][:, 0:1], AF.Sigmoid),
              reads=[b_ps[4]], writes=[b_cmb], tag="sigev%d" % ct)

    if debug:
        dma("sync", dbg["cmb"].ap(), CMB[:, :], reads=[b_cmb], tag="dbgcmb")
        dma("sync", dbg["sig"].ap(), SIG[:, :], reads=[b_cmb], tag="dbgsig")

    def unit_srcs(u):
        if u < NE:
            return d_wg.ap()[u], d_wu.ap()[u], d_wd.ap()[u]
        blk = u - NE
        return (d_swg.ap()[:, blk * IFF:(blk + 1) * IFF],
                d_swu.ap()[:, blk * IFF:(blk + 1) * IFF],
                d_swd.ap()[blk * IFF:(blk + 1) * IFF, :])

    gu_dma_ops = {}
    gu_last_mm = {}

    def emit_gu_dmas(u, lo=0, hi=IFCH):
        wg_src, wu_src, _ = unit_srcs(u)
        if u not in gu_dma_ops:
            gu_dma_ops[u] = {}
        for i in range(lo, hi):
            gbase = (i % 2) * 2048
            ubase = 4096 + (i % 2) * 2048
            wr_g = wa_bufs(gbase, gbase + 2048) if i < 2 else []
            wr_u = wa_bufs(ubase, ubase + 2048) if i < 2 else []
            dep_g = [gu_last_mm[("g", i % 2)]] if (i < 2 and ("g", i % 2) in gu_last_mm) else []
            dep_u = [gu_last_mm[("u", i % 2)]] if (i < 2 and ("u", i % 2) in gu_last_mm) else []
            og = dma("gpsimd", WA[:, gbase:gbase + 2048].rearrange("p (c f) -> p c f", f=128),
                     wg_src[:, i * 128:(i + 1) * 128].rearrange("(c p) f -> p c f", p=128),
                     writes=wr_g, deps=dep_g, tag="u%dgw%d" % (u, i))
            ou = dma("gpsimd", WA[:, ubase:ubase + 2048].rearrange("p (c f) -> p c f", f=128),
                     wu_src[:, i * 128:(i + 1) * 128].rearrange("(c p) f -> p c f", p=128),
                     writes=wr_u, deps=dep_u, tag="u%duw%d" % (u, i))
            gu_dma_ops[u][("g", i)] = og
            gu_dma_ops[u][("u", i)] = ou

    down_mm_last = {0: None, 1: None}
    dw_ops = {}
    dw_pending = {}

    def emit_down_dma(u, ft):
        _, _, wd_src = unit_srcs(u)
        dsl = ft % 2
        dview = UB[:, dsl * 5632:(dsl + 1) * 5632].rearrange("p (c f) -> p c f", f=512)
        wr = u_bufs(dsl * 11264, (dsl + 1) * 11264) if (u == 0 and ft < 2) else []
        op = dma("gpsimd", dview, wd_src[:, ft * 512:(ft + 1) * 512].rearrange(
            "(c p) f -> p c f", p=128),
            writes=wr, tag="u%ddw%d" % (u, ft))
        dw_ops[(u, ft)] = op
        dw_pending.setdefault((u, ft) if ft >= 2 else ("n", u, dsl), op)
        return op

    out_ops = []
    emit_gu_dmas(0)
    emit_down_dma(0, 0)
    emit_down_dma(0, 1)
    for u in range(NUNIT):
        pool = [0, 1, 4]
        for i in range(IFCH):
            gbase = (i % 2) * 2048
            ubase = 4096 + (i % 2) * 2048
            for tt in range(2):
                k = i * 2 + tt
                bg = pool[(2 * k) % 3]
                bu = pool[(2 * k + 1) % 3]

                def gm(e, tt=tt, gbase=gbase, bg=bg):
                    ins = None
                    for ic in range(ICH):
                        ins = e.matmul(PS[bg][:, :],
                                       WA[:, gbase + ic * 128: gbase + (ic + 1) * 128],
                                       XN[:, ic * NT + tt * 512: ic * NT + tt * 512 + 512],
                                       start=(ic == 0), stop=(ic == ICH - 1))
                    return ins
                op_gm = p.add("tensor", gm, reads=[b_xn], writes=[b_ps[bg]],
                              deps=[gu_dma_ops[u][("g", i)]], tag="u%dgm%d_%d" % (u, i, tt))
                if ("g", i + 2) in gu_dma_ops[u]:
                    gu_dma_ops[u][("g", i + 2)].deps.append(op_gm)
                gu_last_mm[("g", i % 2)] = op_gm

                def um(e, tt=tt, ubase=ubase, bu=bu):
                    ins = None
                    for ic in range(ICH):
                        ins = e.matmul(PS[bu][:, :],
                                       WA[:, ubase + ic * 128: ubase + (ic + 1) * 128],
                                       XN[:, ic * NT + tt * 512: ic * NT + tt * 512 + 512],
                                       start=(ic == 0), stop=(ic == ICH - 1))
                    return ins
                op_um = p.add("tensor", um, reads=[b_xn], writes=[b_ps[bu]],
                              deps=[gu_dma_ops[u][("u", i)]], tag="u%dum%d_%d" % (u, i, tt))
                if ("u", i + 2) in gu_dma_ops[u]:
                    gu_dma_ops[u][("u", i + 2)].deps.append(op_um)
                gu_last_mm[("u", i % 2)] = op_um
                gsl = k % 2
                p.add("scalar", lambda e, gsl=gsl, bg=bg: e.activation(
                    GS[:, gsl * 512:(gsl + 1) * 512], PS[bg][:, :], AF.Silu),
                    reads=[b_ps[bg]], writes=[b_gs[gsl]], tag="u%dsil%d_%d" % (u, i, tt))
                hslice = A1[:, i * NT + tt * 512: i * NT + tt * 512 + 512]
                p.add("vector", lambda e, h2=hslice, gsl=gsl, bu=bu: e.tensor_tensor(
                    h2, GS[:, gsl * 512:(gsl + 1) * 512], PS[bu][:, :], ALU.mult),
                    reads=[b_gs[gsl], b_ps[bu]], writes=[b_a1[i]], tag="u%dh%d_%d" % (u, i, tt))
        if u + 1 < NUNIT:
            emit_gu_dmas(u + 1, 0, 2)
        dw23 = [emit_down_dma(u, 2), emit_down_dma(u, 3)]
        if u + 1 < NUNIT:
            dw01n = [emit_down_dma(u + 1, 0), emit_down_dma(u + 1, 1)]
        else:
            dw01n = []
        if u + 1 < NUNIT:
            emit_gu_dmas(u + 1, 2, IFCH)
        for ft in range(4):
            dsl = ft % 2
            dbase = dsl * 5632
            for ctg in range(4):
                cts = [ctg * 2, ctg * 2 + 1]
                pair = [[2, 3], [5, 6]][ctg % 2]

                def dmm(e, cts=cts, dbase=dbase, pair=pair):
                    ins = None
                    for i in range(IFCH):
                        for kk, ct in enumerate(cts):
                            ins = e.matmul(PS[pair[kk]][:, :],
                                           A1[:, i * NT + ct * 128: i * NT + ct * 128 + 128],
                                           UB[:, dbase + i * 512: dbase + (i + 1) * 512],
                                           start=(i == 0), stop=(i == IFCH - 1))
                    return ins
                op_dmm = p.add("tensor", dmm, reads=[b_a1[i2] for i2 in range(IFCH)],
                               writes=[b_ps[k] for k in pair],
                               deps=[dw_ops[(u, ft)]],
                               tag="u%ddm%d_%d" % (u, ft, ctg))
                if ft < 2:
                    dw23[ft].deps.append(op_dmm)
                elif dw01n:
                    dw01n[ft - 2].deps.append(op_dmm)
                down_mm_last[ft % 2] = op_dmm
                for kk, ct in enumerate(cts):
                    oaslice = OA[:, ct * DM + ft * 512: ct * DM + (ft + 1) * 512]
                    sap = CMB[:, ct * NE + u: ct * NE + u + 1] if u < NE else SIG[:, ct:ct + 1]
                    p.add("vector", lambda e, o=oaslice, kk=kk, sap=sap, pair=pair:
                          e.scalar_tensor_tensor(o, PS[pair[kk]][:, :], sap, o,
                                                 ALU.mult, ALU.add),
                          reads=[b_ps[pair[kk]], b_cmb, b_oa[ct]], writes=[b_oa[ct]],
                          tag="u%ddev%d_%d" % (u, ft, ct))
        if debug and u == 0:
            for i2 in range(IFCH):
                dma("sync", dbg["h0"].ap()[:, i2 * NT:(i2 + 1) * NT], A1[:, i2 * NT:(i2 + 1) * NT],
                    reads=[b_a1[i2]], tag="dbgh0_%d" % i2)
            _dmp = dma("sync", dbg["dsl0"].ap(), UB[:, 0:IFCH * 512],
                       deps=[dw_ops[(0, 2)], down_mm_last[0]], tag="dbgdsl0")
            if (1, 0) in dw_ops:
                dw_ops[(1, 0)].deps.append(_dmp)
            for ct in range(TCH):
                dma("sync", dbg["oau0"].ap()[ct * 128:(ct + 1) * 128, :],
                    OA[:, ct * DM:(ct + 1) * DM], reads=[b_oa[ct]], tag="dbgoau0_%d" % ct)
        if u == NUNIT - 1:
            for ct in range(TCH):
                out_ops.append(dma("sync", d_out[ct * 128:(ct + 1) * 128, :],
                                   OA[:, ct * DM:(ct + 1) * DM],
                                   reads=[b_oa[ct]], tag="outst%d" % ct))

    p.add("scalar", lambda e: e.memzero(STAT[:, 0:1]), deps=out_ops, tag="finwait")

    import contextlib
    with contextlib.ExitStack() as st:
        sems = {}
        for e in ENGS:
            sems["eng_" + e] = st.enter_context(nc.semaphore("s_" + e))
        for q in ("gpsimd", "sync", "scalar"):
            sems["dma_" + q] = [st.enter_context(nc.semaphore("sd_%s_%d" % (q, i)))
                                for i in range(NSEM[q])]
        block = st.enter_context(nc.Block())
        p.finalize_and_emit(block, sems)
    return nc


_cached = {}


def _get_nc(debug=False):
    if debug not in _cached:
        _cached[debug] = build_program(debug)
    return _cached[debug]


def _rope_tables():
    inv_freq = 1.0 / ROPE_BASE ** (np.arange(0, HD, 2, dtype=np.float32) / HD)
    t = np.arange(S, dtype=np.float32)
    freqs = np.outer(t, inv_freq).astype(np.float32)
    emb = np.concatenate([freqs, freqs], -1)
    return np.cos(emb).astype(np.float32), np.sin(emb).astype(np.float32)


def make_in_maps(inputs):
    x = np.asarray(inputs["L0_attn_x"], dtype=np.float32)
    wq = np.asarray(inputs["L0_attn_Wq"], dtype=np.float32)
    wk = np.asarray(inputs["L0_attn_Wk"], dtype=np.float32)
    wv = np.asarray(inputs["L0_attn_Wv"], dtype=np.float32)
    wo = np.asarray(inputs["L0_attn_Wo"], dtype=np.float32)
    rw = np.asarray(inputs["L0_ffn_router_w"], dtype=np.float32)
    wg = np.ascontiguousarray(np.asarray(inputs["L0_ffn_w_gate"], dtype=np.float32))
    wu = np.ascontiguousarray(np.asarray(inputs["L0_ffn_w_up"], dtype=np.float32))
    wd = np.ascontiguousarray(np.asarray(inputs["L0_ffn_w_down"], dtype=np.float32))
    swg = np.ascontiguousarray(np.asarray(inputs["L0_ffn_sw_gate"], dtype=np.float32))
    swu = np.ascontiguousarray(np.asarray(inputs["L0_ffn_sw_up"], dtype=np.float32))
    swd = np.ascontiguousarray(np.asarray(inputs["L0_ffn_sw_down"], dtype=np.float32))
    sg = np.asarray(inputs["L0_ffn_shared_gate_w"], dtype=np.float32)

    wqT = np.ascontiguousarray(wq.T)
    wkT = np.ascontiguousarray(wk.T)
    wvT = np.ascontiguousarray(wv.T)
    woT = np.ascontiguousarray(wo.T)
    rtT = np.ascontiguousarray(rw.reshape(NE, ICH, 128).transpose(2, 1, 0).reshape(128, ICH * NE))
    sgT = np.ascontiguousarray(sg.reshape(ICH, 128).T)

    cos, sin = _rope_tables()
    cosT = np.ascontiguousarray(cos.T)
    sinT = np.ascontiguousarray(sin.T)

    j = np.arange(128)
    maskJI = (j[:, None] <= j[None, :]).astype(np.float32)
    id32 = np.eye(128, dtype=np.float32)
    idb = np.eye(128, dtype=np.float32)
    R = np.zeros((128, 128), dtype=np.float32)
    for dp in range(64):
        R[dp + 64, dp] = -1.0
    for dp in range(64, 128):
        R[dp - 64, dp] = 1.0
    rmat = R
    onesb = np.ones((128, 1), dtype=np.float32)

    in_maps = []
    for core in range(8):
        b = core // 2
        hh = core % 2
        in_maps.append({
            "x_main": np.ascontiguousarray(x[b, hh * NT:(hh + 1) * NT, :]),
            "x_prefix": np.ascontiguousarray(x[b, 0:NT, :]),
            "wqT": wqT, "wkT": wkT, "wvT": wvT, "woT": woT, "rtT": rtT,
            "wg": wg, "wu": wu, "wd": wd, "swg": swg, "swu": swu, "swd": swd,
            "sgT": sgT,
            "cosT": np.ascontiguousarray(cosT[:, hh * NT:(hh + 1) * NT]),
            "sinT": np.ascontiguousarray(sinT[:, hh * NT:(hh + 1) * NT]),
            "cosTp": np.ascontiguousarray(cosT[:, 0:NT]),
            "sinTp": np.ascontiguousarray(sinT[:, 0:NT]),
            "maskJI": maskJI, "id32": id32, "idb": idb, "rmat": rmat,
            "onesb": onesb,
            "pmask": np.full((128, 1), float(hh), dtype=np.float32),
        })
    return in_maps


def kernel(**inputs):
    nc = _get_nc(False)
    in_maps = make_in_maps(inputs)
    res = run_bass_kernel_spmd(nc, in_maps, core_ids=list(range(8)))
    out = np.empty((B, S, DM), dtype=np.float32)
    for core in range(8):
        b = core // 2
        hh = core % 2
        out[b, hh * NT:(hh + 1) * NT, :] = res.results[core]["out"]
    return out


# version 23
# speedup vs baseline: 1.1017x; 1.0000x over previous
import numpy as np
import ml_dtypes
import concourse.bass as bass
import concourse.mybir as mybir
from concourse.bass_utils import run_bass_kernel_spmd

F32 = mybir.dt.float32
BF16 = mybir.dt.bfloat16
AF = mybir.ActivationFunctionType
ALU = mybir.AluOpType
AX = mybir.AxisListType

B, S, DM = 4, 2048, 2048
H, KVH, HD = 16, 4, 128
NT = 1024
TCH = NT // 128
ICH = DM // 128
NE, IFF, SI = 8, 1408, 5632
IFCH = IFF // 128
NUNIT = 12
LN_EPS = 1e-5
ATT_EPS = 1e-6
ROPE_BASE = 10000.0

ENGS = ("tensor", "vector", "scalar", "gpsimd", "sync")
NSEM = {"gpsimd": 10, "sync": 6, "scalar": 4}


class Op:
    __slots__ = ("eng", "emit", "deps", "kind", "seq", "observed", "qidx", "tag", "eidx")

    def __init__(self, eng, emit, deps, kind, tag=""):
        self.eng = eng
        self.emit = emit
        self.deps = deps
        self.kind = kind
        self.seq = None
        self.observed = False
        self.qidx = None
        self.tag = tag


class Buf:
    __slots__ = ("writer", "readers", "name", "psum")

    def __init__(self, name="", psum=False):
        self.writer = None
        self.readers = []
        self.name = name
        self.psum = psum


class Prog:
    def __init__(self):
        self.ops = []
        self.dma_counts = {"gpsimd": 0, "sync": 0, "scalar": 0}

    def add(self, eng, emit, reads=(), writes=(), deps=(), kind="compute", tag=""):
        dd = list(deps)
        for b in reads:
            if b.writer is not None:
                dd.append(b.writer)
            if b.psum:
                dd.extend(r for r in b.readers if r.eng != eng)
        for b in writes:
            if b.writer is not None:
                dd.append(b.writer)
            dd.extend(b.readers)
        op = Op(eng, emit, dd, kind, tag)
        if kind == "dma":
            i = self.dma_counts[eng]
            self.dma_counts[eng] += 1
            op.qidx = (i % NSEM[eng], i // NSEM[eng])
        self.ops.append(op)
        for b in reads:
            b.readers.append(op)
        for b in writes:
            b.writer = op
            b.readers = []
        return op

    def finalize_and_emit(self, block, sems):
        for op in self.ops:
            for d in op.deps:
                if d.kind != "dma" and d.eng != op.eng:
                    d.observed = True
        cnt = {e: 0 for e in ENGS}
        for op in self.ops:
            if op.kind == "compute" and op.observed:
                cnt[op.eng] += 1
                op.seq = cnt[op.eng]
        eidx_cnt = {e: 0 for e in ENGS}
        for op in self.ops:
            op.eidx = eidx_cnt[op.eng]
            eidx_cnt[op.eng] += 1
        per_eng = {e: [] for e in ENGS}
        for op in self.ops:
            waits = {}
            if op.kind == "dma" and op.qidx[1] > 0:
                waits[("dma", op.eng, op.qidx[0])] = 16 * op.qidx[1]
            for d in op.deps:
                if d.kind == "dma":
                    key, val = ("dma", d.eng, d.qidx[0]), 16 * (d.qidx[1] + 1)
                elif d.eng != op.eng:
                    key, val = ("eng", d.eng), d.seq
                else:
                    continue
                if waits.get(key, -1) < val:
                    waits[key] = val
            per_eng[op.eng].append((op, waits))

        def emit_engine(eng_obj, eng_name):
            hw = {}
            last_drain = -1
            for op, waits in per_eng[eng_name]:
                need_drain = eng_name != "tensor" and any(
                    d.eng == eng_name and d.kind == "compute"
                    and d.eidx > last_drain for d in op.deps)
                if need_drain:
                    eng_obj.drain()
                    last_drain = op.eidx - 1
                for key in sorted(waits, key=str):
                    val = waits[key]
                    if hw.get(key, 0) >= val:
                        continue
                    hw[key] = val
                    if key[0] == "dma":
                        sem = sems["dma_" + key[1]][key[2]]
                    else:
                        sem = sems["eng_" + key[1]]
                    eng_obj.wait_ge(sem, val)
                ins = op.emit(eng_obj)
                if op.kind == "dma":
                    ins.then_inc(sems["dma_" + op.eng][op.qidx[0]], 16)
                elif op.observed:
                    ins.then_inc(sems["eng_" + op.eng], 1)

        @block.tensor
        def _(t):
            emit_engine(t, "tensor")

        @block.vector
        def _(v):
            emit_engine(v, "vector")

        @block.scalar
        def _(s):
            emit_engine(s, "scalar")

        @block.gpsimd
        def _(g):
            emit_engine(g, "gpsimd")

        @block.sync
        def _(sy):
            emit_engine(sy, "sync")


def build_program(debug=False):
    nc = bass.Bass("TRN2", target_bir_lowering=False)

    d_x = nc.dram_tensor("x_main", [NT, DM], F32, kind="ExternalInput")
    d_xp = nc.dram_tensor("x_prefix", [NT, DM], F32, kind="ExternalInput")
    d_wqT = nc.dram_tensor("wqT", [DM, H * HD], F32, kind="ExternalInput")
    d_wkT = nc.dram_tensor("wkT", [DM, KVH * HD], F32, kind="ExternalInput")
    d_wvT = nc.dram_tensor("wvT", [DM, KVH * HD], F32, kind="ExternalInput")
    d_woT = nc.dram_tensor("woT", [H * HD, DM], F32, kind="ExternalInput")
    d_rtT = nc.dram_tensor("rtT", [128, ICH * NE], F32, kind="ExternalInput")
    d_wg = nc.dram_tensor("wg", [NE, DM, IFF], F32, kind="ExternalInput")
    d_wu = nc.dram_tensor("wu", [NE, DM, IFF], F32, kind="ExternalInput")
    d_wd = nc.dram_tensor("wd", [NE, IFF, DM], F32, kind="ExternalInput")
    d_swg = nc.dram_tensor("swg", [DM, SI], F32, kind="ExternalInput")
    d_swu = nc.dram_tensor("swu", [DM, SI], F32, kind="ExternalInput")
    d_swd = nc.dram_tensor("swd", [SI, DM], F32, kind="ExternalInput")
    d_sgT = nc.dram_tensor("sgT", [128, ICH], F32, kind="ExternalInput")
    d_cos = nc.dram_tensor("cosT", [128, NT], F32, kind="ExternalInput")
    d_sin = nc.dram_tensor("sinT", [128, NT], F32, kind="ExternalInput")
    d_cosp = nc.dram_tensor("cosTp", [128, NT], F32, kind="ExternalInput")
    d_sinp = nc.dram_tensor("sinTp", [128, NT], F32, kind="ExternalInput")
    d_mask = nc.dram_tensor("maskJI", [128, 128], F32, kind="ExternalInput")
    d_id32 = nc.dram_tensor("id32", [128, 128], F32, kind="ExternalInput")
    d_idb = nc.dram_tensor("idb", [128, 128], F32, kind="ExternalInput")
    d_rmat = nc.dram_tensor("rmat", [128, 128], F32, kind="ExternalInput")
    d_ones = nc.dram_tensor("onesb", [128, 1], F32, kind="ExternalInput")
    d_pmask = nc.dram_tensor("pmask", [128, 1], F32, kind="ExternalInput")
    d_out = nc.dram_tensor("out", [NT, DM], F32, kind="ExternalOutput")

    dbg = {}
    if debug:
        dbg["xn"] = nc.dram_tensor("dbg_xn", [128, ICH * NT], BF16, kind="ExternalOutput")
        dbg["qf"] = nc.dram_tensor("dbg_qf", [128, H * NT], BF16, kind="ExternalOutput")
        dbg["kf"] = nc.dram_tensor("dbg_kf", [128, KVH * NT], BF16, kind="ExternalOutput")
        dbg["vt"] = nc.dram_tensor("dbg_vt", [128, TCH * KVH * 129], BF16, kind="ExternalOutput")
        dbg["ao"] = nc.dram_tensor("dbg_ao", [128, H * NT], BF16, kind="ExternalOutput")
        dbg["cmb"] = nc.dram_tensor("dbg_cmb", [128, TCH * NE], F32, kind="ExternalOutput")
        dbg["sig"] = nc.dram_tensor("dbg_sig", [128, TCH], F32, kind="ExternalOutput")
        dbg["oa1"] = nc.dram_tensor("dbg_oa1", [NT, DM], F32, kind="ExternalOutput")
        dbg["pkf"] = nc.dram_tensor("dbg_pkf", [128, KVH * NT], BF16, kind="ExternalOutput")
        dbg["pvt"] = nc.dram_tensor("dbg_pvt", [128, TCH * KVH * 129], BF16, kind="ExternalOutput")
        dbg["pst"] = nc.dram_tensor("dbg_pst", [128, KVH * 129], F32, kind="ExternalOutput")
        dbg["h0"] = nc.dram_tensor("dbg_h0", [128, IFCH * NT], BF16, kind="ExternalOutput")
        dbg["dsl0"] = nc.dram_tensor("dbg_dsl0", [128, IFCH * 512], BF16, kind="ExternalOutput")
        dbg["oau0"] = nc.dram_tensor("dbg_oau0", [NT, DM], F32, kind="ExternalOutput")

    OA = nc.alloc_sbuf_tensor("OA", [128, TCH * DM], F32)
    XN = nc.alloc_sbuf_tensor("XN", [128, ICH * NT], BF16)
    A1 = nc.alloc_sbuf_tensor("A1", [128, 16 * NT], BF16)
    KF = nc.alloc_sbuf_tensor("KF", [128, KVH * NT], BF16)
    VT = nc.alloc_sbuf_tensor("VT", [128, TCH * KVH * 129], BF16)
    KTOK = nc.alloc_sbuf_tensor("KTOK", [128, KVH * 128], BF16)
    STS = nc.alloc_sbuf_tensor("STS", [128, KVH * 129], BF16)
    STS32 = nc.alloc_sbuf_tensor("STS32", [128, KVH * 129], F32)
    U = nc.alloc_sbuf_tensor("U", [128, 5632], F32)
    UB = U.bitcast(BF16)

    WA = nc.alloc_sbuf_tensor("WA", [128, 12288], BF16)
    GS = nc.alloc_sbuf_tensor("GS", [128, 2 * 512], F32)
    MASK = nc.alloc_sbuf_tensor("MASK", [128, 128], F32)
    ID32 = nc.alloc_sbuf_tensor("ID32", [128, 128], F32)
    IDB = nc.alloc_sbuf_tensor("IDB", [128, 128], BF16)
    RMAT = nc.alloc_sbuf_tensor("RMAT", [128, 128], BF16)
    ONESB = nc.alloc_sbuf_tensor("ONESB", [128, 1], BF16)
    PMASK = nc.alloc_sbuf_tensor("PMASK", [128, 1], F32)
    RT32 = nc.alloc_sbuf_tensor("RT32", [128, ICH * NE], F32)
    SGT = nc.alloc_sbuf_tensor("SGT", [128, ICH], BF16)
    CMB = nc.alloc_sbuf_tensor("CMB", [128, TCH * NE], F32)
    SIG = nc.alloc_sbuf_tensor("SIG", [128, TCH], F32)
    AM = nc.alloc_sbuf_tensor("AM", [128, 2 * 128], BF16)
    STAT = nc.alloc_sbuf_tensor("STAT", [128, 16 * 8], F32)
    T2 = nc.alloc_sbuf_tensor("T2", [128, 32], F32)
    DEN = nc.alloc_sbuf_tensor("DEN", [128, 16], F32)
    LG = nc.alloc_sbuf_tensor("LG", [128, TCH * NE], F32)
    TOP = nc.alloc_sbuf_tensor("TOP", [128, 192], F32)
    TRASH = nc.alloc_sbuf_tensor("TRASH", [128, 2048], BF16)
    CSB = TRASH

    PS = [nc.alloc_psum_tensor("ps%d" % i, [128, 512], F32) for i in range(7)]
    PT = nc.alloc_psum_tensor("pt", [128, 1024], BF16)

    p = Prog()

    b_oa = [Buf("oa%d" % i) for i in range(TCH)]
    b_xn = Buf("xn")
    b_a1 = [Buf("a1_%d" % h) for h in range(16)]
    b_kf_h = [Buf("kf%d" % g) for g in range(KVH)]
    b_vt = Buf("vt")
    b_ktok = Buf("ktok")
    b_sts = Buf("sts")
    b_sts32 = [Buf("sts32_%d" % g) for g in range(KVH)]
    b_gs = [Buf("gs0"), Buf("gs1"), Buf("gs2")]
    b_ps = [Buf("ps%d" % i, psum=True) for i in range(7)]
    b_pt = Buf("pt", psum=True)
    b_const = Buf("const")
    b_rt = Buf("rt")
    b_cmb = Buf("cmb")
    b_am = [Buf("am0"), Buf("am1")]
    b_stat = [Buf("stat%d" % i) for i in range(16)]
    b_stat2 = [Buf("statq%d" % i) for i in range(16)]
    b_t2 = Buf("t2")
    b_den = [Buf("den%d" % i) for i in range(4)]
    b_denr = Buf("denr")
    b_lg = Buf("lg")
    b_csb = b_trash = Buf("trash")
    b_top = Buf("top")
    b_wa = [Buf("wa%d" % i) for i in range(24)]
    b_u = [Buf("u%d" % i) for i in range(11)]

    def wa_bufs(c0, c1):
        return b_wa[c0 // 512: (c1 + 511) // 512]

    def u_bufs(byte0, byte1):
        return b_u[byte0 // 2048: (byte1 + 2047) // 2048]

    def bcast(t, col, n):
        return bass.AP(t, col, [[t.shape[1], 128], [0, n]])

    def dma(eng, dst_ap, src_ap, reads=(), writes=(), deps=(), tag=""):
        return p.add(eng, lambda e, d=dst_ap, s=src_ap: e.dma_start(d, s),
                     reads=reads, writes=writes, deps=deps, kind="dma", tag=tag)

    dma("sync", MASK[:, :], d_mask.ap(), writes=[b_const], tag="mask")
    dma("sync", ID32[:, :], d_id32.ap(), writes=[b_const], tag="id32")
    dma("gpsimd", IDB[:, :], d_idb.ap(), writes=[b_const], tag="idb")
    dma("gpsimd", RMAT[:, :], d_rmat.ap(), writes=[b_const], tag="rmat")
    dma("gpsimd", ONESB[:, :], d_ones.ap(), writes=[b_const], tag="ones")
    dma("sync", PMASK[:, :], d_pmask.ap(), writes=[b_const], tag="pmask")
    dma("sync", RT32[:, :], d_rtT.ap(), writes=[b_rt], tag="rt")
    dma("gpsimd", SGT[:, :], d_sgT.ap(), writes=[b_rt], tag="sgt")
    for ct in range(TCH):
        dma("sync", OA[:, ct * DM:(ct + 1) * DM], d_x[ct * 128:(ct + 1) * 128, :],
            writes=[b_oa[ct]], tag="xmain%d" % ct)

    ones_ap = bass.AP(VT, 128, [[TCH * KVH * 129, 128], [129, TCH * KVH], [1, 1]])
    p.add("vector", lambda e, a=ones_ap: e.memset(a, 1.0), writes=[b_vt], tag="vtones")
    p.add("vector", lambda e: e.memset(STS32[:, :], 0.0), writes=b_sts32, tag="sts0")

    def ln_statsA(uid, idx, src_ap, src_bufs, sum_eng="vector"):
        st = idx * 8
        sb = b_stat[idx]
        sb2 = b_stat2[idx]
        if sum_eng == "vector":
            p.add("vector", lambda e, s=src_ap, st=st: e.reduce_sum(
                STAT[:, st:st + 1], s, axis=AX.X),
                reads=src_bufs, writes=[sb], tag="lnsum%s" % uid)
        else:
            p.add("scalar", lambda e, s=src_ap, st=st: e.activation(
                TRASH[:, :], s, AF.Copy, accum_out=STAT[:, st:st + 1]),
                reads=src_bufs, writes=[sb, b_trash], tag="lnsum%s" % uid)
        p.add("scalar", lambda e, s=src_ap, st=st: e.activation(
            GS.bitcast(BF16)[:, :], s, AF.Square, accum_out=STAT[:, st + 1:st + 2]),
            reads=src_bufs, writes=[sb2] + b_gs, tag="lnsq%s" % uid)

    def ln_statsB(uid, idx):
        st = idx * 8
        sb = b_stat[idx]
        sb2 = b_stat2[idx]

        def stats(e, st=st):
            e.tensor_scalar(STAT[:, st + 2:st + 3], STAT[:, st:st + 1], 1.0 / DM, None, ALU.mult)
            e.tensor_scalar(STAT[:, st + 4:st + 5], STAT[:, st + 1:st + 2], 1.0 / DM, None, ALU.mult)
            e.drain()
            e.tensor_tensor(STAT[:, st + 3:st + 4], STAT[:, st + 2:st + 3],
                            STAT[:, st + 2:st + 3], ALU.mult)
            e.drain()
            e.tensor_tensor(STAT[:, st + 4:st + 5], STAT[:, st + 4:st + 5],
                            STAT[:, st + 3:st + 4], ALU.subtract)
            e.drain()
            return e.tensor_scalar(STAT[:, st + 4:st + 5], STAT[:, st + 4:st + 5],
                                   LN_EPS, None, ALU.add)
        p.add("vector", stats, reads=[sb, sb2], writes=[sb], tag="lnstat%s" % uid)

        def rstd(e, st=st):
            e.activation(STAT[:, st + 7:st + 8], STAT[:, st + 4:st + 5], AF.Ln)
            e.drain()
            return e.activation(STAT[:, st + 6:st + 7], STAT[:, st + 7:st + 8],
                                AF.Exp, scale=-0.5)
        p.add("scalar", rstd, reads=[sb], writes=[sb], tag="lnrstd%s" % uid)

    def ln_norm(uid, idx, src_ap, src_bufs, xr_slot, dstT, dst_bufs, keep32=False,
                xs=None, xr_b=None):
        st = idx * 8
        sb = b_stat[idx]
        if xs is None:
            xs = U[:, xr_slot * 2048:(xr_slot + 1) * 2048]
            xr_b = u_bufs(xr_slot * 8192, (xr_slot + 1) * 8192)
        p.add("vector", lambda e, s=src_ap, x=xs, st=st: e.tensor_scalar(
            x, s, STAT[:, st + 2:st + 3], STAT[:, st + 6:st + 7], ALU.subtract, ALU.mult),
            reads=src_bufs + [sb], writes=xr_b, tag="lnxn%s" % uid)
        ct = int(uid.split("_")[-1])
        for q in range(4):
            bank = q % 2

            def tp(e, q=q, bank=bank, x=xs):
                ins = None
                for k in range(4):
                    ic = q * 4 + k
                    ins = e.transpose(PS[bank][:, k * 128:(k + 1) * 128],
                                      x[:, ic * 128:(ic + 1) * 128], ID32[:, :])
                return ins
            p.add("tensor", tp, reads=xr_b + [b_const], writes=[b_ps[bank]],
                  tag="lntp%s_%d" % (uid, q))
            dap = bass.AP(dstT, (q * 4) * NT + ct * 128, [[dstT.shape[1], 128], [NT, 4], [1, 128]])
            ev_eng = "vector" if q % 2 == 0 else "scalar"

            def ev(e, a=dap, bank=bank):
                if hasattr(e, "tensor_copy"):
                    return e.tensor_copy(a, PS[bank][:, 0:512])
                return e.copy(a, PS[bank][:, 0:512])
            p.add(ev_eng, ev, reads=[b_ps[bank]],
                  writes=[dst_bufs[q * 4 + k2] for k2 in range(4)],
                  tag="lnev%s_%d" % (uid, q))
            if keep32:
                oslot = 1 - xr_slot
                d32 = U[:, oslot * 2048 + q * 512: oslot * 2048 + (q + 1) * 512]
                k_eng = "scalar" if q % 2 == 0 else "vector"

                def k32(e, d=d32, bank=bank):
                    if hasattr(e, "tensor_copy"):
                        return e.tensor_copy(d, PS[bank][:, 0:512])
                    return e.copy(d, PS[bank][:, 0:512])
                p.add(k_eng, k32,
                      reads=[b_ps[bank]], writes=u_bufs(oslot * 8192 + q * 2048,
                                                        oslot * 8192 + (q + 1) * 2048),
                      tag="lnev32%s_%d" % (uid, q))

    def kq_proj(name, wT_d, dout_ch, srcT, src_bufs, dst, dst_bufs, pm, drip=None):
        pend = []
        loc = [[Buf("%sl%d_%d" % (name, o2, t2)) for t2 in range(2)]
               for o2 in range(dout_ch)]

        GSB = GS.bitcast(BF16)

        def emit_rope(oc, tt, bank, rbank, dslice):
            cosap = CSB[:, tt * 512: tt * 512 + 512]
            sinap = CSB[:, NT + tt * 512: NT + tt * 512 + 512]
            p.add("tensor", lambda e, d=dslice, rbank=rbank: e.matmul(
                PS[rbank][:, :], RMAT[:, :], d, start=True, stop=True),
                reads=[loc[oc][tt], b_const], writes=[b_ps[rbank]],
                tag="%srot%d_%d" % (name, oc, tt))
            p.add("scalar", lambda e, rbank=rbank: e.copy(GSB[:, 1024:1536], PS[rbank][:, :]),
                  reads=[b_ps[rbank]], writes=[b_gs[2]], tag="%srotev%d_%d" % (name, oc, tt))
            p.add("vector", lambda e, c2=cosap, d=dslice: e.tensor_tensor(
                GSB[:, 0:512], d, c2, ALU.mult),
                reads=[loc[oc][tt], b_csb], writes=[b_gs[0]], tag="%st1%d_%d" % (name, oc, tt))
            p.add("vector", lambda e, s2=sinap: e.tensor_tensor(
                GSB[:, 512:1024], GSB[:, 1024:1536], s2, ALU.mult),
                reads=[b_gs[2], b_csb], writes=[b_gs[1]], tag="%st2%d_%d" % (name, oc, tt))
            p.add("vector", lambda e: e.tensor_tensor(GSB[:, 0:512], GSB[:, 0:512],
                                                      GSB[:, 512:1024], ALU.add),
                  reads=[b_gs[0], b_gs[1]], writes=[b_gs[0]], tag="%st3%d_%d" % (name, oc, tt))
            p.add("vector", lambda e: e.tensor_scalar(GSB[:, 512:1024], GSB[:, 0:512],
                                                      0.0, None, ALU.min),
                  reads=[b_gs[0]], writes=[b_gs[1]], tag="%smin%d_%d" % (name, oc, tt))
            p.add("scalar", lambda e: e.activation(GSB[:, 512:1024], GSB[:, 512:1024], AF.Exp),
                  reads=[b_gs[1]], writes=[b_gs[1]], tag="%sexp%d_%d" % (name, oc, tt))
            p.add("vector", lambda e: e.tensor_scalar(GSB[:, 0:512], GSB[:, 0:512],
                                                      0.0, None, ALU.max),
                  reads=[b_gs[0]], writes=[b_gs[0]], tag="%smax%d_%d" % (name, oc, tt))
            if pm:
                def fin(e, d=dslice):
                    e.tensor_tensor(GSB[:, 0:512], GSB[:, 0:512], GSB[:, 512:1024], ALU.add)
                    e.drain()
                    return e.tensor_scalar(d, GSB[:, 0:512], PMASK[:, 0:1], None, ALU.mult)
                p.add("vector", fin, reads=[b_gs[0], b_gs[1], b_const],
                      writes=[dst_bufs[oc], loc[oc][tt], b_gs[0]], tag="%sfin%d_%d" % (name, oc, tt))
            else:
                p.add("vector", lambda e, d=dslice: e.tensor_tensor(
                    d, GSB[:, 0:512], GSB[:, 512:1024], ALU.add),
                    reads=[b_gs[0], b_gs[1]], writes=[dst_bufs[oc], loc[oc][tt]],
                    tag="%sfin%d_%d" % (name, oc, tt))

        for oc in range(dout_ch):
            if drip:
                drip(oc)
            wslot = oc % 2
            wbase = wslot * 2048
            wview = WA[:, wbase:wbase + 2048].rearrange("p (c f) -> p c f", f=128)
            dma("gpsimd", wview,
                wT_d[:, oc * 128:(oc + 1) * 128].rearrange("(c p) f -> p c f", p=128),
                writes=wa_bufs(wbase, wbase + 2048), tag="%sw%d" % (name, oc))
            for tt in range(2):
                k_idx = oc * 2 + tt
                bank = k_idx % 4

                def mm(e, tt=tt, wbase=wbase, bank=bank, srcT=srcT):
                    ins = None
                    for ic in range(ICH):
                        ins = e.matmul(PS[bank][:, :],
                                       WA[:, wbase + ic * 128: wbase + (ic + 1) * 128],
                                       srcT[:, ic * NT + tt * 512: ic * NT + tt * 512 + 512],
                                       start=(ic == 0), stop=(ic == ICH - 1))
                    return ins
                p.add("tensor", mm, reads=wa_bufs(wbase, wbase + 2048) + src_bufs,
                      writes=[b_ps[bank]], tag="%smm%d_%d" % (name, oc, tt))
                dslice = dst[:, oc * NT + tt * 512: oc * NT + tt * 512 + 512]
                p.add("scalar", lambda e, d=dslice, bank=bank: e.copy(d, PS[bank][:, :]),
                      reads=[b_ps[bank]], writes=[loc[oc][tt]], tag="%sev%d_%d" % (name, oc, tt))
                pend.append((oc, tt, bank, 4 + k_idx % 2, dslice))
                if len(pend) > 3:
                    emit_rope(*pend.pop(0))
        while pend:
            emit_rope(*pend.pop(0))

    def v_proj(name, srcT, src_bufs, load_w):
        if load_w:
            wview = WA[:, 4096:4096 + ICH * 512].rearrange("p (c f) -> p c f", f=512)
            dma("gpsimd", wview, d_wvT.ap().rearrange("(c p) f -> p c f", p=128),
                writes=wa_bufs(4096, 4096 + ICH * 512), tag="%swv" % name)
        for ct in range(TCH):
            bank = ct % 2

            def mm(e, ct=ct, bank=bank, srcT=srcT):
                ins = None
                for ic in range(ICH):
                    ins = e.matmul(PS[bank][:, :],
                                   srcT[:, ic * NT + ct * 128: ic * NT + ct * 128 + 128],
                                   WA[:, 4096 + ic * 512: 4096 + (ic + 1) * 512],
                                   start=(ic == 0), stop=(ic == ICH - 1))
                return ins
            p.add("tensor", mm, reads=wa_bufs(4096, 4096 + ICH * 512) + src_bufs,
                  writes=[b_ps[bank]], tag="%svmm%d" % (name, ct))
            dap = bass.AP(VT, ct * KVH * 129, [[VT.shape[1], 128], [129, KVH], [1, 128]])
            p.add("vector", lambda e, a=dap, bank=bank: e.tensor_copy(a, PS[bank][:, :]),
                  reads=[b_ps[bank]], writes=[b_vt], tag="%svev%d" % (name, ct))

    state_banks = [0, 6, 1, 4]

    def state_ktok(name, c):
        def tp(e, c=c):
            ins = None
            for g in range(KVH):
                ins = e.transpose(PT[:, g * 128:(g + 1) * 128],
                                  KF[:, g * NT + c * 128: g * NT + c * 128 + 128], IDB[:, :])
            return ins
        p.add("tensor", tp, reads=b_kf_h + [b_const], writes=[b_pt], tag="%sktp%d" % (name, c))
        p.add("scalar", lambda e: e.copy(KTOK[:, :], PT[:, 0:KVH * 128]),
              reads=[b_pt], writes=[b_ktok], tag="%sktev%d" % (name, c))

    def state_update(name, c, stcp=False):
        for g in range(KVH):
            bank = state_banks[g]
            p.add("tensor", lambda e, g=g, c=c, bank=bank: e.matmul(
                PS[bank][:, 0:129], KTOK[:, g * 128:(g + 1) * 128],
                VT[:, c * KVH * 129 + g * 129: c * KVH * 129 + (g + 1) * 129],
                start=True, stop=True),
                reads=[b_ktok, b_vt], writes=[b_ps[bank]], tag="%ssmm%d_%d" % (name, c, g))
            p.add("vector", lambda e, g=g, bank=bank: e.tensor_tensor(
                STS32[:, g * 129:(g + 1) * 129], STS32[:, g * 129:(g + 1) * 129],
                PS[bank][:, 0:129], ALU.add),
                reads=[b_ps[bank], b_sts32[g]], writes=[b_sts32[g]],
                tag="%ssacc%d_%d" % (name, c, g))
            if stcp:
                p.add("scalar", lambda e, g=g: e.copy(
                    STS[:, g * 129:(g + 1) * 129], STS32[:, g * 129:(g + 1) * 129]),
                    reads=[b_sts32[g]], writes=[b_sts], tag="%sstcp%d_%d" % (name, c, g))

    WA32 = WA.bitcast(F32)
    main_norms = []

    def emit_main_norm(ct):
        ln_norm("m_%d" % ct, 8 + ct, OA[:, ct * DM:(ct + 1) * DM], [b_oa[ct]], ct % 2,
                XN, [b_xn] * 16, keep32=True)
        oslot = 1 - ct % 2

        def rmm32(e, oslot=oslot):
            ins = None
            for ic in range(ICH):
                ins = e.matmul(PS[4][:, 0:NE],
                               U[:, oslot * 2048 + ic * 128: oslot * 2048 + (ic + 1) * 128],
                               RT32[:, ic * NE:(ic + 1) * NE], start=(ic == 0), stop=(ic == ICH - 1))
            return ins
        p.add("tensor", rmm32, reads=u_bufs(oslot * 8192, (oslot + 1) * 8192) + [b_rt],
              writes=[b_ps[4]], tag="rmm%d" % ct)

        p.add("vector", lambda e, ct=ct: e.tensor_copy(LG[:, ct * NE:(ct + 1) * NE],
                                                       PS[4][:, 0:NE]),
              reads=[b_ps[4]], writes=[b_lg], tag="lgstash%d" % ct)

    def xpre_dma(pc):
        xsl = pc % 3
        dma("scalar", WA32[:, xsl * 2048:(xsl + 1) * 2048], d_xp[pc * 128:(pc + 1) * 128, :],
            writes=wa_bufs(xsl * 4096, (xsl + 1) * 4096), tag="xpre%d" % pc)
    xpre_dma(0)
    xpre_dma(1)
    for pc in range(TCH + 2):
        if pc >= 2:
            qc = pc - 2
            xsl = qc % 3
            xs_ap = WA32[:, xsl * 2048:(xsl + 1) * 2048]
            ln_norm("p_%d" % qc, qc, xs_ap, wa_bufs(xsl * 4096, (xsl + 1) * 4096),
                    0, A1, b_a1, xs=xs_ap, xr_b=wa_bufs(xsl * 4096, (xsl + 1) * 4096))
            emit_main_norm(qc)
        if 1 <= pc and pc + 1 < TCH:
            xpre_dma(pc + 1)
        if pc < TCH:
            xsl = pc % 3
            ln_statsA("p_%d" % pc, pc, WA32[:, xsl * 2048:(xsl + 1) * 2048],
                      wa_bufs(xsl * 4096, (xsl + 1) * 4096), sum_eng="scalar")
            ln_statsA("m_%d" % pc, 8 + pc, OA[:, pc * DM:(pc + 1) * DM], [b_oa[pc]])
        if 1 <= pc <= TCH:
            ln_statsB("p_%d" % (pc - 1), pc - 1)
            ln_statsB("m_%d" % (pc - 1), 8 + pc - 1)

    def g3(t, col):
        return bass.AP(t, col, [[t.shape[1], 128], [NE, TCH], [1, NE]])

    def gbc(t, col):
        return bass.AP(t, col, [[t.shape[1], 128], [1, TCH], [0, NE]])

    def top2all(e):
        EQ1 = TOP[:, 0:64]
        L2 = TOP[:, 64:128]
        T1 = TOP[:, 128:192]
        M1, M2, D, E, W1, W2 = 0, 8, 16, 24, 0, 8
        e.reduce_max(T2[:, 0:8], g3(LG, 0), axis=AX.X)
        e.drain()
        e.tensor_tensor(EQ1, LG[:, :], gbc(T2, 0), ALU.is_equal)
        e.drain()
        e.scalar_tensor_tensor(L2, EQ1, -1e30, LG[:, :], ALU.mult, ALU.add)
        e.drain()
        e.reduce_max(T2[:, 8:16], g3(TOP, 64), axis=AX.X)
        e.drain()
        e.tensor_tensor(L2, L2, gbc(T2, 8), ALU.is_equal)
        e.drain()
        return e.tensor_tensor(T2[:, 16:24], T2[:, 8:16], T2[:, 0:8], ALU.subtract)
    p.add("vector", top2all, reads=[b_lg], writes=[b_top, b_t2], tag="top2all_a")
    p.add("scalar", lambda e: e.activation(T2[:, 24:32], T2[:, 16:24], AF.Exp),
          reads=[b_t2], writes=[b_t2], tag="top2all_e")

    def top2allc(e):
        EQ1 = TOP[:, 0:64]
        EQ2 = TOP[:, 64:128]
        T1 = TOP[:, 128:192]
        e.tensor_scalar(T2[:, 16:24], T2[:, 24:32], 1.0, None, ALU.add)
        e.drain()
        e.reciprocal(T2[:, 16:24], T2[:, 16:24])
        e.drain()
        e.tensor_tensor(T2[:, 24:32], T2[:, 24:32], T2[:, 16:24], ALU.mult)
        e.drain()
        e.tensor_tensor(T1, EQ1, gbc(T2, 16), ALU.mult)
        e.tensor_tensor(EQ2, EQ2, gbc(T2, 24), ALU.mult)
        e.drain()
        return e.tensor_tensor(CMB[:, :], T1, EQ2, ALU.add)
    p.add("vector", top2allc, reads=[b_t2, b_top], writes=[b_cmb, b_top], tag="top2all_c")

    def drip_main_stats(oc):
        for ct in (2 * oc, 2 * oc + 1):
            if ct < TCH:
                ln_stats("m_%d" % ct, 8 + ct, OA[:, ct * DM:(ct + 1) * DM], [b_oa[ct]])
    dma("gpsimd", CSB[:, 0:NT], d_cosp.ap(), writes=[b_csb], tag="cosp")
    dma("gpsimd", CSB[:, NT:2 * NT], d_sinp.ap(), writes=[b_csb], tag="sinp")
    all_a1 = list(b_a1)
    kq_proj("pk", d_wkT.ap(), KVH, A1, all_a1, KF, b_kf_h, pm=True)
    v_proj("pv", A1, all_a1, load_w=True)
    for c in range(TCH):
        state_ktok("p", c)
        state_update("p", c)
    if debug:
        dma("sync", dbg["pkf"].ap(), KF[:, :], reads=b_kf_h, tag="dbgpkf")
        dma("sync", dbg["pvt"].ap(), VT[:, :], reads=[b_vt], tag="dbgpvt")
        dma("sync", dbg["pst"].ap(), STS32[:, :], reads=list(b_sts32), tag="dbgpst")

    if debug:
        for blk in range(ICH):
            dma("sync", dbg["xn"].ap()[:, blk * NT:(blk + 1) * NT], XN[:, blk * NT:(blk + 1) * NT],
                reads=[b_xn], tag="dbgxn%d" % blk)

    dma("gpsimd", CSB[:, 0:NT], d_cos.ap(), writes=[b_csb], tag="cosm")
    dma("gpsimd", CSB[:, NT:2 * NT], d_sin.ap(), writes=[b_csb], tag="sinm")

    kq_proj("q", d_wqT.ap(), H, XN, [b_xn], A1, b_a1, pm=False)
    kq_proj("k", d_wkT.ap(), KVH, XN, [b_xn], KF, b_kf_h, pm=False)
    v_proj("v", XN, [b_xn], load_w=False)

    if debug:
        for g in range(KVH):
            dma("sync", dbg["kf"].ap()[:, g * NT:(g + 1) * NT], KF[:, g * NT:(g + 1) * NT],
                reads=[b_kf_h[g]], tag="dbgkf%d" % g)
        for h in range(H):
            dma("sync", dbg["qf"].ap()[:, h * NT:(h + 1) * NT], A1[:, h * NT:(h + 1) * NT],
                reads=[b_a1[h]], tag="dbgqf%d" % h)
        dma("sync", dbg["vt"].ap()[:, :], VT[:, :], reads=[b_vt], tag="dbgvt")

    for c in range(TCH):
        if c == 0:
            p.add("scalar", lambda e: e.copy(STS[:, :], STS32[:, :]),
                  reads=list(b_sts32), writes=[b_sts], tag="stcp%d" % c)
        numrot5 = [1, 4, 2, 3, 5]

        def emit_abar(h):
            g = h // KVH
            abank = 0 if h % 2 == 0 else 6
            amap = AM[:, (h % 2) * 128:(h % 2) * 128 + 128]
            p.add("tensor", lambda e, g=g, h=h, c=c, ab=abank: e.matmul(
                PS[ab][:, 0:128], KF[:, g * NT + c * 128: g * NT + c * 128 + 128],
                A1[:, h * NT + c * 128: h * NT + c * 128 + 128], start=True, stop=True),
                reads=[b_kf_h[g], b_a1[h]], writes=[b_ps[abank]], tag="abar%d_%d" % (c, h))
            p.add("vector", lambda e, a=amap, ab=abank: e.tensor_tensor(
                a, PS[ab][:, 0:128], MASK[:, :], ALU.mult),
                reads=[b_ps[abank], b_const], writes=[b_am[h % 2]], tag="amask%d_%d" % (c, h))

        emit_abar(0)
        for hb in range(4):
            for j in range(4):
                h = hb * 4 + j
                g = h // KVH
                amb = b_am[h % 2]
                amap = AM[:, (h % 2) * 128:(h % 2) * 128 + 128]
                nbank = numrot5[h % 5]
                if h + 1 < H:
                    emit_abar(h + 1)

                def nmm(e, g=g, h=h, c=c, a=amap, nbank=nbank):
                    e.matmul(PS[nbank][:, 0:129], a,
                             VT[:, c * KVH * 129 + g * 129: c * KVH * 129 + (g + 1) * 129],
                             start=True, stop=False)
                    return e.matmul(PS[nbank][:, 0:129],
                                    A1[:, h * NT + c * 128: h * NT + c * 128 + 128],
                                    STS[:, g * 129:(g + 1) * 129], start=False, stop=True)
                p.add("tensor", nmm, reads=[amb, b_vt, b_a1[h], b_sts], writes=[b_ps[nbank]],
                      tag="nmm%d_%d" % (c, h))
                p.add("vector", lambda e, j=j, nbank=nbank: e.tensor_scalar(
                    DEN[:, j:j + 1], PS[nbank][:, 128:129], ATT_EPS, None, ALU.add),
                    reads=[b_ps[nbank]], writes=[b_den[j]], tag="dstash%d_%d" % (c, h))
            p.add("vector", lambda e: e.reciprocal(DEN[:, 4:8], DEN[:, 0:4]),
                  reads=list(b_den), writes=[b_denr], tag="drecip%d_%d" % (c, hb))
            for j in range(4):
                h = hb * 4 + j
                nbank = numrot5[h % 5]
                p.add("scalar", lambda e, h=h, c=c, j=j, nbank=nbank: e.activation(
                    A1[:, h * NT + c * 128: h * NT + c * 128 + 128], PS[nbank][:, 0:128],
                    AF.Copy, scale=DEN[:, 4 + j:5 + j]),
                    reads=[b_ps[nbank], b_denr], writes=[b_a1[h]], tag="oev%d_%d" % (c, h))
        state_ktok("m", c)
        state_update("m", c, stcp=(c < TCH - 1))
        for hh in range(2):
            def tp(e, c=c, hh=hh):
                ins = None
                for k in range(8):
                    h = hh * 8 + k
                    ins = e.transpose(PT[:, k * 128:(k + 1) * 128],
                                      A1[:, h * NT + c * 128: h * NT + c * 128 + 128], IDB[:, :])
                return ins
            p.add("tensor", tp, reads=[b_a1[hh * 8 + k2] for k2 in range(8)] + [b_const],
                  writes=[b_pt], tag="flip%d_%d" % (c, hh))
            dap = bass.AP(A1, (hh * 8) * NT + c * 128, [[A1.shape[1], 128], [NT, 8], [1, 128]])
            if hh == 0:
                p.add("vector", lambda e, a=dap: e.tensor_copy(a, PT[:, :]),
                      reads=[b_pt], writes=[b_a1[hh * 8 + k2] for k2 in range(8)],
                      tag="flipev%d_%d" % (c, hh))
            else:
                p.add("scalar", lambda e, a=dap: e.copy(a, PT[:, :]),
                      reads=[b_pt], writes=[b_a1[hh * 8 + k2] for k2 in range(8)],
                      tag="flipev%d_%d" % (c, hh))

    if debug:
        for h in range(H):
            dma("sync", dbg["ao"].ap()[:, h * NT:(h + 1) * NT], A1[:, h * NT:(h + 1) * NT],
                reads=[b_a1[h]], tag="dbgao%d" % h)

    for fq in range(4):
        if fq % 2 == 0:
            sl_t, sl_bufs = UB, u_bufs(0, 16384)
        else:
            sl_t, sl_bufs = WA, wa_bufs(0, 8192)
        for qq in range(4):
            dst = sl_t[:, qq * 2048:(qq + 1) * 2048].rearrange("p (c f) -> p c f", f=512)
            src = d_woT[qq * 512:(qq + 1) * 512, fq * 512:(fq + 1) * 512].rearrange(
                "(c p) f -> p c f", p=128)
            dma("gpsimd", dst, src, writes=sl_bufs, tag="wo%d_%d" % (fq, qq))
        for ctg in range(4):
            cts = [ctg * 2, ctg * 2 + 1]
            pair = [[2, 3], [5, 6]][ctg % 2]
            for cc in range(16):
                def mm(e, cc=cc, cts=cts, sl_t=sl_t, pair=pair):
                    ins = None
                    for k, ct in enumerate(cts):
                        ins = e.matmul(PS[pair[k]][:, :],
                                       A1[:, cc * NT + ct * 128: cc * NT + ct * 128 + 128],
                                       sl_t[:, cc * 512:(cc + 1) * 512],
                                       start=(cc == 0), stop=(cc == 15))
                    return ins
                p.add("tensor", mm, reads=sl_bufs + [b_a1[cc]],
                      writes=[b_ps[k] for k in pair], tag="womm%d_%d_%d" % (fq, ctg, cc))
            for k, ct in enumerate(cts):
                p.add("vector", lambda e, k=k, ct=ct, fq=fq, pair=pair: e.tensor_tensor(
                    OA[:, ct * DM + fq * 512: ct * DM + (fq + 1) * 512],
                    OA[:, ct * DM + fq * 512: ct * DM + (fq + 1) * 512],
                    PS[pair[k]][:, :], ALU.add),
                    reads=[b_ps[pair[k]], b_oa[ct]], writes=[b_oa[ct]],
                    tag="woev%d_%d_%d" % (fq, ctg, ct))

    if debug:
        for ct in range(TCH):
            dma("sync", dbg["oa1"].ap()[ct * 128:(ct + 1) * 128, :], OA[:, ct * DM:(ct + 1) * DM],
                reads=[b_oa[ct]], tag="dbgoa1_%d" % ct)

    for ct in range(TCH):
        def gmm(e, ct=ct):
            ins = None
            for ic in range(ICH):
                ins = e.matmul(PS[4][:, 0:1],
                               XN[:, ic * NT + ct * 128: ic * NT + ct * 128 + 128],
                               SGT[:, ic:ic + 1], start=(ic == 0), stop=(ic == ICH - 1))
            return ins
        p.add("tensor", gmm, reads=[b_xn, b_rt], writes=[b_ps[4]], tag="gmm%d" % ct)
        p.add("scalar", lambda e, ct=ct: e.activation(SIG[:, ct:ct + 1], PS[4][:, 0:1], AF.Sigmoid),
              reads=[b_ps[4]], writes=[b_cmb], tag="sigev%d" % ct)

    if debug:
        dma("sync", dbg["cmb"].ap(), CMB[:, :], reads=[b_cmb], tag="dbgcmb")
        dma("sync", dbg["sig"].ap(), SIG[:, :], reads=[b_cmb], tag="dbgsig")

    def unit_srcs(u):
        if u < NE:
            return d_wg.ap()[u], d_wu.ap()[u], d_wd.ap()[u]
        blk = u - NE
        return (d_swg.ap()[:, blk * IFF:(blk + 1) * IFF],
                d_swu.ap()[:, blk * IFF:(blk + 1) * IFF],
                d_swd.ap()[blk * IFF:(blk + 1) * IFF, :])

    gu_dma_ops = {}
    gu_last_mm = {}

    def emit_gu_dmas(u, lo=0, hi=IFCH):
        wg_src, wu_src, _ = unit_srcs(u)
        if u not in gu_dma_ops:
            gu_dma_ops[u] = {}
        for i in range(lo, hi):
            gbase = (i % 2) * 2048
            ubase = 4096 + (i % 2) * 2048
            wr_g = wa_bufs(gbase, gbase + 2048) if i < 2 else []
            wr_u = wa_bufs(ubase, ubase + 2048) if i < 2 else []
            dep_g = [gu_last_mm[("g", i % 2)]] if (i < 2 and ("g", i % 2) in gu_last_mm) else []
            dep_u = [gu_last_mm[("u", i % 2)]] if (i < 2 and ("u", i % 2) in gu_last_mm) else []
            og = dma("gpsimd", WA[:, gbase:gbase + 2048].rearrange("p (c f) -> p c f", f=128),
                     wg_src[:, i * 128:(i + 1) * 128].rearrange("(c p) f -> p c f", p=128),
                     writes=wr_g, deps=dep_g, tag="u%dgw%d" % (u, i))
            ou = dma("gpsimd", WA[:, ubase:ubase + 2048].rearrange("p (c f) -> p c f", f=128),
                     wu_src[:, i * 128:(i + 1) * 128].rearrange("(c p) f -> p c f", p=128),
                     writes=wr_u, deps=dep_u, tag="u%duw%d" % (u, i))
            gu_dma_ops[u][("g", i)] = og
            gu_dma_ops[u][("u", i)] = ou

    down_mm_last = {0: None, 1: None}
    dw_ops = {}
    dw_pending = {}

    def emit_down_dma(u, ft):
        _, _, wd_src = unit_srcs(u)
        dsl = ft % 2
        dview = UB[:, dsl * 5632:(dsl + 1) * 5632].rearrange("p (c f) -> p c f", f=512)
        wr = u_bufs(dsl * 11264, (dsl + 1) * 11264) if (u == 0 and ft < 2) else []
        op = dma("gpsimd", dview, wd_src[:, ft * 512:(ft + 1) * 512].rearrange(
            "(c p) f -> p c f", p=128),
            writes=wr, tag="u%ddw%d" % (u, ft))
        dw_ops[(u, ft)] = op
        dw_pending.setdefault((u, ft) if ft >= 2 else ("n", u, dsl), op)
        return op

    out_ops = []
    emit_gu_dmas(0)
    emit_down_dma(0, 0)
    emit_down_dma(0, 1)
    for u in range(NUNIT):
        pool = [0, 1, 4]
        for i in range(IFCH):
            gbase = (i % 2) * 2048
            ubase = 4096 + (i % 2) * 2048
            for tt in range(2):
                k = i * 2 + tt
                bg = pool[(2 * k) % 3]
                bu = pool[(2 * k + 1) % 3]

                def gm(e, tt=tt, gbase=gbase, bg=bg):
                    ins = None
                    for ic in range(ICH):
                        ins = e.matmul(PS[bg][:, :],
                                       WA[:, gbase + ic * 128: gbase + (ic + 1) * 128],
                                       XN[:, ic * NT + tt * 512: ic * NT + tt * 512 + 512],
                                       start=(ic == 0), stop=(ic == ICH - 1))
                    return ins
                op_gm = p.add("tensor", gm, reads=[b_xn], writes=[b_ps[bg]],
                              deps=[gu_dma_ops[u][("g", i)]], tag="u%dgm%d_%d" % (u, i, tt))
                if ("g", i + 2) in gu_dma_ops[u]:
                    gu_dma_ops[u][("g", i + 2)].deps.append(op_gm)
                gu_last_mm[("g", i % 2)] = op_gm

                def um(e, tt=tt, ubase=ubase, bu=bu):
                    ins = None
                    for ic in range(ICH):
                        ins = e.matmul(PS[bu][:, :],
                                       WA[:, ubase + ic * 128: ubase + (ic + 1) * 128],
                                       XN[:, ic * NT + tt * 512: ic * NT + tt * 512 + 512],
                                       start=(ic == 0), stop=(ic == ICH - 1))
                    return ins
                op_um = p.add("tensor", um, reads=[b_xn], writes=[b_ps[bu]],
                              deps=[gu_dma_ops[u][("u", i)]], tag="u%dum%d_%d" % (u, i, tt))
                if ("u", i + 2) in gu_dma_ops[u]:
                    gu_dma_ops[u][("u", i + 2)].deps.append(op_um)
                gu_last_mm[("u", i % 2)] = op_um
                gsl = k % 2
                p.add("scalar", lambda e, gsl=gsl, bg=bg: e.activation(
                    GS[:, gsl * 512:(gsl + 1) * 512], PS[bg][:, :], AF.Silu),
                    reads=[b_ps[bg]], writes=[b_gs[gsl]], tag="u%dsil%d_%d" % (u, i, tt))
                hslice = A1[:, i * NT + tt * 512: i * NT + tt * 512 + 512]
                p.add("vector", lambda e, h2=hslice, gsl=gsl, bu=bu: e.tensor_tensor(
                    h2, GS[:, gsl * 512:(gsl + 1) * 512], PS[bu][:, :], ALU.mult),
                    reads=[b_gs[gsl], b_ps[bu]], writes=[b_a1[i]], tag="u%dh%d_%d" % (u, i, tt))
        if u + 1 < NUNIT:
            emit_gu_dmas(u + 1, 0, 2)
        dw23 = [emit_down_dma(u, 2), emit_down_dma(u, 3)]
        if u + 1 < NUNIT:
            dw01n = [emit_down_dma(u + 1, 0), emit_down_dma(u + 1, 1)]
        else:
            dw01n = []
        if u + 1 < NUNIT:
            emit_gu_dmas(u + 1, 2, IFCH)
        for ft in range(4):
            dsl = ft % 2
            dbase = dsl * 5632
            for ctg in range(4):
                cts = [ctg * 2, ctg * 2 + 1]
                pair = [[2, 3], [5, 6]][ctg % 2]

                def dmm(e, cts=cts, dbase=dbase, pair=pair):
                    ins = None
                    for i in range(IFCH):
                        for kk, ct in enumerate(cts):
                            ins = e.matmul(PS[pair[kk]][:, :],
                                           A1[:, i * NT + ct * 128: i * NT + ct * 128 + 128],
                                           UB[:, dbase + i * 512: dbase + (i + 1) * 512],
                                           start=(i == 0), stop=(i == IFCH - 1))
                    return ins
                op_dmm = p.add("tensor", dmm, reads=[b_a1[i2] for i2 in range(IFCH)],
                               writes=[b_ps[k] for k in pair],
                               deps=[dw_ops[(u, ft)]],
                               tag="u%ddm%d_%d" % (u, ft, ctg))
                if ft < 2:
                    dw23[ft].deps.append(op_dmm)
                elif dw01n:
                    dw01n[ft - 2].deps.append(op_dmm)
                down_mm_last[ft % 2] = op_dmm
                for kk, ct in enumerate(cts):
                    oaslice = OA[:, ct * DM + ft * 512: ct * DM + (ft + 1) * 512]
                    sap = CMB[:, ct * NE + u: ct * NE + u + 1] if u < NE else SIG[:, ct:ct + 1]
                    p.add("vector", lambda e, o=oaslice, kk=kk, sap=sap, pair=pair:
                          e.scalar_tensor_tensor(o, PS[pair[kk]][:, :], sap, o,
                                                 ALU.mult, ALU.add),
                          reads=[b_ps[pair[kk]], b_cmb, b_oa[ct]], writes=[b_oa[ct]],
                          tag="u%ddev%d_%d" % (u, ft, ct))
        if debug and u == 0:
            for i2 in range(IFCH):
                dma("sync", dbg["h0"].ap()[:, i2 * NT:(i2 + 1) * NT], A1[:, i2 * NT:(i2 + 1) * NT],
                    reads=[b_a1[i2]], tag="dbgh0_%d" % i2)
            _dmp = dma("sync", dbg["dsl0"].ap(), UB[:, 0:IFCH * 512],
                       deps=[dw_ops[(0, 2)], down_mm_last[0]], tag="dbgdsl0")
            if (1, 0) in dw_ops:
                dw_ops[(1, 0)].deps.append(_dmp)
            for ct in range(TCH):
                dma("sync", dbg["oau0"].ap()[ct * 128:(ct + 1) * 128, :],
                    OA[:, ct * DM:(ct + 1) * DM], reads=[b_oa[ct]], tag="dbgoau0_%d" % ct)
        if u == NUNIT - 1:
            for ct in range(TCH):
                out_ops.append(dma("sync", d_out[ct * 128:(ct + 1) * 128, :],
                                   OA[:, ct * DM:(ct + 1) * DM],
                                   reads=[b_oa[ct]], tag="outst%d" % ct))

    p.add("scalar", lambda e: e.memzero(STAT[:, 0:1]), deps=out_ops, tag="finwait")

    import contextlib
    with contextlib.ExitStack() as st:
        sems = {}
        for e in ENGS:
            sems["eng_" + e] = st.enter_context(nc.semaphore("s_" + e))
        for q in ("gpsimd", "sync", "scalar"):
            sems["dma_" + q] = [st.enter_context(nc.semaphore("sd_%s_%d" % (q, i)))
                                for i in range(NSEM[q])]
        block = st.enter_context(nc.Block())
        p.finalize_and_emit(block, sems)
    return nc


_cached = {}


def _get_nc(debug=False):
    if debug not in _cached:
        _cached[debug] = build_program(debug)
    return _cached[debug]


def _rope_tables():
    inv_freq = 1.0 / ROPE_BASE ** (np.arange(0, HD, 2, dtype=np.float32) / HD)
    t = np.arange(S, dtype=np.float32)
    freqs = np.outer(t, inv_freq).astype(np.float32)
    emb = np.concatenate([freqs, freqs], -1)
    return np.cos(emb).astype(np.float32), np.sin(emb).astype(np.float32)


def make_in_maps(inputs):
    x = np.asarray(inputs["L0_attn_x"], dtype=np.float32)
    wq = np.asarray(inputs["L0_attn_Wq"], dtype=np.float32)
    wk = np.asarray(inputs["L0_attn_Wk"], dtype=np.float32)
    wv = np.asarray(inputs["L0_attn_Wv"], dtype=np.float32)
    wo = np.asarray(inputs["L0_attn_Wo"], dtype=np.float32)
    rw = np.asarray(inputs["L0_ffn_router_w"], dtype=np.float32)
    wg = np.ascontiguousarray(np.asarray(inputs["L0_ffn_w_gate"], dtype=np.float32))
    wu = np.ascontiguousarray(np.asarray(inputs["L0_ffn_w_up"], dtype=np.float32))
    wd = np.ascontiguousarray(np.asarray(inputs["L0_ffn_w_down"], dtype=np.float32))
    swg = np.ascontiguousarray(np.asarray(inputs["L0_ffn_sw_gate"], dtype=np.float32))
    swu = np.ascontiguousarray(np.asarray(inputs["L0_ffn_sw_up"], dtype=np.float32))
    swd = np.ascontiguousarray(np.asarray(inputs["L0_ffn_sw_down"], dtype=np.float32))
    sg = np.asarray(inputs["L0_ffn_shared_gate_w"], dtype=np.float32)

    wqT = np.ascontiguousarray(wq.T)
    wkT = np.ascontiguousarray(wk.T)
    wvT = np.ascontiguousarray(wv.T)
    woT = np.ascontiguousarray(wo.T)
    rtT = np.ascontiguousarray(rw.reshape(NE, ICH, 128).transpose(2, 1, 0).reshape(128, ICH * NE))
    sgT = np.ascontiguousarray(sg.reshape(ICH, 128).T)

    cos, sin = _rope_tables()
    cosT = np.ascontiguousarray(cos.T)
    sinT = np.ascontiguousarray(sin.T)

    j = np.arange(128)
    maskJI = (j[:, None] <= j[None, :]).astype(np.float32)
    id32 = np.eye(128, dtype=np.float32)
    idb = np.eye(128, dtype=np.float32)
    R = np.zeros((128, 128), dtype=np.float32)
    for dp in range(64):
        R[dp + 64, dp] = -1.0
    for dp in range(64, 128):
        R[dp - 64, dp] = 1.0
    rmat = R
    onesb = np.ones((128, 1), dtype=np.float32)

    in_maps = []
    for core in range(8):
        b = core // 2
        hh = core % 2
        in_maps.append({
            "x_main": np.ascontiguousarray(x[b, hh * NT:(hh + 1) * NT, :]),
            "x_prefix": np.ascontiguousarray(x[b, 0:NT, :]),
            "wqT": wqT, "wkT": wkT, "wvT": wvT, "woT": woT, "rtT": rtT,
            "wg": wg, "wu": wu, "wd": wd, "swg": swg, "swu": swu, "swd": swd,
            "sgT": sgT,
            "cosT": np.ascontiguousarray(cosT[:, hh * NT:(hh + 1) * NT]),
            "sinT": np.ascontiguousarray(sinT[:, hh * NT:(hh + 1) * NT]),
            "cosTp": np.ascontiguousarray(cosT[:, 0:NT]),
            "sinTp": np.ascontiguousarray(sinT[:, 0:NT]),
            "maskJI": maskJI, "id32": id32, "idb": idb, "rmat": rmat,
            "onesb": onesb,
            "pmask": np.full((128, 1), float(hh), dtype=np.float32),
        })
    return in_maps


def kernel(**inputs):
    nc = _get_nc(False)
    in_maps = make_in_maps(inputs)
    res = run_bass_kernel_spmd(nc, in_maps, core_ids=list(range(8)))
    out = np.empty((B, S, DM), dtype=np.float32)
    for core in range(8):
        b = core // 2
        hh = core % 2
        out[b, hh * NT:(hh + 1) * NT, :] = res.results[core]["out"]
    return out
